# version 20
# baseline (speedup 1.0000x reference)
import numpy as np
from contextlib import ExitStack

import concourse.bass as bass
import concourse.mybir as mybir
from concourse.bass_core_rust import IndirectOffsetOnAxis
from concourse.bass_utils import run_bass_kernel_spmd

F32 = mybir.dt.float32
BF16 = mybir.dt.bfloat16
I32 = mybir.dt.int32
AL = mybir.AluOpType
ACT = mybir.ActivationFunctionType
AX = mybir.AxisListType.X

NC = 8
B, S, D = 2, 16384, 1024
TOK = B * S
TPC = TOK // NC
NT = TPC // 128
M, E, H = 2048, 8, 512
MPC = M // NC
EPS = 1e-5
DINV = 1.0 / D
HINV = 1.0 / H

ENG2Q = {"te": "tensor", "ve": "vector", "sc": "scalar", "gpc": "gpsimd",
         "cc": "gpsimd", "ldw": "gpsimd", "lsw": "gpsimd", "lgc": "gpsimd",
         "lsc": "sync",
         "ldx0": "gpsimd", "ldx1": "gpsimd", "ldx2": "gpsimd", "ldx3": "gpsimd",
         "ldx4": "gpsimd", "ldx5": "gpsimd", "mgx": "gpsimd", "sel0": "gpsimd",
         "scat": "gpsimd", "rst": "gpsimd", "ew0": "gpsimd",
         "ew1": "gpsimd", "ew3q": "gpsimd", "ebq": "sync",
         "pat0": "gpsimd", "pat1": "gpsimd",
         "st0": "sync", "st1": "sync"}


def build_program(n_tiles=NT):
    nc = bass.Bass("TRN2", target_bir_lowering=False)

    def din(name, shape, dt=F32):
        return nc.dram_tensor(name, shape, dt, kind="ExternalInput").ap()

    x_shard = din("x_shard", [TPC, D])
    x_full = din("x_full", [TOK, D])
    W_in = din("W_in", [D, D])
    U_in = din("U_in", [D, D])
    bias_in = din("bias_in", [D])
    sw1T_in = din("sw1T_in", [D, H])
    sw2T_in = din("sw2T_in", [D, H])
    sw3T_in = din("sw3T_in", [H, D])
    sb1_in = din("sb1_in", [H])
    sb2_in = din("sb2_in", [H])
    sb3_in = din("sb3_in", [D])
    slng_in = din("slng_in", [H])
    slnb_in = din("slnb_in", [H])
    geT_in = din("geT_in", [D, 9])
    geb_in = din("geb_in", [9])
    expb_in = din("expb_in", [128, E])
    ew1t_in = din("ew1t_in", [E, D, H])
    ew2t_in = din("ew2t_in", [E, D, H])
    ew3t_in = din("ew3t_in", [E, H, D])
    eb1_in = din("eb1_in", [E, H])
    eb2_in = din("eb2_in", [E, H])
    eb3_in = din("eb3_in", [E, D])
    selg_in = din("selg_in", [128, 4])
    selb_in = din("selb_in", [128, 4])
    slngc_in = din("slngc_in", [128, 4])
    slnbc_in = din("slnbc_in", [128, 4])
    biasc_in = din("biasc_in", [128, 8])
    ibm_in = din("ibm_in", [MPC], I32)
    itm_in = din("itm_in", [MPC], I32)
    ibf_in = din("ibf_in", [128, 16], I32)
    itf_in = din("itf_in", [128, 16], I32)
    coreoff_in = din("coreoff_in", [128, 1])
    idn_in = din("idn_in", [128, 128])
    out_d = nc.dram_tensor("out", [TPC, D], F32, kind="ExternalOutput").ap()

    ag_in_d = nc.dram_tensor("ag_in", [MPC, D], BF16, kind="Internal")
    ag_out_d = nc.dram_tensor("ag_out", [M + 1, D], BF16, kind="Internal")
    sel_d = nc.dram_tensor("sel_d", [TPC + 132], I32, kind="Internal")

    es = ExitStack()
    sb = lambda name, shape, dt=BF16: es.enter_context(nc.sbuf_tensor(name, shape, dt))
    psum = lambda name, shape: es.enter_context(nc.psum_tensor(name, shape, F32))

    sems = {k: es.enter_context(nc.semaphore(k)) for k in
            ("te", "ve", "sc", "gpc", "cc", "ldw", "lsw", "lgc", "lsc", "ldx0",
             "ldx1", "ldx2", "ldx3", "ldx4", "ldx5", "mgx", "sel0", "scat", "rst",
             "ew0", "ew1", "ew3q", "ebq", "pat0", "pat1", "st0", "st1")}
    planned = {k: 0 for k in sems}
    emitted = {k: 0 for k in sems}
    waited = {}
    queues = {"gpsimd": [], "sync": [], "tensor": [], "vector": [], "scalar": []}

    def _wait(eng_obj, qname, toks):
        for tok in toks:
            if tok is None:
                continue
            sname, val = tok
            key = (qname, sname)
            if waited.get(key, -1) >= val:
                continue
            eng_obj.wait_ge(sems[sname], val)
            waited[key] = val

    def emit(sem_name, fn, waits=(), inc=1):
        q = ENG2Q[sem_name]
        planned[sem_name] += inc
        val = planned[sem_name]
        ws = tuple(waits)

        def run(eng):
            _wait(eng, q, ws)
            instr = fn(eng)
            instr.then_inc(sems[sem_name], inc)
            emitted[sem_name] += inc
            assert emitted[sem_name] == val, (sem_name, emitted[sem_name], val)
        queues[q].append(run)
        return (sem_name, val)

    def emit_seq(sem_name, fn, waits=(), inc=1):
        return emit(sem_name, fn, waits, inc)

    def gdma(dst, src, waits=(), sem="ldw", **kw):
        return emit(sem, lambda g: g.dma_start(out=dst, in_=src, **kw), waits, 16)

    def sdma(dst, src, waits=(), sem="lsc", **kw):
        return emit(sem, lambda sy: sy.dma_start(out=dst, in_=src, **kw), waits, 16)

    def ve(fn, waits=()):
        return emit("ve", fn, waits)

    def sc(fn, waits=()):
        return emit("sc", fn, waits)

    def te(fn, waits=()):
        return emit("te", fn, waits)

    def gp(fn, waits=()):
        return emit("gpc", fn, waits)

    WcT = sb("WcT", [128, 8 * 1024])
    sw1T = sb("sw1T", [128, 8 * 512])
    sw2T = sb("sw2T", [128, 8 * 512])
    sw3Tg = sb("sw3Tg", [128, 4 * 1024])
    geT = sb("geT", [128, 8 * 16])
    idnb = sb("idnb", [128, 128])
    ones_row = sb("ones_row", [1, 128])
    bcrow = sb("bcrow", [1, D])
    vbrow = sb("vbrow", [1, D], F32)
    vbrow2 = [sb("vbrow2_0", [1, D], F32)]
    vbrow_bf = sb("vbrow_bf", [1, D])
    sb1row = sb("sb1row", [1, H])
    sb2row = sb("sb2row", [1, H])
    gebrow = sb("gebrow", [1, 16])
    sb3row = sb("sb3row", [1, D], F32)
    eps_t = sb("eps_t", [128, 1], F32)
    expb = sb("expb", [128, E], F32)
    coreoff = sb("coreoff", [128, 1], F32)
    zrow = sb("zrow", [1, D])
    selg_c = sb("selg_c", [128, 4], F32)
    selb_c = sb("selb_c", [128, 4])
    slnb_c = sb("slnb_c", [128, 4])
    slng_c = sb("slng_c", [128, 4], F32)
    biasc = sb("biasc", [128, 8])

    es_build = ExitStack()
    Wsb = es_build.enter_context(nc.sbuf_tensor("Wsb", [128, 8 * 1024], BF16))
    Usb = es_build.enter_context(nc.sbuf_tensor("Usb", [128, 8 * 1024], BF16))
    es_build.close()

    ew1s = [sb("ew1s0", [128, 8 * 512]), sb("ew1s1", [128, 8 * 512])]
    ew2s = [sb("ew2s0", [128, 8 * 512]), sb("ew2s1", [128, 8 * 512])]
    ew3s = [sb("ew3s0", [128, 4 * 1024])]
    xT2b = [sb(f"xT2b{s}", [128, D]) for s in range(3)]
    eb1r = [sb("eb1r0", [1, H]), sb("eb1r1", [1, H])]
    eb2r = [sb("eb2r0", [1, H]), sb("eb2r1", [1, H])]
    eb3r = [sb("eb3r0", [1, D], F32)]
    vber = [sb("vber0", [1, D])]

    xb = [sb(f"xb{s}", [128, D]) for s in range(6)]
    xn1b = [sb(f"xn1b{s}", [128, D]) for s in range(3)]
    xT1b = [sb(f"xT1b{s}", [128, D]) for s in range(3)]
    hcb = [sb(f"hcb{s}", [128, D]) for s in range(4)]
    x2rb = [sb(f"x2rb{s}", [128, D]) for s in range(7)]
    sqb_ = sb("sqb_", [128, D])
    sqb2 = sb("sqb2", [128, D])
    sqb = [sqb_, sqb2]
    silub = [sb(f"silub{s}", [128, H]) for s in range(2)]
    gb = [sb(f"gb{s}", [128, H]) for s in range(3)]
    zTb = [sb(f"zTb{s}", [128, H]) for s in range(2)]
    fb = [sb(f"fb{s}", [128, D]) for s in range(3)]
    outb = [sb(f"outb{s}", [128, D], F32) for s in range(2)]
    stt = [sb(f"stt{s}", [128, 16], F32) for s in range(10)]
    bnst = [sb(f"bnst{s}", [128, 16], F32) for s in range(2)]
    bnstr = [sb(f"bnstr{s}", [128, 12], F32) for s in range(2)]
    selall_i = sb("selall_i", [128, max(1, n_tiles)], I32)
    ivm_all = sb("ivm_all", [128, max(1, n_tiles)], F32)
    self_all = sb("self_all", [128, max(1, n_tiles)], F32)

    S1, S2, TMP, MU, SIG, A, PM, IVM, A4M, SELF, SHW, EXW, TOP1, MX = range(14)
    BETA = 14

    xmT = sb("xmT", [128, 2 * D])
    estt = [sb(f"estt{j}", [128, 16], F32) for j in range(2)]
    egb = [sb(f"egb{j}", [128, H]) for j in range(2)]
    esilub = [sb(f"esilub{j}", [128, H]) for j in range(2)]
    ezTb = [sb(f"ezTb{j}", [128, H]) for j in range(2)]
    accf = [sb(f"accf{j}", [128, D]) for j in range(2)]
    scoresb = [sb(f"scores{j}", [128, 16], F32) for j in range(2)]
    maskb = [sb(f"maskb{j}", [128, E], F32) for j in range(2)]
    maskpb = [sb(f"maskpb{j}", [128, E], F32) for j in range(2)]
    m8 = sb("m8", [128, E], F32)
    ibm_sb = sb("ibm_sb", [128, 2], I32)
    itm_sb = sb("itm_sb", [128, 2], I32)
    idxf = sb("idxf", [128, 8], F32)
    gmi = sb("gmi", [128, 2], I32)

    ibf_sb = sb("ibf_sb", [128, 16], I32)
    itf_sb = sb("itf_sb", [128, 16], I32)
    bfv = sb("bfv", [128, 16], F32)
    tfv = sb("tfv", [128, 16], F32)
    vmask = sb("vmask", [128, 16], F32)
    tmp16 = sb("tmp16", [128, 16], F32)
    gflat = sb("gflat", [128, 16], F32)
    selt_i = sb("selt_i", [128, 16], I32)
    selt2_i = sb("selt2_i", [128, 16], I32)
    miota = sb("miota", [128, 16], I32)
    selinit = sb("selinit", [128, 33], I32)
    dumrow = sb("dumrow", [128, 16], F32)
    dumi16 = sb("dumi16", [128, 16], I32)

    hcps = psum("hcps", [128, 1024])
    trps = psum("trps", [128, 1024])
    x12ps = psum("x12ps", [128, 1024])
    fps = psum("fps", [128, 1024])

    blk = es.enter_context(nc.Block())

    free = {"tr": None, "hc": None, "x12": None, "f": None}

    t_ones = gp(lambda g: g.memset(ones_row[:], 1.0))
    t_eps = ve(lambda v: v.memset(eps_t[:], EPS))
    t_zrow = ve(lambda v: v.memset(zrow[:], 0.0))
    t_selinit = ve(lambda v: v.memset(selinit[:], M))
    gdma(idnb[:], idn_in, sem="lgc")
    gdma(sb1row[0:1, :], sb1_in[None, :], sem="lgc")
    gdma(sb2row[0:1, :], sb2_in[None, :], sem="lgc")
    gdma(gebrow[0:1, 0:9], geb_in[None, :], sem="lgc")
    gdma(selb_c[:], selb_in, sem="lgc")
    gdma(slnb_c[:], slnbc_in, sem="lgc")
    gdma(biasc[:], biasc_in, sem="lgc")
    LGC_ALL = ("lgc", planned["lgc"])
    sdma(expb[:], expb_in)
    sdma(coreoff[:], coreoff_in)
    sdma(sb3row[0:1, :], sb3_in[None, :])
    sdma(selg_c[:], selg_in)
    sdma(slng_c[:], slngc_in)
    sdma(ibf_sb[:, :], ibf_in)
    sdma(itf_sb[:, :], itf_in)
    sdma(bass.AP(sel_d, 0, [[33, 128], [1, 33]]), selinit[:, :], waits=(t_selinit,))
    sdma(ibm_sb[:, 0:1], ibm_in[0:128][:, None])
    sdma(ibm_sb[:, 1:2], ibm_in[128:256][:, None])
    sdma(itm_sb[:, 0:1], itm_in[0:128][:, None])
    sdma(itm_sb[:, 1:2], itm_in[128:256][:, None])
    t_agz = sdma(ag_out_d.ap()[M:M + 1, :], zrow[0:1, :], waits=(t_zrow,))

    for k in range(8):
        gdma(Wsb[:, k * 1024:(k + 1) * 1024], W_in[k * 128:(k + 1) * 128, :])
    for k in range(8):
        gdma(Usb[:, k * 1024:(k + 1) * 1024], U_in[k * 128:(k + 1) * 128, :])
    LDW_ALL = ("ldw", planned["ldw"])
    for k in range(8):
        gdma(sw1T[:, k * 512:(k + 1) * 512], sw1T_in[k * 128:(k + 1) * 128, :],
             sem="lsw")
        gdma(sw2T[:, k * 512:(k + 1) * 512], sw2T_in[k * 128:(k + 1) * 128, :],
             sem="lsw")
    for k in range(4):
        gdma(sw3Tg[:, k * 1024:(k + 1) * 1024], sw3T_in[k * 128:(k + 1) * 128, :],
             sem="lsw")
    for k in range(8):
        gdma(geT[:, k * 16:k * 16 + 9], geT_in[k * 128:(k + 1) * 128, :], sem="lsw")
    LSW_ALL = ("lsw", planned["lsw"])
    LSC_ALL = ("lsc", planned["lsc"])

    t_idn = LGC_ALL
    t_sb1 = t_sb2 = t_geb = LGC_ALL
    t_selb = t_slnb = t_biasc = LGC_ALL
    t_selg = t_slng = LSC_ALL
    t_w8 = t_u8 = LDW_ALL
    t_sw12 = t_sw3l = t_gel = LSW_ALL
    t_agz = LSC_ALL
    t_expb = t_coff = t_sb3 = LSC_ALL
    t_ibf = t_itf = t_sel0 = LSC_ALL

    wc_evs = []
    for gi, (m_, n_) in enumerate((m_, n_) for m_ in range(8) for n_ in range(2)):
        off = 512 * (gi % 2)

        def mms(t, m_=m_, n_=n_, off=off):
            for k_ in range(8):
                last_ = t.matmul(hcps[:, off:off + 512],
                         Wsb[:, k_ * 1024 + m_ * 128: k_ * 1024 + (m_ + 1) * 128],
                         Usb[:, k_ * 1024 + n_ * 512: k_ * 1024 + (n_ + 1) * 512],
                         start=(k_ == 0), stop=(k_ == 7))
            return last_
        w = [t_w8, t_u8]
        if len(wc_evs) >= 2:
            w.append(wc_evs[-2])
        tmm = te(mms, w)
        wc_evs.append(sc(lambda s_, m_=m_, n_=n_, off=off: s_.copy(
            WcT[:, m_ * 1024 + n_ * 512: m_ * 1024 + (n_ + 1) * 512],
            hcps[:, off:off + 512]), [tmm]))
    t_wc = wc_evs[-1]

    def bc_mms(t):
        for n_ in range(2):
            for k_ in range(8):
                last_ = t.matmul(hcps[0:1, n_ * 512:(n_ + 1) * 512],
                         biasc[:, k_:k_ + 1],
                         Usb[:, k_ * 1024 + n_ * 512: k_ * 1024 + (n_ + 1) * 512],
                         start=(k_ == 0), stop=(k_ == 7))
        return last_
    t_bcmm = te(bc_mms, [t_biasc, t_u8, t_wc])
    t_bc = ve(lambda v: v.tensor_copy(bcrow[0:1, :], hcps[0:1, 0:1024]), [t_bcmm])
    free["hc"] = t_bc

    def vb_mms(t):
        for n_ in range(2):
            for k_ in range(4):
                last_ = t.matmul(fps[0:1, n_ * 512:(n_ + 1) * 512],
                         slnb_c[:, k_:k_ + 1],
                         sw3Tg[:, k_ * 1024 + n_ * 512: k_ * 1024 + (n_ + 1) * 512],
                         start=(k_ == 0), stop=(k_ == 3))
        return last_
    t_vbmm = te(vb_mms, [t_slnb, t_sw3l])
    t_vb1 = ve(lambda v: v.tensor_copy(vbrow[0:1, :], fps[0:1, 0:1024]), [t_vbmm])
    t_vb2 = ve(lambda v: v.tensor_add(vbrow[0:1, :], vbrow[0:1, :], sb3row[0:1, :]),
               [t_vb1, t_sb3])
    t_vb = ve(lambda v: v.tensor_copy(vbrow_bf[0:1, :], vbrow[0:1, :]), [t_vb2])
    free["f"] = t_vb1

    t_g3 = None
    for k in range(4):
        t_g3 = ve(lambda v, k=k: v.tensor_single_scalar(
            sw3Tg[:, k * 1024:(k + 1) * 1024], sw3Tg[:, k * 1024:(k + 1) * 1024],
            slng_c[:, k:k + 1], AL.mult), [t_vbmm, t_slng, t_sw3l])

    t_miota = gp(lambda g: g.iota(miota[:, :], [[128, 16]], channel_multiplier=1))
    t_dumi = gp(lambda g: g.iota(dumi16[:, :], [[8, 16]], base=TPC, channel_multiplier=0,
                                 allow_small_or_imprecise_dtypes=True))
    t_dum = ve(lambda v: v.tensor_copy(dumrow[:, :], dumi16[:, :]), [t_dumi])

    t_b = ve(lambda v: v.tensor_copy(bfv[:, :], ibf_sb[:, :]), [t_ibf])
    t_t = ve(lambda v: v.tensor_copy(tfv[:, :], itf_sb[:, :]), [t_itf])
    t_1 = ve(lambda v: v.tensor_single_scalar(vmask[:, :], bfv[:, :], -0.5, AL.is_gt), [t_b])
    t_2 = ve(lambda v: v.tensor_single_scalar(tmp16[:, :], bfv[:, :], B - 0.5, AL.is_lt), [t_b, t_1])
    t_3 = ve(lambda v: v.tensor_mul(vmask[:, :], vmask[:, :], tmp16[:, :]), [t_1, t_2])
    t_4 = ve(lambda v: v.tensor_single_scalar(tmp16[:, :], tfv[:, :], -0.5, AL.is_gt), [t_t, t_3])
    t_5 = ve(lambda v: v.tensor_mul(vmask[:, :], vmask[:, :], tmp16[:, :]), [t_3, t_4])
    t_6 = ve(lambda v: v.tensor_single_scalar(tmp16[:, :], tfv[:, :], S - 0.5, AL.is_lt), [t_t, t_5])
    t_7 = ve(lambda v: v.tensor_mul(vmask[:, :], vmask[:, :], tmp16[:, :]), [t_5, t_6])
    t_8 = ve(lambda v: v.scalar_tensor_tensor(gflat[:, :], bfv[:, :], float(S), tfv[:, :],
                                              AL.mult, AL.add), [t_b, t_t, t_7])
    t_9 = ve(lambda v: v.tensor_single_scalar(gflat[:, :], gflat[:, :], coreoff[:, 0:1],
                                              AL.subtract), [t_8, t_coff])
    t_10 = ve(lambda v: v.tensor_single_scalar(tmp16[:, :], gflat[:, :], -0.5, AL.is_gt), [t_9, t_7])
    t_11 = ve(lambda v: v.tensor_mul(vmask[:, :], vmask[:, :], tmp16[:, :]), [t_7, t_10])
    t_12 = ve(lambda v: v.tensor_single_scalar(tmp16[:, :], gflat[:, :], TPC - 0.5, AL.is_lt), [t_9, t_11])
    t_13 = ve(lambda v: v.tensor_mul(vmask[:, :], vmask[:, :], tmp16[:, :]), [t_11, t_12])
    t_14 = ve(lambda v: v.tensor_mul(gflat[:, :], gflat[:, :], vmask[:, :]), [t_13, t_9])
    t_14b = ve(lambda v: v.tensor_copy(selt2_i[:, :], gflat[:, :]), [t_14])
    t_15 = ve(lambda v: v.tensor_single_scalar(tmp16[:, :], vmask[:, :], -1.0, AL.mult),
              [t_13, t_14])
    t_15b = ve(lambda v: v.tensor_single_scalar(tmp16[:, :], tmp16[:, :], 1.0, AL.add),
               [t_15])
    t_15c = ve(lambda v: v.tensor_mul(tmp16[:, :], tmp16[:, :], dumrow[:, :]),
               [t_15b, t_dum])
    t_16 = ve(lambda v: v.tensor_add(gflat[:, :], gflat[:, :], tmp16[:, :]),
              [t_14, t_14b, t_15c])
    t_17 = ve(lambda v: v.tensor_copy(selt_i[:, :], gflat[:, :]), [t_16])
    scat_w = (t_17, t_miota, t_sel0)

    def emit_sel_scatters():
        for jj in range(16):
            emit("scat", lambda g, jj=jj: g.indirect_dma_start(
                out=bass.AP(sel_d, 0, [[1, TPC + 132], [1, 1]]),
                out_offset=IndirectOffsetOnAxis(selt_i[:, jj:jj + 1], 0),
                in_=miota[:, jj:jj + 1], in_offset=None),
                scat_w, 16)
        return ("scat", planned["scat"])

    def ln_stats_math(s, t_s1, t_s2, dinv):
        return ln_stats_math_st(stt[s], t_s1, t_s2, dinv)

    def ln_stats_math_st(st, t_s1, t_s2, dinv):
        t_mu = ve(lambda v: v.tensor_single_scalar(
            st[:, MU:MU + 1], st[:, S1:S1 + 1], dinv, AL.mult), [t_s1])
        t_tt = ve(lambda v: v.scalar_tensor_tensor(
            st[:, TMP:TMP + 1], st[:, S1:S1 + 1], dinv * dinv, st[:, S1:S1 + 1],
            AL.mult, AL.mult), [t_s1])
        t_v = ve(lambda v: v.scalar_tensor_tensor(
            st[:, SIG:SIG + 1], st[:, S2:S2 + 1], dinv, st[:, TMP:TMP + 1],
            AL.mult, AL.subtract), [t_s2, t_tt])
        t_sg = sc(lambda s_: s_.activation(
            st[:, SIG:SIG + 1], st[:, SIG:SIG + 1], ACT.Sqrt, bias=eps_t[:]),
            [t_v, t_eps])
        t_a = ve(lambda v: v.reciprocal(st[:, A:A + 1], st[:, SIG:SIG + 1]), [t_sg])
        return t_a, t_mu

    def ln_stats_math_b(st, t_s1, t_s2, dinv):
        t_a, t_mu = ln_stats_math_st(st, t_s1, t_s2, dinv)
        t_b = ve(lambda v: v.scalar_tensor_tensor(
            st[:, BETA:BETA + 1], st[:, MU:MU + 1], -1.0, st[:, A:A + 1],
            AL.mult, AL.mult), [t_a, t_mu])
        return t_a, t_mu, t_b

    def bn_ln_stats(s, src, width, t_src):
        st = stt[s]
        bs = bnstr[s]
        nh = (width + 511) // 512
        toks = []
        for hh in range(nh):
            w0, w1 = hh * 512, min(width, (hh + 1) * 512)
            toks.append(ve(lambda v, hh=hh, w0=w0, w1=w1: v.bn_stats(
                bs[:, hh * 6:(hh + 1) * 6], src[:, w0:w1]), [t_src] + toks[-1:]))
        t_ag = ve(lambda v, nh=nh: v.bn_aggr(bs[:, 10:12], bs[:, 0:6 * nh]), toks)
        t_mu = ve(lambda v: v.tensor_copy(st[:, MU:MU + 1], bs[:, 10:11]), [t_ag])
        t_sg = sc(lambda s_: s_.activation(
            st[:, SIG:SIG + 1], bs[:, 11:12], ACT.Sqrt, bias=eps_t[:]), [t_ag, t_eps])
        t_a = ve(lambda v: v.reciprocal(st[:, A:A + 1], st[:, SIG:SIG + 1]), [t_sg])
        return t_a, t_mu

    def transpose_chunks(src, dst, nchunk, t_src, dst_free=None, off=0,
                         defer=False):
        def mms(t):
            for k in range(nchunk):
                last_ = t.matmul(trps[:, off + k * 128:off + (k + 1) * 128],
                         src[:, k * 128:(k + 1) * 128], idnb[:],
                         start=True, stop=True)
            return last_
        tmm = te(mms, [t_src, free["tr"], t_idn])

        def do_evac(extra_free=None):
            w = [tmm]
            if dst_free:
                w.append(dst_free)
            if extra_free:
                w.append(extra_free)
            tev = sc(lambda s_: s_.copy(dst[:, 0:nchunk * 128],
                                        trps[:, off:off + nchunk * 128]), w)
            free["tr"] = tev
            return tev
        if defer:
            return do_evac, tmm
        return do_evac(), tmm

    def proj(dst_ps, psname, xT_sb, xoff, w_sb, wtile, nout, kchunks, brow, t_brow,
             t_xT, t_w):
        def mms(t):
            for k_ in range(kchunks):
                for n_ in range(nout):
                    last_ = t.matmul(dst_ps[:, n_ * 512:(n_ + 1) * 512],
                             xT_sb[:, xoff + k_ * 128: xoff + (k_ + 1) * 128],
                             w_sb[:, k_ * wtile + n_ * 512: k_ * wtile + (n_ + 1) * 512],
                             start=(k_ == 0), stop=False)
            for n_ in range(nout):
                last_ = t.matmul(dst_ps[:, n_ * 512:(n_ + 1) * 512], ones_row[0:1, :],
                         brow[0:1, n_ * 512:(n_ + 1) * 512], start=False, stop=True)
            return last_
        return te(mms, [t_xT, t_w, t_brow, t_ones, free[psname]])

    def moe_chain(j):
        s = j
        st = stt[s]
        t_xm = t_moe_xm[j]
        t_a1, t_mu1 = bn_ln_stats(s, xb[s], D, t_xm)
        t_xn1 = ve(lambda v: v.tensor_scalar(
            xn1b[s][:, :], xb[s][:, :], st[:, MU:MU + 1], st[:, A:A + 1],
            AL.subtract, AL.mult), [t_a1, t_mu1, t_xm])
        yield
        t_xT1, t_tr1 = transpose_chunks(xn1b[s], xT1b[s], 8, t_xn1)
        t_hcmm = proj(hcps, "hc", xT1b[s], 0, WcT, 1024, 2, 8, bcrow, t_bc, t_xT1, t_wc)
        free["hc"] = t_hcmm
        t_hc = sc(lambda s_: s_.activation(
            hcb[s][:, :], hcps[:, :], ACT.Identity, accum_out=st[:, S1:S1 + 1]),
            [t_hcmm])
        free["hc"] = t_hc
        yield
        t_s2 = ve(lambda v: v.scalar_tensor_tensor(
            sqb[s][:, :], hcb[s][:, :], 1.0, hcb[s][:, :], AL.mult, AL.mult,
            accum_out=st[:, S2:S2 + 1]), [t_hc])
        t_a2, t_mu2 = ln_stats_math(s, t_hc, t_s2, DINV)
        t_t2 = ve(lambda v: v.tensor_scalar(
            xn1b[s][:, :], hcb[s][:, :], st[:, MU:MU + 1], st[:, A:A + 1],
            AL.subtract, AL.mult), [t_a2, t_mu2, t_tr1])
        yield
        t_x2r = ve(lambda v: v.scalar_tensor_tensor(
            x2rb[s][:, :], xn1b[s][:, :], 1.0, xb[s][:, :], AL.mult, AL.add,
            accum_out=st[:, S1:S1 + 1]), [t_t2, t_xm])
        t_s2b = ve(lambda v: v.scalar_tensor_tensor(
            sqb[s][:, :], x2rb[s][:, :], 1.0, x2rb[s][:, :], AL.mult, AL.mult,
            accum_out=st[:, S2:S2 + 1]), [t_x2r])
        t_a3, t_mu3 = ln_stats_math(s, t_x2r, t_s2b, DINV)
        yield
        t_xn2 = ve(lambda v: v.tensor_scalar(
            hcb[s][:, :], x2rb[s][:, :], st[:, MU:MU + 1], st[:, A:A + 1],
            AL.subtract, AL.mult), [t_a3, t_mu3])

        def mms(t):
            for k in range(8):
                last_ = t.matmul(trps[:, k * 128:(k + 1) * 128],
                         hcb[s][:, k * 128:(k + 1) * 128], idnb[:],
                         start=True, stop=True)
            return last_
        t_tr2 = te(mms, [t_xn2, free["tr"], t_idn])
        t_xT2 = sc(lambda s_: s_.copy(xmT[:, j * 1024:j * 1024 + 1024],
                                      trps[:, 0:1024]), [t_tr2])
        free["tr"] = t_xT2
        t_moe_xT[j] = t_xT2
        yield
        t_x1 = proj(x12ps, "x12", xmT, j * 1024, sw1T, 512, 1, 8, sb1row, t_sb1,
                    t_xT2, t_sw12)
        free["x12"] = t_x1

        def mms2(t):
            for k_ in range(8):
                last_ = t.matmul(x12ps[:, 512:1024],
                         xmT[:, j * 1024 + k_ * 128: j * 1024 + (k_ + 1) * 128],
                         sw2T[:, k_ * 512:(k_ + 1) * 512],
                         start=(k_ == 0), stop=False)
            last_ = t.matmul(x12ps[:, 512:1024], ones_row[0:1, :], sb2row[0:1, :],
                     start=False, stop=True)
            return last_
        t_x2 = te(mms2, [t_xT2, t_sb2, t_x1])
        t_silu = sc(lambda s_: s_.activation(silub[s][:, :], x12ps[:, 0:512], ACT.Silu),
                    [t_x1])
        t_g = ve(lambda v: v.scalar_tensor_tensor(
            gb[s][:, :], x12ps[:, 512:1024], 1.0, silub[s][:, :], AL.mult, AL.mult,
            accum_out=st[:, S1:S1 + 1]), [t_x2, t_silu])
        free["x12"] = t_g
        yield
        t_gs2 = ve(lambda v: v.scalar_tensor_tensor(
            sqb[s][:, 0:512], gb[s][:, :], 1.0, gb[s][:, :], AL.mult, AL.mult,
            accum_out=st[:, S2:S2 + 1]), [t_g])
        t_agk, t_mug = ln_stats_math(s, t_g, t_gs2, HINV)
        t_z = ve(lambda v: v.tensor_scalar(
            gb[s][:, :], gb[s][:, :], st[:, MU:MU + 1], st[:, A:A + 1],
            AL.subtract, AL.mult), [t_agk, t_mug])
        yield

        def gat_mms(t):
            for k_ in range(8):
                last_ = t.matmul(x12ps[:, 0:9],
                         xmT[:, j * 1024 + k_ * 128: j * 1024 + (k_ + 1) * 128],
                         geT[:, k_ * 16:k_ * 16 + 9],
                         start=(k_ == 0), stop=False)
            last_ = t.matmul(x12ps[:, 0:9], ones_row[0:1, :], gebrow[0:1, 0:9],
                     start=False, stop=True)
            return last_
        t_gmm = te(gat_mms, [t_moe_xT[j], t_gel, t_geb, free["x12"], t_ones])
        t_scores = sc(lambda s_: s_.activation(
            scoresb[j][:, 0:9], x12ps[:, 0:9], ACT.Sigmoid), [t_gmm])
        free["x12"] = t_scores
        t_eb = ve(lambda v: v.tensor_add(m8[:, :], scoresb[j][:, 1:9], expb[:, :]),
                  [t_scores, t_expb])
        t_mx = ve(lambda v: v.reduce_max(st[:, MX:MX + 1], m8[:, :], axis=AX), [t_eb])
        t_msk = ve(lambda v: v.tensor_single_scalar(
            maskb[j][:, :], m8[:, :], st[:, MX:MX + 1], AL.is_ge), [t_mx, t_eb])
        t_top = ve(lambda v: v.scalar_tensor_tensor(
            m8[:, :], maskb[j][:, :], 1.0, scoresb[j][:, 1:9], AL.mult, AL.mult,
            accum_out=st[:, TOP1:TOP1 + 1]), [t_msk])
        t_df = ve(lambda v: v.tensor_scalar(
            st[:, TMP:TMP + 1], scoresb[j][:, 0:1], st[:, TOP1:TOP1 + 1], None,
            AL.subtract), [t_top])
        t_shw = sc(lambda s_: s_.activation(
            st[:, SHW:SHW + 1], st[:, TMP:TMP + 1], ACT.Sigmoid), [t_df])
        t_exw = ve(lambda v: v.tensor_scalar(
            st[:, EXW:EXW + 1], st[:, SHW:SHW + 1], -1.0, 1.0, AL.mult, AL.add),
            [t_shw])
        t_mskp = ve(lambda v: v.tensor_single_scalar(
            maskpb[j][:, :], maskb[j][:, :], st[:, EXW:EXW + 1], AL.mult),
            [t_msk, t_exw])
        yield
        t_zT, _ = transpose_chunks(gb[s], zTb[s], 4, t_z)
        t_fmm = proj(fps, "f", zTb[s], 0, sw3Tg, 1024, 2, 4, vbrow_bf, t_vb, t_zT, t_g3)
        moe_fmm[j] = t_fmm
        t_acc0 = ve(lambda v: v.tensor_single_scalar(
            accf[j][:, :], fps[:, :], st[:, SHW:SHW + 1], AL.mult),
            [t_fmm, t_shw])
        free["f"] = t_acc0
        t_moe_prep[j] = (t_acc0, t_mskp)

    t_i1 = ve(lambda v: v.tensor_copy(idxf[:, 0:2], ibm_sb[:, :]), [LSC_ALL])
    t_i2 = ve(lambda v: v.tensor_copy(idxf[:, 2:4], itm_sb[:, :]), [LSC_ALL])
    t_i3 = ve(lambda v: v.tensor_scalar(idxf[:, 0:2], idxf[:, 0:2], 0.0, float(B - 1),
                                        AL.max, AL.min), [t_i1])
    t_i4 = ve(lambda v: v.tensor_scalar(idxf[:, 2:4], idxf[:, 2:4], 0.0, float(S - 1),
                                        AL.max, AL.min), [t_i2])
    t_i5 = ve(lambda v: v.scalar_tensor_tensor(idxf[:, 4:6], idxf[:, 0:2], float(S),
                                               idxf[:, 2:4], AL.mult, AL.add),
              [t_i3, t_i4])
    t_gmi = ve(lambda v: v.tensor_copy(gmi[:, :], idxf[:, 4:6]), [t_i5])

    t_moe_xT = [None, None]
    t_moe_prep = [None, None]
    t_moe_xm = [None, None]
    moe_fmm = [None, None]
    for j in range(2):
        emit("mgx", lambda g, j=j: g.indirect_dma_start(
            out=outb[j][:, :], out_offset=None,
            in_=x_full, in_offset=IndirectOffsetOnAxis(gmi[:, j:j + 1], 0)),
            (t_gmi,), 16)
    MGX_ALL = ("mgx", planned["mgx"])
    for j in range(2):
        t_moe_xm[j] = ve(lambda v, j=j: v.tensor_copy(xb[j][:, :], outb[j][:, :]),
                         [MGX_ALL])
    moe_gens = [moe_chain(0), moe_chain(1)]
    moe_alive = [True, True]
    while any(moe_alive):
        for j in range(2):
            if moe_alive[j]:
                try:
                    next(moe_gens[j])
                except StopIteration:
                    moe_alive[j] = False

    t_last_acc = None
    prev_vb_use = [None]
    e_silu_free = [None, None]
    e_g_free = [None, None]
    e_zT_free = [None, None]
    t_ew12_free = [None, None]
    t_ew12_ld = {}
    t_ew3_ld = {}
    t_eb3_ld = {}
    ew3_state = {"omm": None}
    prep = {}
    exp_tok = {}

    def emit_ew12_loads(e):
        sl = e % 2
        wf = (t_ew12_free[sl],) if t_ew12_free[sl] else (t_bcmm,)
        ewsem = f"ew{sl}"
        for wsb, win in ((ew1s[sl], ew1t_in), (ew2s[sl], ew2t_in)):
            gdma(bass.AP(wsb, 0, [[4096, 128], [512, 8], [1, 512]]),
                 bass.AP(win.tensor, e * 1024 * 512,
                         [[512, 128], [512 * 128, 8], [1, 512]]),
                 waits=wf, sem=ewsem)
        gdma(eb1r[sl][0:1, :], eb1_in[e, None, :], waits=wf, sem=ewsem)
        t_ew12_ld[e] = gdma(eb2r[sl][0:1, :], eb2_in[e, None, :], waits=wf,
                            sem=ewsem)

    def emit_ew3_load(e):
        wf = []
        if ew3_state["omm"] is not None:
            wf.append(ew3_state["omm"])
        if e - 1 in prep:
            wf.append(prep[e - 1]["t_vbe2"])
        gdma(bass.AP(ew3s[0], 0, [[4096, 128], [1024, 4], [1, 1024]]),
             bass.AP(ew3t_in.tensor, e * 512 * 1024,
                     [[1024, 128], [1024 * 128, 4], [1, 1024]]),
             waits=tuple(wf), sem="ew3q")
        t_ew3_ld[e] = ("ew3q", planned["ew3q"])
        t_eb3_ld[e] = sdma(eb3r[0][0:1, :], eb3_in[e, None, :], waits=tuple(wf),
                           sem="ebq")

    def emit_prep_vbe(e):
        t_ld = t_ew3_ld[e]
        t_ldb = t_eb3_ld[e]

        def vbe_mms(t):
            for n_ in range(2):
                for k_ in range(4):
                    last_ = t.matmul(fps[0:1, n_ * 512:(n_ + 1) * 512],
                                     selb_c[:, k_:k_ + 1],
                                     ew3s[0][:, k_ * 1024 + n_ * 512:
                                             k_ * 1024 + (n_ + 1) * 512],
                                     start=(k_ == 0), stop=(k_ == 3))
            return last_
        t_vbemm = te(vbe_mms, [t_ld, t_selb, free["f"]])
        t_vbe1 = ve(lambda v: v.tensor_copy(vbrow2[0][0:1, :], fps[0:1, 0:1024]),
                    [t_vbemm] + ([prev_vb_use[0]] if prev_vb_use[0] else []))
        free["f"] = t_vbe1
        t_vbe2 = ve(lambda v: v.tensor_add(vbrow2[0][0:1, :], vbrow2[0][0:1, :],
                                           eb3r[0][0:1, :]), [t_vbe1, t_ldb])
        t_vbe = ve(lambda v: v.tensor_copy(vber[0][0:1, :], vbrow2[0][0:1, :]),
                   [t_vbe2])
        prev_vb_use[0] = t_vbe
        t_gsl = None
        for k in range(4):
            t_gsl = ve(lambda v, k=k: v.tensor_single_scalar(
                ew3s[0][:, k * 1024:(k + 1) * 1024],
                ew3s[0][:, k * 1024:(k + 1) * 1024],
                selg_c[:, k:k + 1], AL.mult), [t_vbemm, t_selg, t_ld])
        prep[e] = {"t_vbe": t_vbe, "t_gsl": t_gsl, "t_vbe2": t_vbe2}

    def emit_expert_A1(e, j):
        sl = e % 2
        st = estt[j]
        t_ld = t_ew12_ld[e]

        def h_mms(t, sl=sl, j=j):
            for k_ in range(8):
                last_ = t.matmul(x12ps[:, 0:512],
                                 xmT[:, j * 1024 + k_ * 128: j * 1024 + (k_ + 1) * 128],
                                 ew1s[sl][:, k_ * 512:(k_ + 1) * 512],
                                 start=(k_ == 0), stop=False)
                last_ = t.matmul(x12ps[:, 512:1024],
                                 xmT[:, j * 1024 + k_ * 128: j * 1024 + (k_ + 1) * 128],
                                 ew2s[sl][:, k_ * 512:(k_ + 1) * 512],
                                 start=(k_ == 0), stop=False)
            last_ = t.matmul(x12ps[:, 0:512], ones_row[0:1, :], eb1r[sl][0:1, :],
                             start=False, stop=True)
            last_ = t.matmul(x12ps[:, 512:1024], ones_row[0:1, :], eb2r[sl][0:1, :],
                             start=False, stop=True)
            return last_
        t_h = te(h_mms, [t_ld, t_moe_xT[j], free["x12"], t_ones])
        if j == 1:
            t_ew12_free[sl] = t_h
        t_silu = sc(lambda s_, j=j: s_.activation(
            esilub[j][:, :], x12ps[:, 0:512], ACT.Silu),
            [t_h] + ([e_silu_free[j]] if e_silu_free[j] else []))
        t_gz = ve(lambda v, j=j, st=st: v.scalar_tensor_tensor(
            egb[j][:, :], x12ps[:, 512:1024], 1.0, esilub[j][:, :], AL.mult, AL.mult,
            accum_out=st[:, S1:S1 + 1]),
            [t_h, t_silu] + ([e_g_free[j]] if e_g_free[j] else []))
        free["x12"] = t_gz
        t_gz2 = ve(lambda v, j=j, st=st: v.scalar_tensor_tensor(
            esilub[j][:, :], egb[j][:, :], 1.0, egb[j][:, :], AL.mult, AL.mult,
            accum_out=st[:, S2:S2 + 1]), [t_gz])
        e_silu_free[j] = t_gz2
        exp_tok[(e, j)] = (t_gz, t_gz2)

    def emit_expert_A2(e, j):
        st = estt[j]
        t_gz, t_gz2 = exp_tok[(e, j)]
        t_az, t_muz = ln_stats_math_st(st, t_gz, t_gz2, HINV)
        t_zz = ve(lambda v, j=j, st=st: v.tensor_scalar(
            egb[j][:, :], egb[j][:, :], st[:, MU:MU + 1], st[:, A:A + 1],
            AL.subtract, AL.mult), [t_az, t_muz, t_gz2])
        return t_zz

    def emit_expert_B1(e, j, t_zz, off=512, defer=True):
        do_ev, t_ztr2 = transpose_chunks(egb[j], ezTb[j], 4, t_zz, e_zT_free[j],
                                         off=off, defer=True)
        e_g_free[j] = t_ztr2
        if not defer:
            zz_tok[(e, j)] = do_ev()
            return

        def fin(e=e, j=j, do_ev=do_ev):
            zz_tok[(e, j)] = do_ev()
        pend_evac.append(fin)

    def emit_expert_B2(e, j, t_zT):
        nonlocal t_last_acc
        pr = prep[e]

        def o_mms(t, j=j):
            for n_ in range(2):
                for k_ in range(4):
                    last_ = t.matmul(fps[:, n_ * 512:(n_ + 1) * 512],
                                     ezTb[j][:, k_ * 128:(k_ + 1) * 128],
                                     ew3s[0][:, k_ * 1024 + n_ * 512:
                                             k_ * 1024 + (n_ + 1) * 512],
                                     start=(k_ == 0), stop=False)
                last_ = t.matmul(fps[:, n_ * 512:(n_ + 1) * 512], ones_row[0:1, :],
                                 vber[0][0:1, n_ * 512:(n_ + 1) * 512],
                                 start=False, stop=True)
            return last_
        t_omm = te(o_mms, [t_zT, pr["t_vbe"], pr["t_gsl"], free["f"], t_ones])
        e_zT_free[j] = t_omm
        if j == 1:
            ew3_state["omm"] = t_omm
        extra = [t_moe_prep[j][1]]
        if e == 0:
            extra.append(t_moe_prep[j][0])
        t_acc = ve(lambda v, j=j, e=e: v.scalar_tensor_tensor(
            accf[j][:, :], fps[:, :], maskpb[j][:, e:e + 1], accf[j][:, :],
            AL.mult, AL.add), [t_omm] + extra + ([t_last_acc] if t_last_acc else []))
        free["f"] = t_acc
        t_last_acc = t_acc

    units = [(e, j) for e in range(E) for j in range(2)]
    emit_ew12_loads(0)
    emit_ew3_load(0)
    emit_ew12_loads(1)
    emit_prep_vbe(0)

    selh = {"ready": None, "t": None}

    def emit_sel_batch(b):
        if b == 0:
            selh["ready"] = emit_sel_scatters()
        n8 = (n_tiles + 3) // 4
        for i in range(b * n8, min((b + 1) * n8, n_tiles)):
            sdma(selall_i[:, i:i + 1],
                 bass.AP(sel_d, i * 128, [[1, 128], [1, 1]]),
                 waits=(selh["ready"],), sem="sel0")

    def emit_ivm():
        SEL_ALL = ("sel0", planned["sel0"])
        t_selfc = ve(lambda v: v.tensor_copy(self_all[:, :], selall_i[:, :]), [SEL_ALL])
        t_pm_all = ve(lambda v: v.tensor_single_scalar(
            ivm_all[:, :], self_all[:, :], float(M) - 0.5, AL.is_lt), [t_selfc])
        selh["t"] = ve(lambda v: v.tensor_scalar(
            ivm_all[:, :], ivm_all[:, :], -1.0, 1.0, AL.mult, AL.add), [t_pm_all])

    prev_x_free = [t_moe_xT[0], t_moe_xT[1], None, None, None, None]
    prev_bn_free = [None, None]
    prev_xn1_free = [t_moe_xT[0], t_moe_xT[1], None]
    prev_xT1_free = [t_moe_xT[0], t_moe_xT[1], None]
    prev_xT2_free = [None] * 3
    prev_hc_free = [t_moe_xT[0], t_moe_xT[1], None, None]
    prev_x2r_free = [t_moe_xT[0], t_moe_xT[1], None, None, None, None, None]
    prev_g_free = [moe_fmm[0], moe_fmm[1], None]
    prev_silu_free = [moe_fmm[0], moe_fmm[1]]
    prev_fb_free = [None] * 3
    prev_out_free = [t_moe_xm[0], t_moe_xm[1]]
    prev_sq_free = [moe_fmm[0], moe_fmm[1]]
    prev_zT_free = [moe_fmm[0], moe_fmm[1]]
    last_store = None
    TI = [dict() for _ in range(n_tiles)]

    def U0(i):
        s5 = (i + 2) % 6
        TI[i]["t_x"] = gdma(xb[s5][:, :], x_shard[i * 128:(i + 1) * 128, :],
                            waits=(prev_x_free[s5],) if prev_x_free[s5] else (),
                            sem=f"ldx{s5}")

    def U1(i):
        d = TI[i]
        s5 = (i + 2) % 6
        sx = (i + 2) % 3
        s2 = i % 2
        st = stt[i % 10]
        bs = bnst[s2]
        d["st"] = st
        t_x = d["t_x"]
        t_b1 = ve(lambda v, s5=s5, bs=bs: v.bn_stats(bs[:, 0:6], xb[s5][:, 0:512]),
                  [t_x] + ([prev_bn_free[s2]] if prev_bn_free[s2] else []))
        t_b2 = ve(lambda v, s5=s5, bs=bs: v.bn_stats(bs[:, 6:12], xb[s5][:, 512:1024]),
                  [t_b1])
        t_ag = ve(lambda v, bs=bs: v.bn_aggr(bs[:, 10:12], bs[:, 0:12]), [t_b2])
        t_sg = sc(lambda s_, bs=bs: s_.activation(
            bs[:, 12:13], bs[:, 11:12], ACT.Sqrt, bias=eps_t[:]), [t_ag, t_eps])
        t_ra = ve(lambda v, bs=bs: v.reciprocal(bs[:, 13:14], bs[:, 12:13]), [t_sg])
        t_xn1 = gp(lambda g, s5=s5, sx=sx, bs=bs: g.tensor_scalar(
            xn1b[sx][:, :], xb[s5][:, :], bs[:, 10:11], bs[:, 13:14],
            AL.subtract, AL.mult),
            [t_ra, t_ag] + ([prev_xn1_free[sx]] if prev_xn1_free[sx] else []))
        prev_bn_free[s2] = t_xn1
        d["t_xn1"] = t_xn1

    def U2(i):
        d = TI[i]
        sx = (i + 2) % 3
        t_xT1, t_tr1 = transpose_chunks(xn1b[sx], xT1b[sx], 8, d["t_xn1"],
                                        prev_xT1_free[sx])
        prev_xn1_free[sx] = t_tr1
        d["t_xT1"] = t_xT1

    def U3a(i):
        d = TI[i]
        sx = (i + 2) % 3
        t_hcmm = proj(hcps, "hc", xT1b[sx], 0, WcT, 1024, 2, 8, bcrow, t_bc,
                      d["t_xT1"], t_wc)
        free["hc"] = t_hcmm
        prev_xT1_free[sx] = t_hcmm
        d["t_hcmm"] = t_hcmm

    def U3b(i):
        d = TI[i]
        s4 = (i + 2) % 4
        st = d["st"]
        t_hc = sc(lambda s_, s4=s4, st=st: s_.activation(
            hcb[s4][:, :], hcps[:, :], ACT.Identity, accum_out=st[:, S1:S1 + 1]),
            [d["t_hcmm"]] + ([prev_hc_free[s4]] if prev_hc_free[s4] else []))
        free["hc"] = t_hc
        d["t_hc"] = t_hc

    def U4(i):
        d = TI[i]
        s4 = (i + 2) % 4
        sq = i % 2
        st = d["st"]
        t_s2 = ve(lambda v, s4=s4, st=st, sq=sq: v.scalar_tensor_tensor(
            sqb[sq][:, :], hcb[s4][:, :], 1.0, hcb[s4][:, :], AL.mult, AL.mult,
            accum_out=st[:, S2:S2 + 1]),
            [d["t_hc"]] + ([prev_sq_free[sq]] if prev_sq_free[sq] else []))
        t_a2, t_mu2 = ln_stats_math_st(st, d["t_hc"], t_s2, DINV)
        d["t_s2"] = t_s2
        d["t_a2"] = t_a2
        d["t_mu2"] = t_mu2

    def U5(i):
        d = TI[i]
        s4 = (i + 2) % 4
        s6 = (i + 2) % 7
        s5 = (i + 2) % 6
        sq = i % 2
        st = d["st"]
        t_t2 = gp(lambda g, s4=s4, st=st, sq=sq: g.tensor_scalar(
            sqb[sq][:, :], hcb[s4][:, :], st[:, MU:MU + 1], st[:, A:A + 1],
            AL.subtract, AL.mult),
            [d["t_a2"], d["t_mu2"], d["t_s2"]] +
            ([prev_sq_free[sq]] if prev_sq_free[sq] else []))
        t_x2r = ve(lambda v, s5=s5, s6=s6, st=st, sq=sq: v.scalar_tensor_tensor(
            x2rb[s6][:, :], sqb[sq][:, :], 1.0, xb[s5][:, :], AL.mult, AL.add,
            accum_out=st[:, S1:S1 + 1]),
            [t_t2, d["t_x"]] + ([prev_x2r_free[s6]] if prev_x2r_free[s6] else []))
        prev_x_free[s5] = t_x2r
        t_s2b = ve(lambda v, s6=s6, st=st, sq=sq: v.scalar_tensor_tensor(
            sqb[sq][:, :], x2rb[s6][:, :], 1.0, x2rb[s6][:, :], AL.mult, AL.mult,
            accum_out=st[:, S2:S2 + 1]), [t_x2r])
        prev_sq_free[sq] = t_s2b
        t_a3, t_mu3 = ln_stats_math_st(st, t_x2r, t_s2b, DINV)
        d["t_x2r"] = t_x2r
        d["t_a3"] = t_a3
        d["t_mu3"] = t_mu3

    def U6a(i):
        d = TI[i]
        s4 = (i + 2) % 4
        s6 = (i + 2) % 7
        st = d["st"]
        t_xn2 = gp(lambda g, s4=s4, s6=s6, st=st: g.tensor_scalar(
            hcb[s4][:, :], x2rb[s6][:, :], st[:, MU:MU + 1], st[:, A:A + 1],
            AL.subtract, AL.mult), [d["t_a3"], d["t_mu3"]])
        d["t_xn2"] = t_xn2

    def U6b(i):
        d = TI[i]
        s4 = (i + 2) % 4
        sx = i % 3

        def mms(t, s4=s4):
            for k in range(8):
                last_ = t.matmul(trps[:, k * 128:(k + 1) * 128],
                                 hcb[s4][:, k * 128:(k + 1) * 128], idnb[:],
                                 start=True, stop=True)
            return last_
        t_tr2 = te(mms, [d["t_xn2"], free["tr"], t_idn])
        prev_hc_free[s4] = t_tr2
        t_xT2 = sc(lambda s_, sx=sx: s_.copy(xT2b[sx][:, :], trps[:, 0:1024]),
                   [t_tr2] + ([prev_xT2_free[sx]] if prev_xT2_free[sx] else []))
        free["tr"] = t_xT2
        d["t_xT2"] = t_xT2

    def U7(i):
        d = TI[i]
        sx = i % 3

        def mms12(t, sx=sx):
            for k_ in range(8):
                last_ = t.matmul(x12ps[:, 0:512],
                                 xT2b[sx][:, k_ * 128:(k_ + 1) * 128],
                                 sw1T[:, k_ * 512:(k_ + 1) * 512],
                                 start=(k_ == 0), stop=False)
                last_ = t.matmul(x12ps[:, 512:1024],
                                 xT2b[sx][:, k_ * 128:(k_ + 1) * 128],
                                 sw2T[:, k_ * 512:(k_ + 1) * 512],
                                 start=(k_ == 0), stop=False)
            last_ = t.matmul(x12ps[:, 0:512], ones_row[0:1, :], sb1row[0:1, :],
                             start=False, stop=True)
            last_ = t.matmul(x12ps[:, 512:1024], ones_row[0:1, :], sb2row[0:1, :],
                             start=False, stop=True)
            return last_
        t_x2mm = te(mms12, [d["t_xT2"], t_sb1, t_sb2, t_sw12, t_ones, free["x12"]])
        free["x12"] = t_x2mm
        prev_xT2_free[sx] = t_x2mm
        d["t_x1"] = t_x2mm
        d["t_x2mm"] = t_x2mm

    def U8a(i):
        d = TI[i]
        s3 = i % 3
        ss = i % 2
        sq = i % 2
        st = d["st"]
        t_silu = sc(lambda s_, ss=ss: s_.activation(
            silub[ss][:, :], x12ps[:, 0:512], ACT.Silu),
            [d["t_x1"]] + ([prev_silu_free[ss]] if prev_silu_free[ss] else []))
        t_g = ve(lambda v, s3=s3, ss=ss, st=st: v.scalar_tensor_tensor(
            gb[s3][:, :], x12ps[:, 512:1024], 1.0, silub[ss][:, :], AL.mult, AL.mult,
            accum_out=st[:, S1:S1 + 1]),
            [d["t_x2mm"], t_silu] + ([prev_g_free[s3]] if prev_g_free[s3] else []))
        free["x12"] = t_g
        prev_silu_free[ss] = t_g
        t_gs2 = ve(lambda v, s3=s3, st=st, sq=sq: v.scalar_tensor_tensor(
            sqb[sq][:, 0:512], gb[s3][:, :], 1.0, gb[s3][:, :], AL.mult, AL.mult,
            accum_out=st[:, S2:S2 + 1]),
            [t_g] + ([prev_sq_free[sq]] if prev_sq_free[sq] else []))
        prev_sq_free[sq] = t_gs2
        d["t_g"] = t_g
        d["t_gs2"] = t_gs2

    def U8b(i):
        d = TI[i]
        s3 = i % 3
        st = d["st"]
        t_agk, t_mug = ln_stats_math_st(st, d["t_g"], d["t_gs2"], HINV)
        t_z = ve(lambda v, s3=s3, st=st: v.tensor_scalar(
            gb[s3][:, :], gb[s3][:, :], st[:, MU:MU + 1], st[:, A:A + 1],
            AL.subtract, AL.mult), [t_agk, t_mug, d["t_gs2"]])
        d["t_z"] = t_z

    def U9a(i):
        d = TI[i]
        s3 = i % 3
        sz = i % 2
        do_ev, t_ztr = transpose_chunks(gb[s3], zTb[sz], 4, d["t_z"],
                                        prev_zT_free[sz], off=0, defer=True)
        prev_g_free[s3] = t_ztr

        def fin(d=d, do_ev=do_ev):
            d["t_zT"] = do_ev()
        pend_evac.append(fin)

    def U9b(i):
        d = TI[i]
        sz = i % 2
        t_fmm = proj(fps, "f", zTb[sz], 0, sw3Tg, 1024, 2, 4, vbrow_bf, t_vb,
                     d["t_zT"], t_g3)
        free["f"] = t_fmm
        prev_zT_free[sz] = t_fmm
        d["t_fmm"] = t_fmm

    def U10(i):
        nonlocal last_store
        d = TI[i]
        sf = i % 3
        s6 = (i + 2) % 7
        so = i % 2
        sq = i % 2
        st = d["st"]
        t_f = sc(lambda s_, sf=sf, st=st: s_.activation(
            fb[sf][:, :], fps[:, :], ACT.Identity, accum_out=st[:, S1:S1 + 1]),
            [d["t_fmm"], prev_fb_free[sf]])
        free["f"] = t_f
        t_fs2 = ve(lambda v, sf=sf, st=st, sq=sq: v.scalar_tensor_tensor(
            sqb[sq][:, :], fb[sf][:, :], 1.0, fb[sf][:, :], AL.mult, AL.mult,
            accum_out=st[:, S2:S2 + 1]),
            [t_f] + ([prev_sq_free[sq]] if prev_sq_free[sq] else []))
        prev_sq_free[sq] = t_fs2
        t_a4, t_mu4 = ln_stats_math_st(st, t_f, t_fs2, DINV)
        t_a4m = ve(lambda v, st=st, i=i: v.tensor_mul(
            st[:, A4M:A4M + 1], st[:, A:A + 1], ivm_all[:, i:i + 1]),
            [t_a4, selh["t"]])
        t_t4 = ve(lambda v, sf=sf, st=st: v.tensor_scalar(
            fb[sf][:, :], fb[sf][:, :], st[:, MU:MU + 1], st[:, A4M:A4M + 1],
            AL.subtract, AL.mult), [t_a4m, t_mu4, t_fs2])
        w5 = [t_t4, d["t_x2r"]] + ([prev_out_free[so]] if prev_out_free[so] else [])
        t_t5a = gp(lambda g, sf=sf, s6=s6, so=so: g.tensor_tensor(
            outb[so][:, 0:512], fb[sf][:, 0:512], x2rb[s6][:, 0:512], AL.add), w5)
        t_t5b = ve(lambda v, sf=sf, s6=s6, so=so: v.tensor_tensor(
            outb[so][:, 512:1024], fb[sf][:, 512:1024], x2rb[s6][:, 512:1024],
            AL.add), w5)
        t_st = sdma(out_d[i * 128:(i + 1) * 128, :], outb[so][:, :],
                    waits=(t_t5a, t_t5b), sem=f"st{so}")
        prev_fb_free[sf] = t_st
        prev_x2r_free[s6] = t_st
        prev_out_free[so] = t_st
        last_store = t_st

    exp = {"ua1": 0, "ua2": 0, "ub1": 0, "ub": 0, "vbe_due": None, "t_cc": None}

    def emit_r_and_cc():
        for j in range(2):
            st = estt[j]
            bs = bnstr[j]
            t_b1 = ve(lambda v, j=j, bs=bs: v.bn_stats(bs[:, 0:6], accf[j][:, 0:512]),
                      [t_last_acc])
            t_b2 = ve(lambda v, j=j, bs=bs: v.bn_stats(bs[:, 6:12], accf[j][:, 512:1024]),
                      [t_b1])
            t_ag2 = ve(lambda v, bs=bs: v.bn_aggr(bs[:, 10:12], bs[:, 0:12]), [t_b2])
            t_sg = sc(lambda s_, st=st, bs=bs: s_.activation(
                st[:, SIG:SIG + 1], bs[:, 11:12], ACT.Sqrt, bias=eps_t[:]),
                [t_ag2, t_eps])
            t_a = ve(lambda v, st=st: v.reciprocal(st[:, A:A + 1], st[:, SIG:SIG + 1]),
                     [t_sg])
            t_r = ve(lambda v, j=j, st=st, bs=bs: v.tensor_scalar(
                fb[j][:, :], accf[j][:, :], bs[:, 10:11], st[:, A:A + 1],
                AL.subtract, AL.mult),
                [t_a, t_ag2, prev_fb_free[j]])
            gdma(ag_in_d.ap()[j * 128:(j + 1) * 128, :], fb[j][:, :],
                 waits=(t_r,), sem="rst")
        t_rall2 = ("rst", planned["rst"])
        prev_fb_free[0] = t_rall2
        prev_fb_free[1] = t_rall2
        exp["t_cc"] = emit("cc", lambda g: g.collective_compute(
            "AllGather", AL.bypass,
            replica_groups=[list(range(NC))],
            ins=[ag_in_d.ap().opt()],
            outs=[ag_out_d.ap()[0:M, :].opt()]),
            (t_rall2, t_agz), 1)

    zz_tok = {}
    pend_evac = []

    def flush_evacs():
        for f in pend_evac:
            f()
        pend_evac.clear()

    def pump_A2():
        if exp["ua2"] >= exp["ua1"] or exp["ua2"] >= len(units):
            return
        e, j = units[exp["ua2"]]
        zz_tok[(e, j)] = emit_expert_A2(e, j)
        exp["ua2"] += 1

    def pump_B1(off=512, defer=True):
        if (exp["ub1"] >= exp["ua2"] or exp["ub1"] >= len(units)
                or exp["ub1"] >= exp["ub"] + 2):
            return
        e, j = units[exp["ub1"]]
        emit_expert_B1(e, j, zz_tok[(e, j)], off, defer)
        exp["ub1"] += 1

    def pump_B2(lag=None):
        if lag is None:
            lag = 1 if exp["ub1"] < len(units) else 0
        if exp["ub"] >= exp["ub1"] - lag or exp["ub"] >= len(units):
            return
        e, j = units[exp["ub"]]
        emit_expert_B2(e, j, zz_tok[(e, j)])
        if j == 1:
            if e + 1 < E:
                emit_ew3_load(e + 1)
                exp["vbe_due"] = e + 1
            else:
                emit_r_and_cc()
        exp["ub"] += 1

    def pump_A1():
        if exp["ua1"] >= len(units):
            return
        e, j = units[exp["ua1"]]
        emit_expert_A1(e, j)
        if j == 0 and 2 <= e + 1 < E:
            emit_ew12_loads(e + 1)
        exp["ua1"] += 1

    def pump_vbe():
        if exp["vbe_due"] is not None:
            emit_prep_vbe(exp["vbe_due"])
            exp["vbe_due"] = None

    NS = 13
    FRONT = 10
    for i in range(min(3, n_tiles)):
        U0(i)
    for t_it in range(n_tiles + NS - 1):
        if t_it == 3 and n_tiles:
            emit_sel_batch(0)
        if t_it in (4, 5, 6) and n_tiles:
            emit_sel_batch(t_it - 3)
        if t_it == 7 and n_tiles:
            emit_ivm()
        if 0 <= t_it - 12 < n_tiles:
            U10(t_it - 12)
        flush_evacs()
        if 0 <= t_it - 4 < n_tiles:
            U3b(t_it - 4)
        if 0 <= t_it - 10 < n_tiles:
            U8b(t_it - 10)
        pump_A2()
        if t_it < FRONT:
            pump_A2()
        if 0 <= t_it < n_tiles:
            U1(t_it)
        if 0 <= t_it - 4 < n_tiles:
            U4(t_it - 4)
        if 0 <= t_it - 5 < n_tiles:
            U5(t_it - 5)
        if 0 <= t_it - 6 < n_tiles:
            U6a(t_it - 6)
        if 0 <= t_it - 9 < n_tiles:
            U8a(t_it - 9)
        pump_vbe()
        if t_it < FRONT:
            pump_B2(0)
            pump_vbe()
            pump_B2(0)
            pump_B1(512, defer=False)
            pump_B1(0, defer=False)
        else:
            pump_B2()
        if 0 <= t_it - 11 < n_tiles:
            U9b(t_it - 11)
        if 0 <= t_it - 7 < n_tiles:
            U6b(t_it - 7)
        if 0 <= t_it - 3 < n_tiles:
            U3a(t_it - 3)
        if 0 <= t_it - 1 < n_tiles:
            U2(t_it - 1)
        pump_A1()
        if t_it < FRONT:
            pump_A1()
        if 0 <= t_it - 8 < n_tiles:
            U7(t_it - 8)
        if 0 <= t_it - 10 < n_tiles:
            U9a(t_it - 10)
        pump_B1()
        if 3 <= t_it + 1 < n_tiles:
            U0(t_it + 1)
    while exp["ub"] < len(units):
        flush_evacs()
        pump_A2()
        pump_vbe()
        pump_B2()
        pump_B1()
        pump_A1()
    flush_evacs()

    ST0 = ("st0", planned["st0"])
    ST1 = ("st1", planned["st1"])
    PFB = [prev_fb_free[0], prev_fb_free[1]]
    t_ps = None
    t_rm = [None, None]
    t_scat2 = [None, None]
    for jj in range(16):
        s = jj % 2
        t_ld = gdma(fb[s][:, :], ag_out_d.ap()[jj * 128:(jj + 1) * 128, :],
                    waits=tuple(x for x in (exp["t_cc"], t_rm[s], PFB[s]) if x),
                    sem="pat0")
        t_m = ve(lambda v, s=s, jj=jj: v.tensor_single_scalar(
            outb[s][:, :], fb[s][:, :], vmask[:, jj:jj + 1], AL.mult),
            [t_ld, ST0, ST1, t_16] + [x for x in (t_scat2[s], prev_out_free[s]) if x])
        t_rm[s] = t_m
        t_ps = emit("pat1", lambda g, s=s, jj=jj: g.indirect_dma_start(
            out=out_d, out_offset=IndirectOffsetOnAxis(selt2_i[:, jj:jj + 1], 0),
            in_=outb[s][:, :], in_offset=None, compute_op=AL.add),
            tuple(x for x in (t_m, t_ps) if x), 16)
        t_scat2[s] = t_ps
    if n_tiles:
        last_store = t_ps

    if last_store is None:
        last_store = t_cc
    queues["gpsimd"].append(lambda g: _wait(g, "gpsimd", [last_store]))
    queues["sync"].append(lambda sy: _wait(sy, "sync", [last_store]))
    queues["vector"].append(lambda v: _wait(v, "vector", [last_store]))
    queues["scalar"].append(lambda s_: _wait(s_, "scalar", [last_store]))
    queues["tensor"].append(lambda t: _wait(t, "tensor", [last_store]))

    @blk.gpsimd
    def _(g):
        for f in queues["gpsimd"]:
            f(g)

    @blk.sync
    def _(sy):
        for f in queues["sync"]:
            f(sy)

    @blk.tensor
    def _(t):
        for f in queues["tensor"]:
            f(t)

    @blk.vector
    def _(v):
        for f in queues["vector"]:
            f(v)

    @blk.scalar
    def _(s_):
        for f in queues["scalar"]:
            f(s_)

    es.close()
    return nc


def make_in_maps(inputs):
    i = inputs
    x = np.ascontiguousarray(np.asarray(i["L0_attn_x"], np.float32).reshape(TOK, D))
    common = {
        "x_full": x,
        "W_in": np.ascontiguousarray(np.asarray(i["L0_attn_W"], np.float32)),
        "U_in": np.triu(np.ones((D, D), np.float32)),
        "bias_in": np.ascontiguousarray(np.asarray(i["L0_attn_bias"], np.float32)),
        "sw1T_in": np.ascontiguousarray(np.asarray(i["L0_ffn_sw1"], np.float32).T),
        "sw2T_in": np.ascontiguousarray(np.asarray(i["L0_ffn_sw2"], np.float32).T),
        "sw3T_in": np.ascontiguousarray(np.asarray(i["L0_ffn_sw3"], np.float32).T),
        "sb1_in": np.ascontiguousarray(np.asarray(i["L0_ffn_sb1"], np.float32)),
        "sb2_in": np.ascontiguousarray(np.asarray(i["L0_ffn_sb2"], np.float32)),
        "sb3_in": np.ascontiguousarray(np.asarray(i["L0_ffn_sb3"], np.float32)),
        "slng_in": np.ascontiguousarray(np.asarray(i["L0_ffn_s_ln_g"], np.float32)),
        "slnb_in": np.ascontiguousarray(np.asarray(i["L0_ffn_s_ln_b"], np.float32)),
        "geT_in": np.ascontiguousarray(
            np.concatenate([np.asarray(i["L0_ffn_sg_w"], np.float32),
                            np.asarray(i["L0_ffn_eg_w"], np.float32)], axis=0).T),
        "geb_in": np.ascontiguousarray(
            np.concatenate([np.asarray(i["L0_ffn_sg_b"], np.float32),
                            np.asarray(i["L0_ffn_eg_b"], np.float32)], axis=0)),
        "expb_in": np.ascontiguousarray(
            np.broadcast_to(np.asarray(i["L0_ffn_exp_bias"], np.float32)[None, :],
                            (128, E))),
        "ew1t_in": np.ascontiguousarray(
            np.asarray(i["L0_ffn_ew1"], np.float32).transpose(0, 2, 1)),
        "ew2t_in": np.ascontiguousarray(
            np.asarray(i["L0_ffn_ew2"], np.float32).transpose(0, 2, 1)),
        "ew3t_in": np.ascontiguousarray(
            np.asarray(i["L0_ffn_ew3"], np.float32).transpose(0, 2, 1)),
        "eb1_in": np.ascontiguousarray(np.asarray(i["L0_ffn_eb1"], np.float32)),
        "eb2_in": np.ascontiguousarray(np.asarray(i["L0_ffn_eb2"], np.float32)),
        "eb3_in": np.ascontiguousarray(np.asarray(i["L0_ffn_eb3"], np.float32)),
        "selg_in": np.ascontiguousarray(
            np.asarray(i["L0_ffn_sel_ln_g"], np.float32).reshape(4, 128).T),
        "selb_in": np.ascontiguousarray(
            np.asarray(i["L0_ffn_sel_ln_b"], np.float32).reshape(4, 128).T),
        "slngc_in": np.ascontiguousarray(
            np.asarray(i["L0_ffn_s_ln_g"], np.float32).reshape(4, 128).T),
        "slnbc_in": np.ascontiguousarray(
            np.asarray(i["L0_ffn_s_ln_b"], np.float32).reshape(4, 128).T),
        "biasc_in": np.ascontiguousarray(
            np.asarray(i["L0_attn_bias"], np.float32).reshape(8, 128).T),
        "ibf_in": np.ascontiguousarray(
            np.asarray(i["L0_ffn_index_b"], np.int32).reshape(16, 128).T),
        "itf_in": np.ascontiguousarray(
            np.asarray(i["L0_ffn_index_t"], np.int32).reshape(16, 128).T),
        "idn_in": np.eye(128, dtype=np.float32),
    }
    ib = np.ascontiguousarray(np.asarray(i["L0_ffn_index_b"], np.int32))
    it = np.ascontiguousarray(np.asarray(i["L0_ffn_index_t"], np.int32))
    maps = []
    for cid in range(NC):
        mm = dict(common)
        mm["x_shard"] = np.ascontiguousarray(x[cid * TPC:(cid + 1) * TPC])
        mm["ibm_in"] = np.ascontiguousarray(ib[cid * MPC:(cid + 1) * MPC])
        mm["itm_in"] = np.ascontiguousarray(it[cid * MPC:(cid + 1) * MPC])
        mm["coreoff_in"] = np.full((128, 1), float(cid * TPC), np.float32)
        maps.append(mm)
    return maps


def kernel(**inputs):
    nc = build_program(NT)
    in_maps = make_in_maps(inputs)
    res = run_bass_kernel_spmd(nc, in_maps, core_ids=list(range(NC)))
    out = np.concatenate([np.asarray(r["out"], np.float32) for r in res.results], axis=0)
    return out.reshape(B, S, D)


# version 21
# speedup vs baseline: 1.5980x; 1.5980x over previous
import numpy as np
from contextlib import ExitStack

import concourse.bass as bass
import concourse.mybir as mybir
from concourse.bass_core_rust import IndirectOffsetOnAxis
from concourse.bass_utils import run_bass_kernel_spmd

F32 = mybir.dt.float32
BF16 = mybir.dt.bfloat16
I32 = mybir.dt.int32
AL = mybir.AluOpType
ACT = mybir.ActivationFunctionType
AX = mybir.AxisListType.X

NC = 8
B, S, D = 2, 16384, 1024
TOK = B * S
TPC = TOK // NC
NT = TPC // 128
M, E, H = 2048, 8, 512
MPC = M // NC
EPS = 1e-5
DINV = 1.0 / D
HINV = 1.0 / H

ENG2Q = {"te": "tensor", "ve": "vector", "sc": "scalar", "gpc": "gpsimd",
         "cc": "gpsimd", "ldw": "gpsimd", "lsw": "gpsimd", "lgc": "gpsimd",
         "lsc": "sync",
         "ldx0": "gpsimd", "ldx1": "gpsimd", "ldx2": "gpsimd", "ldx3": "gpsimd",
         "ldx4": "gpsimd", "ldx5": "gpsimd", "mgx": "gpsimd", "sel0": "gpsimd",
         "scat": "gpsimd", "rst": "gpsimd", "ew0": "gpsimd",
         "ew1": "gpsimd", "ew3q": "gpsimd", "ebq": "sync",
         "pat0": "gpsimd", "pat1": "gpsimd",
         "st0": "sync", "st1": "sync"}


def build_program(n_tiles=NT):
    nc = bass.Bass("TRN2", target_bir_lowering=False)

    def din(name, shape, dt=F32):
        return nc.dram_tensor(name, shape, dt, kind="ExternalInput").ap()

    x_shard = din("x_shard", [TPC, D])
    x_full = din("x_full", [TOK, D])
    W_in = din("W_in", [D, D])
    U_in = din("U_in", [D, D])
    bias_in = din("bias_in", [D])
    sw1T_in = din("sw1T_in", [D, H])
    sw2T_in = din("sw2T_in", [D, H])
    sw3T_in = din("sw3T_in", [H, D])
    sb1_in = din("sb1_in", [H])
    sb2_in = din("sb2_in", [H])
    sb3_in = din("sb3_in", [D])
    slng_in = din("slng_in", [H])
    slnb_in = din("slnb_in", [H])
    geT_in = din("geT_in", [D, 9])
    geb_in = din("geb_in", [9])
    expb_in = din("expb_in", [128, E])
    ew1t_in = din("ew1t_in", [E, D, H])
    ew2t_in = din("ew2t_in", [E, D, H])
    ew3t_in = din("ew3t_in", [E, H, D])
    eb1_in = din("eb1_in", [E, H])
    eb2_in = din("eb2_in", [E, H])
    eb3_in = din("eb3_in", [E, D])
    selg_in = din("selg_in", [128, 4])
    selb_in = din("selb_in", [128, 4])
    slngc_in = din("slngc_in", [128, 4])
    slnbc_in = din("slnbc_in", [128, 4])
    biasc_in = din("biasc_in", [128, 8])
    ibm_in = din("ibm_in", [MPC], I32)
    itm_in = din("itm_in", [MPC], I32)
    ibf_in = din("ibf_in", [128, 16], I32)
    itf_in = din("itf_in", [128, 16], I32)
    coreoff_in = din("coreoff_in", [128, 1])
    idn_in = din("idn_in", [128, 128])
    out_d = nc.dram_tensor("out", [TPC, D], F32, kind="ExternalOutput").ap()

    ag_in_d = nc.dram_tensor("ag_in", [MPC, D], BF16, kind="Internal")
    ag_out_d = nc.dram_tensor("ag_out", [M + 1, D], BF16, kind="Internal")
    sel_d = nc.dram_tensor("sel_d", [TPC + 132], I32, kind="Internal")

    es = ExitStack()
    sb = lambda name, shape, dt=BF16: es.enter_context(nc.sbuf_tensor(name, shape, dt))
    psum = lambda name, shape: es.enter_context(nc.psum_tensor(name, shape, F32))

    sems = {k: es.enter_context(nc.semaphore(k)) for k in
            ("te", "ve", "sc", "gpc", "cc", "ldw", "lsw", "lgc", "lsc", "ldx0",
             "ldx1", "ldx2", "ldx3", "ldx4", "ldx5", "mgx", "sel0", "scat", "rst",
             "ew0", "ew1", "ew3q", "ebq", "pat0", "pat1", "st0", "st1")}
    planned = {k: 0 for k in sems}
    emitted = {k: 0 for k in sems}
    waited = {}
    queues = {"gpsimd": [], "sync": [], "tensor": [], "vector": [], "scalar": []}

    def _wait(eng_obj, qname, toks):
        for tok in toks:
            if tok is None:
                continue
            sname, val = tok
            key = (qname, sname)
            if waited.get(key, -1) >= val:
                continue
            eng_obj.wait_ge(sems[sname], val)
            waited[key] = val

    def emit(sem_name, fn, waits=(), inc=1):
        q = ENG2Q[sem_name]
        planned[sem_name] += inc
        val = planned[sem_name]
        ws = tuple(waits)

        def run(eng):
            _wait(eng, q, ws)
            instr = fn(eng)
            instr.then_inc(sems[sem_name], inc)
            emitted[sem_name] += inc
            assert emitted[sem_name] == val, (sem_name, emitted[sem_name], val)
        queues[q].append(run)
        return (sem_name, val)

    def emit_seq(sem_name, fn, waits=(), inc=1):
        return emit(sem_name, fn, waits, inc)

    def gdma(dst, src, waits=(), sem="ldw", **kw):
        return emit(sem, lambda g: g.dma_start(out=dst, in_=src, **kw), waits, 16)

    def sdma(dst, src, waits=(), sem="lsc", **kw):
        return emit(sem, lambda sy: sy.dma_start(out=dst, in_=src, **kw), waits, 16)

    def ve(fn, waits=()):
        return emit("ve", fn, waits)

    def sc(fn, waits=()):
        return emit("sc", fn, waits)

    def te(fn, waits=()):
        return emit("te", fn, waits)

    def gp(fn, waits=()):
        return emit("gpc", fn, waits)

    WcT = sb("WcT", [128, 8 * 1024])
    sw1T = sb("sw1T", [128, 8 * 512])
    sw2T = sb("sw2T", [128, 8 * 512])
    sw3Tg = sb("sw3Tg", [128, 4 * 1024])
    geT = sb("geT", [128, 8 * 16])
    idnb = sb("idnb", [128, 128])
    ones_row = sb("ones_row", [1, 128])
    bcrow = sb("bcrow", [1, D])
    vbrow = sb("vbrow", [1, D], F32)
    vbrow2 = [sb("vbrow2_0", [1, D], F32)]
    vbrow_bf = sb("vbrow_bf", [1, D])
    sb1row = sb("sb1row", [1, H])
    sb2row = sb("sb2row", [1, H])
    gebrow = sb("gebrow", [1, 16])
    sb3row = sb("sb3row", [1, D], F32)
    eps_t = sb("eps_t", [128, 1], F32)
    expb = sb("expb", [128, E], F32)
    coreoff = sb("coreoff", [128, 1], F32)
    zrow = sb("zrow", [1, D])
    selg_c = sb("selg_c", [128, 4], F32)
    selb_c = sb("selb_c", [128, 4])
    slnb_c = sb("slnb_c", [128, 4])
    slng_c = sb("slng_c", [128, 4], F32)
    biasc = sb("biasc", [128, 8])

    es_build = ExitStack()
    Wsb = es_build.enter_context(nc.sbuf_tensor("Wsb", [128, 8 * 1024], BF16))
    Usb = es_build.enter_context(nc.sbuf_tensor("Usb", [128, 8 * 1024], BF16))
    es_build.close()

    ew1s = [sb("ew1s0", [128, 8 * 512]), sb("ew1s1", [128, 8 * 512])]
    ew2s = [sb("ew2s0", [128, 8 * 512]), sb("ew2s1", [128, 8 * 512])]
    ew3s = [sb("ew3s0", [128, 4 * 1024])]
    xT2b = [sb(f"xT2b{s}", [128, D]) for s in range(3)]
    eb1r = [sb("eb1r0", [1, H]), sb("eb1r1", [1, H])]
    eb2r = [sb("eb2r0", [1, H]), sb("eb2r1", [1, H])]
    eb3r = [sb("eb3r0", [1, D], F32)]
    vber = [sb("vber0", [1, D])]

    xb = [sb(f"xb{s}", [128, D]) for s in range(6)]
    xn1b = [sb(f"xn1b{s}", [128, D]) for s in range(3)]
    xT1b = [sb(f"xT1b{s}", [128, D]) for s in range(3)]
    hcb = [sb(f"hcb{s}", [128, D]) for s in range(4)]
    x2rb = [sb(f"x2rb{s}", [128, D]) for s in range(7)]
    sqb_ = sb("sqb_", [128, D])
    sqb2 = sb("sqb2", [128, D])
    sqb = [sqb_, sqb2]
    silub = [sb(f"silub{s}", [128, H]) for s in range(2)]
    gb = [sb(f"gb{s}", [128, H]) for s in range(3)]
    zTb = [sb(f"zTb{s}", [128, H]) for s in range(2)]
    fb = [sb(f"fb{s}", [128, D]) for s in range(3)]
    outb = [sb(f"outb{s}", [128, D], F32) for s in range(2)]
    stt = [sb(f"stt{s}", [128, 16], F32) for s in range(10)]
    bnst = [sb(f"bnst{s}", [128, 16], F32) for s in range(2)]
    bnstr = [sb(f"bnstr{s}", [128, 12], F32) for s in range(2)]
    selall_i = sb("selall_i", [128, max(1, n_tiles)], I32)
    ivm_all = sb("ivm_all", [128, max(1, n_tiles)], F32)
    self_all = sb("self_all", [128, max(1, n_tiles)], F32)

    S1, S2, TMP, MU, SIG, A, PM, IVM, A4M, SELF, SHW, EXW, TOP1, MX = range(14)
    BETA = 14

    xmT = sb("xmT", [128, 2 * D])
    estt = [sb(f"estt{j}", [128, 16], F32) for j in range(2)]
    egb = [sb(f"egb{j}", [128, H]) for j in range(2)]
    esilub = [sb(f"esilub{j}", [128, H]) for j in range(2)]
    ezTb = [sb(f"ezTb{j}", [128, H]) for j in range(2)]
    accf = [sb(f"accf{j}", [128, D]) for j in range(2)]
    scoresb = [sb(f"scores{j}", [128, 16], F32) for j in range(2)]
    maskb = [sb(f"maskb{j}", [128, E], F32) for j in range(2)]
    maskpb = [sb(f"maskpb{j}", [128, E], F32) for j in range(2)]
    m8 = sb("m8", [128, E], F32)
    ibm_sb = sb("ibm_sb", [128, 2], I32)
    itm_sb = sb("itm_sb", [128, 2], I32)
    idxf = sb("idxf", [128, 8], F32)
    gmi = sb("gmi", [128, 2], I32)

    ibf_sb = sb("ibf_sb", [128, 16], I32)
    itf_sb = sb("itf_sb", [128, 16], I32)
    bfv = sb("bfv", [128, 16], F32)
    tfv = sb("tfv", [128, 16], F32)
    vmask = sb("vmask", [128, 16], F32)
    tmp16 = sb("tmp16", [128, 16], F32)
    gflat = sb("gflat", [128, 16], F32)
    selt_i = sb("selt_i", [128, 16], I32)
    selt2_i = sb("selt2_i", [128, 16], I32)
    miota = sb("miota", [128, 16], I32)
    selinit = sb("selinit", [128, 33], I32)
    dumrow = sb("dumrow", [128, 16], F32)
    dumi16 = sb("dumi16", [128, 16], I32)

    hcps = psum("hcps", [128, 1024])
    trps = psum("trps", [128, 1024])
    x12ps = psum("x12ps", [128, 1024])
    fps = psum("fps", [128, 1024])

    blk = es.enter_context(nc.Block())

    free = {"tr": None, "hc": None, "x12": None, "f": None}

    t_ones = gp(lambda g: g.memset(ones_row[:], 1.0))
    t_eps = ve(lambda v: v.memset(eps_t[:], EPS))
    t_zrow = ve(lambda v: v.memset(zrow[:], 0.0))
    t_selinit = ve(lambda v: v.memset(selinit[:], M))
    gdma(idnb[:], idn_in, sem="lgc")
    gdma(sb1row[0:1, :], sb1_in[None, :], sem="lgc")
    gdma(sb2row[0:1, :], sb2_in[None, :], sem="lgc")
    gdma(gebrow[0:1, 0:9], geb_in[None, :], sem="lgc")
    gdma(selb_c[:], selb_in, sem="lgc")
    gdma(slnb_c[:], slnbc_in, sem="lgc")
    gdma(biasc[:], biasc_in, sem="lgc")
    LGC_ALL = ("lgc", planned["lgc"])
    sdma(expb[:], expb_in)
    sdma(coreoff[:], coreoff_in)
    sdma(sb3row[0:1, :], sb3_in[None, :])
    sdma(selg_c[:], selg_in)
    sdma(slng_c[:], slngc_in)
    sdma(ibf_sb[:, :], ibf_in)
    sdma(itf_sb[:, :], itf_in)
    sdma(bass.AP(sel_d, 0, [[33, 128], [1, 33]]), selinit[:, :], waits=(t_selinit,))
    sdma(ibm_sb[:, 0:1], ibm_in[0:128][:, None])
    sdma(ibm_sb[:, 1:2], ibm_in[128:256][:, None])
    sdma(itm_sb[:, 0:1], itm_in[0:128][:, None])
    sdma(itm_sb[:, 1:2], itm_in[128:256][:, None])
    t_agz = sdma(ag_out_d.ap()[M:M + 1, :], zrow[0:1, :], waits=(t_zrow,))

    for k in range(8):
        gdma(Wsb[:, k * 1024:(k + 1) * 1024], W_in[k * 128:(k + 1) * 128, :])
    for k in range(8):
        gdma(Usb[:, k * 1024:(k + 1) * 1024], U_in[k * 128:(k + 1) * 128, :])
    LDW_ALL = ("ldw", planned["ldw"])
    for k in range(8):
        gdma(sw1T[:, k * 512:(k + 1) * 512], sw1T_in[k * 128:(k + 1) * 128, :],
             sem="lsw")
        gdma(sw2T[:, k * 512:(k + 1) * 512], sw2T_in[k * 128:(k + 1) * 128, :],
             sem="lsw")
    for k in range(4):
        gdma(sw3Tg[:, k * 1024:(k + 1) * 1024], sw3T_in[k * 128:(k + 1) * 128, :],
             sem="lsw")
    for k in range(8):
        gdma(geT[:, k * 16:k * 16 + 9], geT_in[k * 128:(k + 1) * 128, :], sem="lsw")
    LSW_ALL = ("lsw", planned["lsw"])
    LSC_ALL = ("lsc", planned["lsc"])

    t_idn = LGC_ALL
    t_sb1 = t_sb2 = t_geb = LGC_ALL
    t_selb = t_slnb = t_biasc = LGC_ALL
    t_selg = t_slng = LSC_ALL
    t_w8 = t_u8 = LDW_ALL
    t_sw12 = t_sw3l = t_gel = LSW_ALL
    t_agz = LSC_ALL
    t_expb = t_coff = t_sb3 = LSC_ALL
    t_ibf = t_itf = t_sel0 = LSC_ALL

    wc_evs = []
    for gi, (m_, n_) in enumerate((m_, n_) for m_ in range(8) for n_ in range(2)):
        off = 512 * (gi % 2)

        def mms(t, m_=m_, n_=n_, off=off):
            for k_ in range(8):
                last_ = t.matmul(hcps[:, off:off + 512],
                         Wsb[:, k_ * 1024 + m_ * 128: k_ * 1024 + (m_ + 1) * 128],
                         Usb[:, k_ * 1024 + n_ * 512: k_ * 1024 + (n_ + 1) * 512],
                         start=(k_ == 0), stop=(k_ == 7))
            return last_
        w = [t_w8, t_u8]
        if len(wc_evs) >= 2:
            w.append(wc_evs[-2])
        tmm = te(mms, w)
        wc_evs.append(sc(lambda s_, m_=m_, n_=n_, off=off: s_.copy(
            WcT[:, m_ * 1024 + n_ * 512: m_ * 1024 + (n_ + 1) * 512],
            hcps[:, off:off + 512]), [tmm]))
    t_wc = wc_evs[-1]

    def bc_mms(t):
        for n_ in range(2):
            for k_ in range(8):
                last_ = t.matmul(hcps[0:1, n_ * 512:(n_ + 1) * 512],
                         biasc[:, k_:k_ + 1],
                         Usb[:, k_ * 1024 + n_ * 512: k_ * 1024 + (n_ + 1) * 512],
                         start=(k_ == 0), stop=(k_ == 7))
        return last_
    t_bcmm = te(bc_mms, [t_biasc, t_u8, t_wc])
    t_bc = ve(lambda v: v.tensor_copy(bcrow[0:1, :], hcps[0:1, 0:1024]), [t_bcmm])
    free["hc"] = t_bc

    def vb_mms(t):
        for n_ in range(2):
            for k_ in range(4):
                last_ = t.matmul(fps[0:1, n_ * 512:(n_ + 1) * 512],
                         slnb_c[:, k_:k_ + 1],
                         sw3Tg[:, k_ * 1024 + n_ * 512: k_ * 1024 + (n_ + 1) * 512],
                         start=(k_ == 0), stop=(k_ == 3))
        return last_
    t_vbmm = te(vb_mms, [t_slnb, t_sw3l])
    t_vb1 = ve(lambda v: v.tensor_copy(vbrow[0:1, :], fps[0:1, 0:1024]), [t_vbmm])
    t_vb2 = ve(lambda v: v.tensor_add(vbrow[0:1, :], vbrow[0:1, :], sb3row[0:1, :]),
               [t_vb1, t_sb3])
    t_vb = ve(lambda v: v.tensor_copy(vbrow_bf[0:1, :], vbrow[0:1, :]), [t_vb2])
    free["f"] = t_vb1

    t_g3 = None
    for k in range(4):
        t_g3 = ve(lambda v, k=k: v.tensor_single_scalar(
            sw3Tg[:, k * 1024:(k + 1) * 1024], sw3Tg[:, k * 1024:(k + 1) * 1024],
            slng_c[:, k:k + 1], AL.mult), [t_vbmm, t_slng, t_sw3l])

    t_miota = gp(lambda g: g.iota(miota[:, :], [[128, 16]], channel_multiplier=1))
    t_dumi = gp(lambda g: g.iota(dumi16[:, :], [[8, 16]], base=TPC, channel_multiplier=0,
                                 allow_small_or_imprecise_dtypes=True))
    t_dum = ve(lambda v: v.tensor_copy(dumrow[:, :], dumi16[:, :]), [t_dumi])

    t_b = ve(lambda v: v.tensor_copy(bfv[:, :], ibf_sb[:, :]), [t_ibf])
    t_t = ve(lambda v: v.tensor_copy(tfv[:, :], itf_sb[:, :]), [t_itf])
    t_1 = ve(lambda v: v.tensor_single_scalar(vmask[:, :], bfv[:, :], -0.5, AL.is_gt), [t_b])
    t_2 = ve(lambda v: v.tensor_single_scalar(tmp16[:, :], bfv[:, :], B - 0.5, AL.is_lt), [t_b, t_1])
    t_3 = ve(lambda v: v.tensor_mul(vmask[:, :], vmask[:, :], tmp16[:, :]), [t_1, t_2])
    t_4 = ve(lambda v: v.tensor_single_scalar(tmp16[:, :], tfv[:, :], -0.5, AL.is_gt), [t_t, t_3])
    t_5 = ve(lambda v: v.tensor_mul(vmask[:, :], vmask[:, :], tmp16[:, :]), [t_3, t_4])
    t_6 = ve(lambda v: v.tensor_single_scalar(tmp16[:, :], tfv[:, :], S - 0.5, AL.is_lt), [t_t, t_5])
    t_7 = ve(lambda v: v.tensor_mul(vmask[:, :], vmask[:, :], tmp16[:, :]), [t_5, t_6])
    t_8 = ve(lambda v: v.scalar_tensor_tensor(gflat[:, :], bfv[:, :], float(S), tfv[:, :],
                                              AL.mult, AL.add), [t_b, t_t, t_7])
    t_9 = ve(lambda v: v.tensor_single_scalar(gflat[:, :], gflat[:, :], coreoff[:, 0:1],
                                              AL.subtract), [t_8, t_coff])
    t_10 = ve(lambda v: v.tensor_single_scalar(tmp16[:, :], gflat[:, :], -0.5, AL.is_gt), [t_9, t_7])
    t_11 = ve(lambda v: v.tensor_mul(vmask[:, :], vmask[:, :], tmp16[:, :]), [t_7, t_10])
    t_12 = ve(lambda v: v.tensor_single_scalar(tmp16[:, :], gflat[:, :], TPC - 0.5, AL.is_lt), [t_9, t_11])
    t_13 = ve(lambda v: v.tensor_mul(vmask[:, :], vmask[:, :], tmp16[:, :]), [t_11, t_12])
    t_14 = ve(lambda v: v.tensor_mul(gflat[:, :], gflat[:, :], vmask[:, :]), [t_13, t_9])
    t_14b = ve(lambda v: v.tensor_copy(selt2_i[:, :], gflat[:, :]), [t_14])
    t_15 = ve(lambda v: v.tensor_single_scalar(tmp16[:, :], vmask[:, :], -1.0, AL.mult),
              [t_13, t_14])
    t_15b = ve(lambda v: v.tensor_single_scalar(tmp16[:, :], tmp16[:, :], 1.0, AL.add),
               [t_15])
    t_15c = ve(lambda v: v.tensor_mul(tmp16[:, :], tmp16[:, :], dumrow[:, :]),
               [t_15b, t_dum])
    t_16 = ve(lambda v: v.tensor_add(gflat[:, :], gflat[:, :], tmp16[:, :]),
              [t_14, t_14b, t_15c])
    t_17 = ve(lambda v: v.tensor_copy(selt_i[:, :], gflat[:, :]), [t_16])
    scat_w = (t_17, t_miota, t_sel0)

    def emit_sel_scatters():
        for jj in range(16):
            emit("scat", lambda g, jj=jj: g.indirect_dma_start(
                out=bass.AP(sel_d, 0, [[1, TPC + 132], [1, 1]]),
                out_offset=IndirectOffsetOnAxis(selt_i[:, jj:jj + 1], 0),
                in_=miota[:, jj:jj + 1], in_offset=None),
                scat_w, 16)
        return ("scat", planned["scat"])

    def ln_stats_math(s, t_s1, t_s2, dinv):
        return ln_stats_math_st(stt[s], t_s1, t_s2, dinv)

    def ln_stats_math_st(st, t_s1, t_s2, dinv):
        t_mu = ve(lambda v: v.tensor_single_scalar(
            st[:, MU:MU + 1], st[:, S1:S1 + 1], dinv, AL.mult), [t_s1])
        t_tt = ve(lambda v: v.scalar_tensor_tensor(
            st[:, TMP:TMP + 1], st[:, S1:S1 + 1], dinv * dinv, st[:, S1:S1 + 1],
            AL.mult, AL.mult), [t_s1])
        t_v = ve(lambda v: v.scalar_tensor_tensor(
            st[:, SIG:SIG + 1], st[:, S2:S2 + 1], dinv, st[:, TMP:TMP + 1],
            AL.mult, AL.subtract), [t_s2, t_tt])
        t_sg = sc(lambda s_: s_.activation(
            st[:, SIG:SIG + 1], st[:, SIG:SIG + 1], ACT.Sqrt, bias=eps_t[:]),
            [t_v, t_eps])
        t_a = ve(lambda v: v.reciprocal(st[:, A:A + 1], st[:, SIG:SIG + 1]), [t_sg])
        return t_a, t_mu

    def ln_stats_math_b(st, t_s1, t_s2, dinv):
        t_a, t_mu = ln_stats_math_st(st, t_s1, t_s2, dinv)
        t_b = ve(lambda v: v.scalar_tensor_tensor(
            st[:, BETA:BETA + 1], st[:, MU:MU + 1], -1.0, st[:, A:A + 1],
            AL.mult, AL.mult), [t_a, t_mu])
        return t_a, t_mu, t_b

    def bn_ln_stats(s, src, width, t_src):
        st = stt[s]
        bs = bnstr[s]
        nh = (width + 511) // 512
        toks = []
        for hh in range(nh):
            w0, w1 = hh * 512, min(width, (hh + 1) * 512)
            toks.append(ve(lambda v, hh=hh, w0=w0, w1=w1: v.bn_stats(
                bs[:, hh * 6:(hh + 1) * 6], src[:, w0:w1]), [t_src] + toks[-1:]))
        t_ag = ve(lambda v, nh=nh: v.bn_aggr(bs[:, 10:12], bs[:, 0:6 * nh]), toks)
        t_mu = ve(lambda v: v.tensor_copy(st[:, MU:MU + 1], bs[:, 10:11]), [t_ag])
        t_sg = sc(lambda s_: s_.activation(
            st[:, SIG:SIG + 1], bs[:, 11:12], ACT.Sqrt, bias=eps_t[:]), [t_ag, t_eps])
        t_a = ve(lambda v: v.reciprocal(st[:, A:A + 1], st[:, SIG:SIG + 1]), [t_sg])
        return t_a, t_mu

    def transpose_chunks(src, dst, nchunk, t_src, dst_free=None, off=0,
                         defer=False):
        def mms(t):
            for k in range(nchunk):
                last_ = t.matmul(trps[:, off + k * 128:off + (k + 1) * 128],
                         src[:, k * 128:(k + 1) * 128], idnb[:],
                         start=True, stop=True)
            return last_
        tmm = te(mms, [t_src, free["tr"], t_idn])

        def do_evac(extra_free=None):
            w = [tmm]
            if dst_free:
                w.append(dst_free)
            if extra_free:
                w.append(extra_free)
            tev = sc(lambda s_: s_.copy(dst[:, 0:nchunk * 128],
                                        trps[:, off:off + nchunk * 128]), w)
            free["tr"] = tev
            return tev
        if defer:
            return do_evac, tmm
        return do_evac(), tmm

    def proj(dst_ps, psname, xT_sb, xoff, w_sb, wtile, nout, kchunks, brow, t_brow,
             t_xT, t_w):
        def mms(t):
            for k_ in range(kchunks):
                for n_ in range(nout):
                    last_ = t.matmul(dst_ps[:, n_ * 512:(n_ + 1) * 512],
                             xT_sb[:, xoff + k_ * 128: xoff + (k_ + 1) * 128],
                             w_sb[:, k_ * wtile + n_ * 512: k_ * wtile + (n_ + 1) * 512],
                             start=(k_ == 0), stop=False)
            for n_ in range(nout):
                last_ = t.matmul(dst_ps[:, n_ * 512:(n_ + 1) * 512], ones_row[0:1, :],
                         brow[0:1, n_ * 512:(n_ + 1) * 512], start=False, stop=True)
            return last_
        return te(mms, [t_xT, t_w, t_brow, t_ones, free[psname]])

    def moe_chain(j):
        s = j
        st = stt[s]
        t_xm = t_moe_xm[j]
        t_a1, t_mu1 = bn_ln_stats(s, xb[s], D, t_xm)
        t_xn1 = ve(lambda v: v.tensor_scalar(
            xn1b[s][:, :], xb[s][:, :], st[:, MU:MU + 1], st[:, A:A + 1],
            AL.subtract, AL.mult), [t_a1, t_mu1, t_xm])
        yield
        t_xT1, t_tr1 = transpose_chunks(xn1b[s], xT1b[s], 8, t_xn1)
        t_hcmm = proj(hcps, "hc", xT1b[s], 0, WcT, 1024, 2, 8, bcrow, t_bc, t_xT1, t_wc)
        free["hc"] = t_hcmm
        t_hc = sc(lambda s_: s_.activation(
            hcb[s][:, :], hcps[:, :], ACT.Identity, accum_out=st[:, S1:S1 + 1]),
            [t_hcmm])
        free["hc"] = t_hc
        yield
        t_s2 = ve(lambda v: v.scalar_tensor_tensor(
            sqb[s][:, :], hcb[s][:, :], 1.0, hcb[s][:, :], AL.mult, AL.mult,
            accum_out=st[:, S2:S2 + 1]), [t_hc])
        t_a2, t_mu2 = ln_stats_math(s, t_hc, t_s2, DINV)
        t_t2 = ve(lambda v: v.tensor_scalar(
            xn1b[s][:, :], hcb[s][:, :], st[:, MU:MU + 1], st[:, A:A + 1],
            AL.subtract, AL.mult), [t_a2, t_mu2, t_tr1])
        yield
        t_x2r = ve(lambda v: v.scalar_tensor_tensor(
            x2rb[s][:, :], xn1b[s][:, :], 1.0, xb[s][:, :], AL.mult, AL.add,
            accum_out=st[:, S1:S1 + 1]), [t_t2, t_xm])
        t_s2b = ve(lambda v: v.scalar_tensor_tensor(
            sqb[s][:, :], x2rb[s][:, :], 1.0, x2rb[s][:, :], AL.mult, AL.mult,
            accum_out=st[:, S2:S2 + 1]), [t_x2r])
        t_a3, t_mu3 = ln_stats_math(s, t_x2r, t_s2b, DINV)
        yield
        t_xn2 = ve(lambda v: v.tensor_scalar(
            hcb[s][:, :], x2rb[s][:, :], st[:, MU:MU + 1], st[:, A:A + 1],
            AL.subtract, AL.mult), [t_a3, t_mu3])

        def mms(t):
            for k in range(8):
                last_ = t.matmul(trps[:, k * 128:(k + 1) * 128],
                         hcb[s][:, k * 128:(k + 1) * 128], idnb[:],
                         start=True, stop=True)
            return last_
        t_tr2 = te(mms, [t_xn2, free["tr"], t_idn])
        t_xT2 = sc(lambda s_: s_.copy(xmT[:, j * 1024:j * 1024 + 1024],
                                      trps[:, 0:1024]), [t_tr2])
        free["tr"] = t_xT2
        t_moe_xT[j] = t_xT2
        yield
        t_x1 = proj(x12ps, "x12", xmT, j * 1024, sw1T, 512, 1, 8, sb1row, t_sb1,
                    t_xT2, t_sw12)
        free["x12"] = t_x1

        def mms2(t):
            for k_ in range(8):
                last_ = t.matmul(x12ps[:, 512:1024],
                         xmT[:, j * 1024 + k_ * 128: j * 1024 + (k_ + 1) * 128],
                         sw2T[:, k_ * 512:(k_ + 1) * 512],
                         start=(k_ == 0), stop=False)
            last_ = t.matmul(x12ps[:, 512:1024], ones_row[0:1, :], sb2row[0:1, :],
                     start=False, stop=True)
            return last_
        t_x2 = te(mms2, [t_xT2, t_sb2, t_x1])
        t_silu = sc(lambda s_: s_.activation(silub[s][:, :], x12ps[:, 0:512], ACT.Silu),
                    [t_x1])
        t_g = ve(lambda v: v.scalar_tensor_tensor(
            gb[s][:, :], x12ps[:, 512:1024], 1.0, silub[s][:, :], AL.mult, AL.mult,
            accum_out=st[:, S1:S1 + 1]), [t_x2, t_silu])
        free["x12"] = t_g
        yield
        t_gs2 = ve(lambda v: v.scalar_tensor_tensor(
            sqb[s][:, 0:512], gb[s][:, :], 1.0, gb[s][:, :], AL.mult, AL.mult,
            accum_out=st[:, S2:S2 + 1]), [t_g])
        t_agk, t_mug = ln_stats_math(s, t_g, t_gs2, HINV)
        t_z = ve(lambda v: v.tensor_scalar(
            gb[s][:, :], gb[s][:, :], st[:, MU:MU + 1], st[:, A:A + 1],
            AL.subtract, AL.mult), [t_agk, t_mug])
        yield

        def gat_mms(t):
            for k_ in range(8):
                last_ = t.matmul(x12ps[:, 0:9],
                         xmT[:, j * 1024 + k_ * 128: j * 1024 + (k_ + 1) * 128],
                         geT[:, k_ * 16:k_ * 16 + 9],
                         start=(k_ == 0), stop=False)
            last_ = t.matmul(x12ps[:, 0:9], ones_row[0:1, :], gebrow[0:1, 0:9],
                     start=False, stop=True)
            return last_
        t_gmm = te(gat_mms, [t_moe_xT[j], t_gel, t_geb, free["x12"], t_ones])
        t_scores = sc(lambda s_: s_.activation(
            scoresb[j][:, 0:9], x12ps[:, 0:9], ACT.Sigmoid), [t_gmm])
        free["x12"] = t_scores
        t_eb = ve(lambda v: v.tensor_add(m8[:, :], scoresb[j][:, 1:9], expb[:, :]),
                  [t_scores, t_expb])
        t_mx = ve(lambda v: v.reduce_max(st[:, MX:MX + 1], m8[:, :], axis=AX), [t_eb])
        t_msk = ve(lambda v: v.tensor_single_scalar(
            maskb[j][:, :], m8[:, :], st[:, MX:MX + 1], AL.is_ge), [t_mx, t_eb])
        t_top = ve(lambda v: v.scalar_tensor_tensor(
            m8[:, :], maskb[j][:, :], 1.0, scoresb[j][:, 1:9], AL.mult, AL.mult,
            accum_out=st[:, TOP1:TOP1 + 1]), [t_msk])
        t_df = ve(lambda v: v.tensor_scalar(
            st[:, TMP:TMP + 1], scoresb[j][:, 0:1], st[:, TOP1:TOP1 + 1], None,
            AL.subtract), [t_top])
        t_shw = sc(lambda s_: s_.activation(
            st[:, SHW:SHW + 1], st[:, TMP:TMP + 1], ACT.Sigmoid), [t_df])
        t_exw = ve(lambda v: v.tensor_scalar(
            st[:, EXW:EXW + 1], st[:, SHW:SHW + 1], -1.0, 1.0, AL.mult, AL.add),
            [t_shw])
        t_mskp = ve(lambda v: v.tensor_single_scalar(
            maskpb[j][:, :], maskb[j][:, :], st[:, EXW:EXW + 1], AL.mult),
            [t_msk, t_exw])
        yield
        t_zT, _ = transpose_chunks(gb[s], zTb[s], 4, t_z)
        t_fmm = proj(fps, "f", zTb[s], 0, sw3Tg, 1024, 2, 4, vbrow_bf, t_vb, t_zT, t_g3)
        moe_fmm[j] = t_fmm
        t_acc0 = ve(lambda v: v.tensor_single_scalar(
            accf[j][:, :], fps[:, :], st[:, SHW:SHW + 1], AL.mult),
            [t_fmm, t_shw])
        free["f"] = t_acc0
        t_moe_prep[j] = (t_acc0, t_mskp)

    t_i1 = ve(lambda v: v.tensor_copy(idxf[:, 0:2], ibm_sb[:, :]), [LSC_ALL])
    t_i2 = ve(lambda v: v.tensor_copy(idxf[:, 2:4], itm_sb[:, :]), [LSC_ALL])
    t_i3 = ve(lambda v: v.tensor_scalar(idxf[:, 0:2], idxf[:, 0:2], 0.0, float(B - 1),
                                        AL.max, AL.min), [t_i1])
    t_i4 = ve(lambda v: v.tensor_scalar(idxf[:, 2:4], idxf[:, 2:4], 0.0, float(S - 1),
                                        AL.max, AL.min), [t_i2])
    t_i5 = ve(lambda v: v.scalar_tensor_tensor(idxf[:, 4:6], idxf[:, 0:2], float(S),
                                               idxf[:, 2:4], AL.mult, AL.add),
              [t_i3, t_i4])
    t_gmi = ve(lambda v: v.tensor_copy(gmi[:, :], idxf[:, 4:6]), [t_i5])

    t_moe_xT = [None, None]
    t_moe_prep = [None, None]
    t_moe_xm = [None, None]
    moe_fmm = [None, None]
    for j in range(2):
        emit("mgx", lambda g, j=j: g.indirect_dma_start(
            out=outb[j][:, :], out_offset=None,
            in_=x_full, in_offset=IndirectOffsetOnAxis(gmi[:, j:j + 1], 0)),
            (t_gmi,), 16)
    MGX_ALL = ("mgx", planned["mgx"])
    for j in range(2):
        t_moe_xm[j] = ve(lambda v, j=j: v.tensor_copy(xb[j][:, :], outb[j][:, :]),
                         [MGX_ALL])
    moe_gens = [moe_chain(0), moe_chain(1)]
    moe_alive = [True, True]
    while any(moe_alive):
        for j in range(2):
            if moe_alive[j]:
                try:
                    next(moe_gens[j])
                except StopIteration:
                    moe_alive[j] = False

    t_last_acc = None
    prev_vb_use = [None]
    e_silu_free = [None, None]
    e_g_free = [None, None]
    e_zT_free = [None, None]
    t_ew12_free = [None, None]
    t_ew12_ld = {}
    t_ew3_ld = {}
    t_eb3_ld = {}
    ew3_state = {"omm": None}
    prep = {}
    exp_tok = {}

    def emit_ew12_loads(e):
        sl = e % 2
        wf = (t_ew12_free[sl],) if t_ew12_free[sl] else (t_bcmm,)
        ewsem = f"ew{sl}"
        for wsb, win in ((ew1s[sl], ew1t_in), (ew2s[sl], ew2t_in)):
            gdma(bass.AP(wsb, 0, [[4096, 128], [512, 8], [1, 512]]),
                 bass.AP(win.tensor, e * 1024 * 512,
                         [[512, 128], [512 * 128, 8], [1, 512]]),
                 waits=wf, sem=ewsem)
        gdma(eb1r[sl][0:1, :], eb1_in[e, None, :], waits=wf, sem=ewsem)
        t_ew12_ld[e] = gdma(eb2r[sl][0:1, :], eb2_in[e, None, :], waits=wf,
                            sem=ewsem)

    def emit_ew3_load(e):
        wf = []
        if ew3_state["omm"] is not None:
            wf.append(ew3_state["omm"])
        if e - 1 in prep:
            wf.append(prep[e - 1]["t_vbe2"])
        gdma(bass.AP(ew3s[0], 0, [[4096, 128], [1024, 4], [1, 1024]]),
             bass.AP(ew3t_in.tensor, e * 512 * 1024,
                     [[1024, 128], [1024 * 128, 4], [1, 1024]]),
             waits=tuple(wf), sem="ew3q")
        t_ew3_ld[e] = ("ew3q", planned["ew3q"])
        t_eb3_ld[e] = sdma(eb3r[0][0:1, :], eb3_in[e, None, :], waits=tuple(wf),
                           sem="ebq")

    def emit_prep_vbe(e):
        t_ld = t_ew3_ld[e]
        t_ldb = t_eb3_ld[e]

        def vbe_mms(t):
            for n_ in range(2):
                for k_ in range(4):
                    last_ = t.matmul(fps[0:1, n_ * 512:(n_ + 1) * 512],
                                     selb_c[:, k_:k_ + 1],
                                     ew3s[0][:, k_ * 1024 + n_ * 512:
                                             k_ * 1024 + (n_ + 1) * 512],
                                     start=(k_ == 0), stop=(k_ == 3))
            return last_
        t_vbemm = te(vbe_mms, [t_ld, t_selb, free["f"]])
        t_vbe1 = ve(lambda v: v.tensor_copy(vbrow2[0][0:1, :], fps[0:1, 0:1024]),
                    [t_vbemm] + ([prev_vb_use[0]] if prev_vb_use[0] else []))
        free["f"] = t_vbe1
        t_vbe2 = ve(lambda v: v.tensor_add(vbrow2[0][0:1, :], vbrow2[0][0:1, :],
                                           eb3r[0][0:1, :]), [t_vbe1, t_ldb])
        t_vbe = ve(lambda v: v.tensor_copy(vber[0][0:1, :], vbrow2[0][0:1, :]),
                   [t_vbe2])
        prev_vb_use[0] = t_vbe
        t_gsl = None
        for k in range(4):
            t_gsl = ve(lambda v, k=k: v.tensor_single_scalar(
                ew3s[0][:, k * 1024:(k + 1) * 1024],
                ew3s[0][:, k * 1024:(k + 1) * 1024],
                selg_c[:, k:k + 1], AL.mult), [t_vbemm, t_selg, t_ld])
        prep[e] = {"t_vbe": t_vbe, "t_gsl": t_gsl, "t_vbe2": t_vbe2}

    def emit_expert_A1(e, j):
        sl = e % 2
        st = estt[j]
        t_ld = t_ew12_ld[e]

        def h_mms(t, sl=sl, j=j):
            for k_ in range(8):
                last_ = t.matmul(x12ps[:, 0:512],
                                 xmT[:, j * 1024 + k_ * 128: j * 1024 + (k_ + 1) * 128],
                                 ew1s[sl][:, k_ * 512:(k_ + 1) * 512],
                                 start=(k_ == 0), stop=False)
                last_ = t.matmul(x12ps[:, 512:1024],
                                 xmT[:, j * 1024 + k_ * 128: j * 1024 + (k_ + 1) * 128],
                                 ew2s[sl][:, k_ * 512:(k_ + 1) * 512],
                                 start=(k_ == 0), stop=False)
            last_ = t.matmul(x12ps[:, 0:512], ones_row[0:1, :], eb1r[sl][0:1, :],
                             start=False, stop=True)
            last_ = t.matmul(x12ps[:, 512:1024], ones_row[0:1, :], eb2r[sl][0:1, :],
                             start=False, stop=True)
            return last_
        t_h = te(h_mms, [t_ld, t_moe_xT[j], free["x12"], t_ones])
        if j == 1:
            t_ew12_free[sl] = t_h
        t_silu = sc(lambda s_, j=j: s_.activation(
            esilub[j][:, :], x12ps[:, 0:512], ACT.Silu),
            [t_h] + ([e_silu_free[j]] if e_silu_free[j] else []))
        t_gz = ve(lambda v, j=j, st=st: v.scalar_tensor_tensor(
            egb[j][:, :], x12ps[:, 512:1024], 1.0, esilub[j][:, :], AL.mult, AL.mult,
            accum_out=st[:, S1:S1 + 1]),
            [t_h, t_silu] + ([e_g_free[j]] if e_g_free[j] else []))
        free["x12"] = t_gz
        t_gz2 = ve(lambda v, j=j, st=st: v.scalar_tensor_tensor(
            esilub[j][:, :], egb[j][:, :], 1.0, egb[j][:, :], AL.mult, AL.mult,
            accum_out=st[:, S2:S2 + 1]), [t_gz])
        e_silu_free[j] = t_gz2
        exp_tok[(e, j)] = (t_gz, t_gz2)

    def emit_expert_A2(e, j):
        st = estt[j]
        t_gz, t_gz2 = exp_tok[(e, j)]
        t_az, t_muz = ln_stats_math_st(st, t_gz, t_gz2, HINV)
        t_zz = ve(lambda v, j=j, st=st: v.tensor_scalar(
            egb[j][:, :], egb[j][:, :], st[:, MU:MU + 1], st[:, A:A + 1],
            AL.subtract, AL.mult), [t_az, t_muz, t_gz2])
        return t_zz

    def emit_expert_B1(e, j, t_zz, off=512, defer=True):
        do_ev, t_ztr2 = transpose_chunks(egb[j], ezTb[j], 4, t_zz, e_zT_free[j],
                                         off=off, defer=True)
        e_g_free[j] = t_ztr2
        if not defer:
            zz_tok[(e, j)] = do_ev()
            return

        def fin(e=e, j=j, do_ev=do_ev):
            zz_tok[(e, j)] = do_ev()
        pend_evac.append(fin)

    def emit_expert_B2(e, j, t_zT):
        nonlocal t_last_acc
        pr = prep[e]

        def o_mms(t, j=j):
            for n_ in range(2):
                for k_ in range(4):
                    last_ = t.matmul(fps[:, n_ * 512:(n_ + 1) * 512],
                                     ezTb[j][:, k_ * 128:(k_ + 1) * 128],
                                     ew3s[0][:, k_ * 1024 + n_ * 512:
                                             k_ * 1024 + (n_ + 1) * 512],
                                     start=(k_ == 0), stop=False)
                last_ = t.matmul(fps[:, n_ * 512:(n_ + 1) * 512], ones_row[0:1, :],
                                 vber[0][0:1, n_ * 512:(n_ + 1) * 512],
                                 start=False, stop=True)
            return last_
        t_omm = te(o_mms, [t_zT, pr["t_vbe"], pr["t_gsl"], free["f"], t_ones])
        e_zT_free[j] = t_omm
        if j == 1:
            ew3_state["omm"] = t_omm
        extra = [t_moe_prep[j][1]]
        if e == 0:
            extra.append(t_moe_prep[j][0])
        t_acc = ve(lambda v, j=j, e=e: v.scalar_tensor_tensor(
            accf[j][:, :], fps[:, :], maskpb[j][:, e:e + 1], accf[j][:, :],
            AL.mult, AL.add), [t_omm] + extra + ([t_last_acc] if t_last_acc else []))
        free["f"] = t_acc
        t_last_acc = t_acc

    units = [(e, j) for e in range(E) for j in range(2)]
    emit_ew12_loads(0)
    emit_ew3_load(0)
    emit_ew12_loads(1)
    emit_prep_vbe(0)

    selh = {"ready": None, "t": None}

    def emit_sel_batch(b):
        if b == 0:
            selh["ready"] = emit_sel_scatters()
        n8 = (n_tiles + 3) // 4
        for i in range(b * n8, min((b + 1) * n8, n_tiles)):
            sdma(selall_i[:, i:i + 1],
                 bass.AP(sel_d, i * 128, [[1, 128], [1, 1]]),
                 waits=(selh["ready"],), sem="sel0")

    def emit_ivm():
        SEL_ALL = ("sel0", planned["sel0"])
        t_selfc = ve(lambda v: v.tensor_copy(self_all[:, :], selall_i[:, :]), [SEL_ALL])
        t_pm_all = ve(lambda v: v.tensor_single_scalar(
            ivm_all[:, :], self_all[:, :], float(M) - 0.5, AL.is_lt), [t_selfc])
        selh["t"] = ve(lambda v: v.tensor_scalar(
            ivm_all[:, :], ivm_all[:, :], -1.0, 1.0, AL.mult, AL.add), [t_pm_all])

    prev_x_free = [t_moe_xT[0], t_moe_xT[1], None, None, None, None]
    prev_bn_free = [None, None]
    prev_xn1_free = [t_moe_xT[0], t_moe_xT[1], None]
    prev_xT1_free = [t_moe_xT[0], t_moe_xT[1], None]
    prev_xT2_free = [None] * 3
    prev_hc_free = [t_moe_xT[0], t_moe_xT[1], None, None]
    prev_x2r_free = [t_moe_xT[0], t_moe_xT[1], None, None, None, None, None]
    prev_g_free = [moe_fmm[0], moe_fmm[1], None]
    prev_silu_free = [moe_fmm[0], moe_fmm[1]]
    prev_fb_free = [None] * 3
    prev_out_free = [t_moe_xm[0], t_moe_xm[1]]
    prev_sq_free = [moe_fmm[0], moe_fmm[1]]
    prev_zT_free = [moe_fmm[0], moe_fmm[1]]
    last_store = None
    TI = [dict() for _ in range(n_tiles)]

    def U0(i):
        s5 = (i + 2) % 6
        TI[i]["t_x"] = gdma(xb[s5][:, :], x_shard[i * 128:(i + 1) * 128, :],
                            waits=(prev_x_free[s5],) if prev_x_free[s5] else (),
                            sem=f"ldx{s5}")

    def U1(i):
        d = TI[i]
        s5 = (i + 2) % 6
        sx = (i + 2) % 3
        s2 = i % 2
        st = stt[i % 10]
        bs = bnst[s2]
        d["st"] = st
        t_x = d["t_x"]
        t_b1 = ve(lambda v, s5=s5, bs=bs: v.bn_stats(bs[:, 0:6], xb[s5][:, 0:512]),
                  [t_x] + ([prev_bn_free[s2]] if prev_bn_free[s2] else []))
        t_b2 = ve(lambda v, s5=s5, bs=bs: v.bn_stats(bs[:, 6:12], xb[s5][:, 512:1024]),
                  [t_b1])
        t_ag = ve(lambda v, bs=bs: v.bn_aggr(bs[:, 10:12], bs[:, 0:12]), [t_b2])
        t_sg = sc(lambda s_, bs=bs: s_.activation(
            bs[:, 12:13], bs[:, 11:12], ACT.Sqrt, bias=eps_t[:]), [t_ag, t_eps])
        t_ra = ve(lambda v, bs=bs: v.reciprocal(bs[:, 13:14], bs[:, 12:13]), [t_sg])
        t_xn1 = ve(lambda v, s5=s5, sx=sx, bs=bs: v.tensor_scalar(
            xn1b[sx][:, :], xb[s5][:, :], bs[:, 10:11], bs[:, 13:14],
            AL.subtract, AL.mult),
            [t_ra, t_ag] + ([prev_xn1_free[sx]] if prev_xn1_free[sx] else []))
        prev_bn_free[s2] = t_xn1
        d["t_xn1"] = t_xn1

    def U2(i):
        d = TI[i]
        sx = (i + 2) % 3
        t_xT1, t_tr1 = transpose_chunks(xn1b[sx], xT1b[sx], 8, d["t_xn1"],
                                        prev_xT1_free[sx])
        prev_xn1_free[sx] = t_tr1
        d["t_xT1"] = t_xT1

    def U3a(i):
        d = TI[i]
        sx = (i + 2) % 3
        t_hcmm = proj(hcps, "hc", xT1b[sx], 0, WcT, 1024, 2, 8, bcrow, t_bc,
                      d["t_xT1"], t_wc)
        free["hc"] = t_hcmm
        prev_xT1_free[sx] = t_hcmm
        d["t_hcmm"] = t_hcmm

    def U3b(i):
        d = TI[i]
        s4 = (i + 2) % 4
        st = d["st"]
        t_hc = sc(lambda s_, s4=s4, st=st: s_.activation(
            hcb[s4][:, :], hcps[:, :], ACT.Identity, accum_out=st[:, S1:S1 + 1]),
            [d["t_hcmm"]] + ([prev_hc_free[s4]] if prev_hc_free[s4] else []))
        free["hc"] = t_hc
        d["t_hc"] = t_hc

    def U4(i):
        d = TI[i]
        s4 = (i + 2) % 4
        sq = i % 2
        st = d["st"]
        t_s2 = ve(lambda v, s4=s4, st=st, sq=sq: v.scalar_tensor_tensor(
            sqb[sq][:, :], hcb[s4][:, :], 1.0, hcb[s4][:, :], AL.mult, AL.mult,
            accum_out=st[:, S2:S2 + 1]),
            [d["t_hc"]] + ([prev_sq_free[sq]] if prev_sq_free[sq] else []))
        t_a2, t_mu2 = ln_stats_math_st(st, d["t_hc"], t_s2, DINV)
        d["t_s2"] = t_s2
        d["t_a2"] = t_a2
        d["t_mu2"] = t_mu2

    def U5(i):
        d = TI[i]
        s4 = (i + 2) % 4
        s6 = (i + 2) % 7
        s5 = (i + 2) % 6
        sq = i % 2
        st = d["st"]
        t_t2 = ve(lambda v, s4=s4, st=st, sq=sq: v.tensor_scalar(
            sqb[sq][:, :], hcb[s4][:, :], st[:, MU:MU + 1], st[:, A:A + 1],
            AL.subtract, AL.mult),
            [d["t_a2"], d["t_mu2"], d["t_s2"]] +
            ([prev_sq_free[sq]] if prev_sq_free[sq] else []))
        t_x2r = ve(lambda v, s5=s5, s6=s6, st=st, sq=sq: v.scalar_tensor_tensor(
            x2rb[s6][:, :], sqb[sq][:, :], 1.0, xb[s5][:, :], AL.mult, AL.add,
            accum_out=st[:, S1:S1 + 1]),
            [t_t2, d["t_x"]] + ([prev_x2r_free[s6]] if prev_x2r_free[s6] else []))
        prev_x_free[s5] = t_x2r
        t_s2b = ve(lambda v, s6=s6, st=st, sq=sq: v.scalar_tensor_tensor(
            sqb[sq][:, :], x2rb[s6][:, :], 1.0, x2rb[s6][:, :], AL.mult, AL.mult,
            accum_out=st[:, S2:S2 + 1]), [t_x2r])
        prev_sq_free[sq] = t_s2b
        t_a3, t_mu3 = ln_stats_math_st(st, t_x2r, t_s2b, DINV)
        d["t_x2r"] = t_x2r
        d["t_a3"] = t_a3
        d["t_mu3"] = t_mu3

    def U6a(i):
        d = TI[i]
        s4 = (i + 2) % 4
        s6 = (i + 2) % 7
        st = d["st"]
        t_xn2 = ve(lambda v, s4=s4, s6=s6, st=st: v.tensor_scalar(
            hcb[s4][:, :], x2rb[s6][:, :], st[:, MU:MU + 1], st[:, A:A + 1],
            AL.subtract, AL.mult), [d["t_a3"], d["t_mu3"]])
        d["t_xn2"] = t_xn2

    def U6b(i):
        d = TI[i]
        s4 = (i + 2) % 4
        sx = i % 3

        def mms(t, s4=s4):
            for k in range(8):
                last_ = t.matmul(trps[:, k * 128:(k + 1) * 128],
                                 hcb[s4][:, k * 128:(k + 1) * 128], idnb[:],
                                 start=True, stop=True)
            return last_
        t_tr2 = te(mms, [d["t_xn2"], free["tr"], t_idn])
        prev_hc_free[s4] = t_tr2
        t_xT2 = sc(lambda s_, sx=sx: s_.copy(xT2b[sx][:, :], trps[:, 0:1024]),
                   [t_tr2] + ([prev_xT2_free[sx]] if prev_xT2_free[sx] else []))
        free["tr"] = t_xT2
        d["t_xT2"] = t_xT2

    def U7(i):
        d = TI[i]
        sx = i % 3

        def mms12(t, sx=sx):
            for k_ in range(8):
                last_ = t.matmul(x12ps[:, 0:512],
                                 xT2b[sx][:, k_ * 128:(k_ + 1) * 128],
                                 sw1T[:, k_ * 512:(k_ + 1) * 512],
                                 start=(k_ == 0), stop=False)
                last_ = t.matmul(x12ps[:, 512:1024],
                                 xT2b[sx][:, k_ * 128:(k_ + 1) * 128],
                                 sw2T[:, k_ * 512:(k_ + 1) * 512],
                                 start=(k_ == 0), stop=False)
            last_ = t.matmul(x12ps[:, 0:512], ones_row[0:1, :], sb1row[0:1, :],
                             start=False, stop=True)
            last_ = t.matmul(x12ps[:, 512:1024], ones_row[0:1, :], sb2row[0:1, :],
                             start=False, stop=True)
            return last_
        t_x2mm = te(mms12, [d["t_xT2"], t_sb1, t_sb2, t_sw12, t_ones, free["x12"]])
        free["x12"] = t_x2mm
        prev_xT2_free[sx] = t_x2mm
        d["t_x1"] = t_x2mm
        d["t_x2mm"] = t_x2mm

    def U8a(i):
        d = TI[i]
        s3 = i % 3
        ss = i % 2
        sq = i % 2
        st = d["st"]
        t_silu = sc(lambda s_, ss=ss: s_.activation(
            silub[ss][:, :], x12ps[:, 0:512], ACT.Silu),
            [d["t_x1"]] + ([prev_silu_free[ss]] if prev_silu_free[ss] else []))
        t_g = ve(lambda v, s3=s3, ss=ss, st=st: v.scalar_tensor_tensor(
            gb[s3][:, :], x12ps[:, 512:1024], 1.0, silub[ss][:, :], AL.mult, AL.mult,
            accum_out=st[:, S1:S1 + 1]),
            [d["t_x2mm"], t_silu] + ([prev_g_free[s3]] if prev_g_free[s3] else []))
        free["x12"] = t_g
        prev_silu_free[ss] = t_g
        t_gs2 = ve(lambda v, s3=s3, st=st, sq=sq: v.scalar_tensor_tensor(
            sqb[sq][:, 0:512], gb[s3][:, :], 1.0, gb[s3][:, :], AL.mult, AL.mult,
            accum_out=st[:, S2:S2 + 1]),
            [t_g] + ([prev_sq_free[sq]] if prev_sq_free[sq] else []))
        prev_sq_free[sq] = t_gs2
        d["t_g"] = t_g
        d["t_gs2"] = t_gs2

    def U8b(i):
        d = TI[i]
        s3 = i % 3
        st = d["st"]
        t_agk, t_mug = ln_stats_math_st(st, d["t_g"], d["t_gs2"], HINV)
        t_z = ve(lambda v, s3=s3, st=st: v.tensor_scalar(
            gb[s3][:, :], gb[s3][:, :], st[:, MU:MU + 1], st[:, A:A + 1],
            AL.subtract, AL.mult), [t_agk, t_mug, d["t_gs2"]])
        d["t_z"] = t_z

    def U9a(i):
        d = TI[i]
        s3 = i % 3
        sz = i % 2
        do_ev, t_ztr = transpose_chunks(gb[s3], zTb[sz], 4, d["t_z"],
                                        prev_zT_free[sz], off=0, defer=True)
        prev_g_free[s3] = t_ztr

        def fin(d=d, do_ev=do_ev):
            d["t_zT"] = do_ev()
        pend_evac.append(fin)

    def U9b(i):
        d = TI[i]
        sz = i % 2
        t_fmm = proj(fps, "f", zTb[sz], 0, sw3Tg, 1024, 2, 4, vbrow_bf, t_vb,
                     d["t_zT"], t_g3)
        free["f"] = t_fmm
        prev_zT_free[sz] = t_fmm
        d["t_fmm"] = t_fmm

    def U10(i):
        nonlocal last_store
        d = TI[i]
        sf = i % 3
        s6 = (i + 2) % 7
        so = i % 2
        sq = i % 2
        st = d["st"]
        t_f = sc(lambda s_, sf=sf, st=st: s_.activation(
            fb[sf][:, :], fps[:, :], ACT.Identity, accum_out=st[:, S1:S1 + 1]),
            [d["t_fmm"], prev_fb_free[sf]])
        free["f"] = t_f
        t_fs2 = ve(lambda v, sf=sf, st=st, sq=sq: v.scalar_tensor_tensor(
            sqb[sq][:, :], fb[sf][:, :], 1.0, fb[sf][:, :], AL.mult, AL.mult,
            accum_out=st[:, S2:S2 + 1]),
            [t_f] + ([prev_sq_free[sq]] if prev_sq_free[sq] else []))
        prev_sq_free[sq] = t_fs2
        t_a4, t_mu4 = ln_stats_math_st(st, t_f, t_fs2, DINV)
        t_a4m = ve(lambda v, st=st, i=i: v.tensor_mul(
            st[:, A4M:A4M + 1], st[:, A:A + 1], ivm_all[:, i:i + 1]),
            [t_a4, selh["t"]])
        t_t4 = ve(lambda v, sf=sf, st=st: v.tensor_scalar(
            fb[sf][:, :], fb[sf][:, :], st[:, MU:MU + 1], st[:, A4M:A4M + 1],
            AL.subtract, AL.mult), [t_a4m, t_mu4, t_fs2])
        w5 = [t_t4, d["t_x2r"]] + ([prev_out_free[so]] if prev_out_free[so] else [])
        t_t5a = gp(lambda g, sf=sf, s6=s6, so=so: g.tensor_tensor(
            outb[so][:, 0:512], fb[sf][:, 0:512], x2rb[s6][:, 0:512], AL.add), w5)
        t_t5b = ve(lambda v, sf=sf, s6=s6, so=so: v.tensor_tensor(
            outb[so][:, 512:1024], fb[sf][:, 512:1024], x2rb[s6][:, 512:1024],
            AL.add), w5)
        t_st = sdma(out_d[i * 128:(i + 1) * 128, :], outb[so][:, :],
                    waits=(t_t5a, t_t5b), sem=f"st{so}")
        prev_fb_free[sf] = t_st
        prev_x2r_free[s6] = t_st
        prev_out_free[so] = t_st
        last_store = t_st

    exp = {"ua1": 0, "ua2": 0, "ub1": 0, "ub": 0, "vbe_due": None, "t_cc": None}

    def emit_r_and_cc():
        for j in range(2):
            st = estt[j]
            bs = bnstr[j]
            t_b1 = ve(lambda v, j=j, bs=bs: v.bn_stats(bs[:, 0:6], accf[j][:, 0:512]),
                      [t_last_acc])
            t_b2 = ve(lambda v, j=j, bs=bs: v.bn_stats(bs[:, 6:12], accf[j][:, 512:1024]),
                      [t_b1])
            t_ag2 = ve(lambda v, bs=bs: v.bn_aggr(bs[:, 10:12], bs[:, 0:12]), [t_b2])
            t_sg = sc(lambda s_, st=st, bs=bs: s_.activation(
                st[:, SIG:SIG + 1], bs[:, 11:12], ACT.Sqrt, bias=eps_t[:]),
                [t_ag2, t_eps])
            t_a = ve(lambda v, st=st: v.reciprocal(st[:, A:A + 1], st[:, SIG:SIG + 1]),
                     [t_sg])
            t_r = ve(lambda v, j=j, st=st, bs=bs: v.tensor_scalar(
                fb[j][:, :], accf[j][:, :], bs[:, 10:11], st[:, A:A + 1],
                AL.subtract, AL.mult),
                [t_a, t_ag2, prev_fb_free[j]])
            gdma(ag_in_d.ap()[j * 128:(j + 1) * 128, :], fb[j][:, :],
                 waits=(t_r,), sem="rst")
        t_rall2 = ("rst", planned["rst"])
        prev_fb_free[0] = t_rall2
        prev_fb_free[1] = t_rall2
        exp["t_cc"] = emit("cc", lambda g: g.collective_compute(
            "AllGather", AL.bypass,
            replica_groups=[list(range(NC))],
            ins=[ag_in_d.ap().opt()],
            outs=[ag_out_d.ap()[0:M, :].opt()]),
            (t_rall2, t_agz), 1)

    zz_tok = {}
    pend_evac = []

    def flush_evacs():
        for f in pend_evac:
            f()
        pend_evac.clear()

    def pump_A2():
        if exp["ua2"] >= exp["ua1"] or exp["ua2"] >= len(units):
            return
        e, j = units[exp["ua2"]]
        zz_tok[(e, j)] = emit_expert_A2(e, j)
        exp["ua2"] += 1

    def pump_B1(off=512, defer=True):
        if (exp["ub1"] >= exp["ua2"] or exp["ub1"] >= len(units)
                or exp["ub1"] >= exp["ub"] + 2):
            return
        e, j = units[exp["ub1"]]
        emit_expert_B1(e, j, zz_tok[(e, j)], off, defer)
        exp["ub1"] += 1

    def pump_B2(lag=None):
        if lag is None:
            lag = 1 if exp["ub1"] < len(units) else 0
        if exp["ub"] >= exp["ub1"] - lag or exp["ub"] >= len(units):
            return
        e, j = units[exp["ub"]]
        emit_expert_B2(e, j, zz_tok[(e, j)])
        if j == 1:
            if e + 1 < E:
                emit_ew3_load(e + 1)
                exp["vbe_due"] = e + 1
            else:
                emit_r_and_cc()
        exp["ub"] += 1

    def pump_A1():
        if exp["ua1"] >= len(units):
            return
        e, j = units[exp["ua1"]]
        emit_expert_A1(e, j)
        if j == 0 and 2 <= e + 1 < E:
            emit_ew12_loads(e + 1)
        exp["ua1"] += 1

    def pump_vbe():
        if exp["vbe_due"] is not None:
            emit_prep_vbe(exp["vbe_due"])
            exp["vbe_due"] = None

    NS = 13
    FRONT = 10
    for i in range(min(3, n_tiles)):
        U0(i)
    for t_it in range(n_tiles + NS - 1):
        if t_it == 3 and n_tiles:
            emit_sel_batch(0)
        if t_it in (4, 5, 6) and n_tiles:
            emit_sel_batch(t_it - 3)
        if t_it == 7 and n_tiles:
            emit_ivm()
        if 0 <= t_it - 12 < n_tiles:
            U10(t_it - 12)
        flush_evacs()
        if 0 <= t_it - 4 < n_tiles:
            U3b(t_it - 4)
        if 0 <= t_it - 10 < n_tiles:
            U8b(t_it - 10)
        pump_A2()
        if t_it < FRONT:
            pump_A2()
        if 0 <= t_it < n_tiles:
            U1(t_it)
        if 0 <= t_it - 4 < n_tiles:
            U4(t_it - 4)
        if 0 <= t_it - 5 < n_tiles:
            U5(t_it - 5)
        if 0 <= t_it - 6 < n_tiles:
            U6a(t_it - 6)
        if 0 <= t_it - 9 < n_tiles:
            U8a(t_it - 9)
        pump_vbe()
        if t_it < FRONT:
            pump_B2(0)
            pump_vbe()
            pump_B2(0)
            pump_B1(512, defer=False)
            pump_B1(0, defer=False)
        else:
            pump_B2()
        if 0 <= t_it - 11 < n_tiles:
            U9b(t_it - 11)
        if 0 <= t_it - 7 < n_tiles:
            U6b(t_it - 7)
        if 0 <= t_it - 3 < n_tiles:
            U3a(t_it - 3)
        if 0 <= t_it - 1 < n_tiles:
            U2(t_it - 1)
        pump_A1()
        if t_it < FRONT:
            pump_A1()
        if 0 <= t_it - 8 < n_tiles:
            U7(t_it - 8)
        if 0 <= t_it - 10 < n_tiles:
            U9a(t_it - 10)
        pump_B1()
        if 3 <= t_it + 1 < n_tiles:
            U0(t_it + 1)
    while exp["ub"] < len(units):
        flush_evacs()
        pump_A2()
        pump_vbe()
        pump_B2()
        pump_B1()
        pump_A1()
    flush_evacs()

    ST0 = ("st0", planned["st0"])
    ST1 = ("st1", planned["st1"])
    PFB = [prev_fb_free[0], prev_fb_free[1]]
    t_ps = None
    t_rm = [None, None]
    t_scat2 = [None, None]
    for jj in range(16):
        s = jj % 2
        t_ld = gdma(fb[s][:, :], ag_out_d.ap()[jj * 128:(jj + 1) * 128, :],
                    waits=tuple(x for x in (exp["t_cc"], t_rm[s], PFB[s]) if x),
                    sem="pat0")
        t_m = ve(lambda v, s=s, jj=jj: v.tensor_single_scalar(
            outb[s][:, :], fb[s][:, :], vmask[:, jj:jj + 1], AL.mult),
            [t_ld, ST0, ST1, t_16] + [x for x in (t_scat2[s], prev_out_free[s]) if x])
        t_rm[s] = t_m
        t_ps = emit("pat1", lambda g, s=s, jj=jj: g.indirect_dma_start(
            out=out_d, out_offset=IndirectOffsetOnAxis(selt2_i[:, jj:jj + 1], 0),
            in_=outb[s][:, :], in_offset=None, compute_op=AL.add),
            tuple(x for x in (t_m, t_ps) if x), 16)
        t_scat2[s] = t_ps
    if n_tiles:
        last_store = t_ps

    if last_store is None:
        last_store = t_cc
    queues["gpsimd"].append(lambda g: _wait(g, "gpsimd", [last_store]))
    queues["sync"].append(lambda sy: _wait(sy, "sync", [last_store]))
    queues["vector"].append(lambda v: _wait(v, "vector", [last_store]))
    queues["scalar"].append(lambda s_: _wait(s_, "scalar", [last_store]))
    queues["tensor"].append(lambda t: _wait(t, "tensor", [last_store]))

    @blk.gpsimd
    def _(g):
        for f in queues["gpsimd"]:
            f(g)

    @blk.sync
    def _(sy):
        for f in queues["sync"]:
            f(sy)

    @blk.tensor
    def _(t):
        for f in queues["tensor"]:
            f(t)

    @blk.vector
    def _(v):
        for f in queues["vector"]:
            f(v)

    @blk.scalar
    def _(s_):
        for f in queues["scalar"]:
            f(s_)

    es.close()
    return nc


def make_in_maps(inputs):
    i = inputs
    x = np.ascontiguousarray(np.asarray(i["L0_attn_x"], np.float32).reshape(TOK, D))
    common = {
        "x_full": x,
        "W_in": np.ascontiguousarray(np.asarray(i["L0_attn_W"], np.float32)),
        "U_in": np.triu(np.ones((D, D), np.float32)),
        "bias_in": np.ascontiguousarray(np.asarray(i["L0_attn_bias"], np.float32)),
        "sw1T_in": np.ascontiguousarray(np.asarray(i["L0_ffn_sw1"], np.float32).T),
        "sw2T_in": np.ascontiguousarray(np.asarray(i["L0_ffn_sw2"], np.float32).T),
        "sw3T_in": np.ascontiguousarray(np.asarray(i["L0_ffn_sw3"], np.float32).T),
        "sb1_in": np.ascontiguousarray(np.asarray(i["L0_ffn_sb1"], np.float32)),
        "sb2_in": np.ascontiguousarray(np.asarray(i["L0_ffn_sb2"], np.float32)),
        "sb3_in": np.ascontiguousarray(np.asarray(i["L0_ffn_sb3"], np.float32)),
        "slng_in": np.ascontiguousarray(np.asarray(i["L0_ffn_s_ln_g"], np.float32)),
        "slnb_in": np.ascontiguousarray(np.asarray(i["L0_ffn_s_ln_b"], np.float32)),
        "geT_in": np.ascontiguousarray(
            np.concatenate([np.asarray(i["L0_ffn_sg_w"], np.float32),
                            np.asarray(i["L0_ffn_eg_w"], np.float32)], axis=0).T),
        "geb_in": np.ascontiguousarray(
            np.concatenate([np.asarray(i["L0_ffn_sg_b"], np.float32),
                            np.asarray(i["L0_ffn_eg_b"], np.float32)], axis=0)),
        "expb_in": np.ascontiguousarray(
            np.broadcast_to(np.asarray(i["L0_ffn_exp_bias"], np.float32)[None, :],
                            (128, E))),
        "ew1t_in": np.ascontiguousarray(
            np.asarray(i["L0_ffn_ew1"], np.float32).transpose(0, 2, 1)),
        "ew2t_in": np.ascontiguousarray(
            np.asarray(i["L0_ffn_ew2"], np.float32).transpose(0, 2, 1)),
        "ew3t_in": np.ascontiguousarray(
            np.asarray(i["L0_ffn_ew3"], np.float32).transpose(0, 2, 1)),
        "eb1_in": np.ascontiguousarray(np.asarray(i["L0_ffn_eb1"], np.float32)),
        "eb2_in": np.ascontiguousarray(np.asarray(i["L0_ffn_eb2"], np.float32)),
        "eb3_in": np.ascontiguousarray(np.asarray(i["L0_ffn_eb3"], np.float32)),
        "selg_in": np.ascontiguousarray(
            np.asarray(i["L0_ffn_sel_ln_g"], np.float32).reshape(4, 128).T),
        "selb_in": np.ascontiguousarray(
            np.asarray(i["L0_ffn_sel_ln_b"], np.float32).reshape(4, 128).T),
        "slngc_in": np.ascontiguousarray(
            np.asarray(i["L0_ffn_s_ln_g"], np.float32).reshape(4, 128).T),
        "slnbc_in": np.ascontiguousarray(
            np.asarray(i["L0_ffn_s_ln_b"], np.float32).reshape(4, 128).T),
        "biasc_in": np.ascontiguousarray(
            np.asarray(i["L0_attn_bias"], np.float32).reshape(8, 128).T),
        "ibf_in": np.ascontiguousarray(
            np.asarray(i["L0_ffn_index_b"], np.int32).reshape(16, 128).T),
        "itf_in": np.ascontiguousarray(
            np.asarray(i["L0_ffn_index_t"], np.int32).reshape(16, 128).T),
        "idn_in": np.eye(128, dtype=np.float32),
    }
    ib = np.ascontiguousarray(np.asarray(i["L0_ffn_index_b"], np.int32))
    it = np.ascontiguousarray(np.asarray(i["L0_ffn_index_t"], np.int32))
    maps = []
    for cid in range(NC):
        mm = dict(common)
        mm["x_shard"] = np.ascontiguousarray(x[cid * TPC:(cid + 1) * TPC])
        mm["ibm_in"] = np.ascontiguousarray(ib[cid * MPC:(cid + 1) * MPC])
        mm["itm_in"] = np.ascontiguousarray(it[cid * MPC:(cid + 1) * MPC])
        mm["coreoff_in"] = np.full((128, 1), float(cid * TPC), np.float32)
        maps.append(mm)
    return maps


def kernel(**inputs):
    nc = build_program(NT)
    in_maps = make_in_maps(inputs)
    res = run_bass_kernel_spmd(nc, in_maps, core_ids=list(range(NC)))
    out = np.concatenate([np.asarray(r["out"], np.float32) for r in res.results], axis=0)
    return out.reshape(B, S, D)


# version 22
# speedup vs baseline: 1.6908x; 1.0580x over previous
import numpy as np
from contextlib import ExitStack

import concourse.bass as bass
import concourse.mybir as mybir
from concourse.bass_core_rust import IndirectOffsetOnAxis
from concourse.bass_utils import run_bass_kernel_spmd

F32 = mybir.dt.float32
BF16 = mybir.dt.bfloat16
I32 = mybir.dt.int32
AL = mybir.AluOpType
ACT = mybir.ActivationFunctionType
AX = mybir.AxisListType.X

NC = 8
B, S, D = 2, 16384, 1024
TOK = B * S
TPC = TOK // NC
NT = TPC // 128
M, E, H = 2048, 8, 512
MPC = M // NC
EPS = 1e-5
DINV = 1.0 / D
HINV = 1.0 / H

ENG2Q = {"te": "tensor", "ve": "vector", "sc": "scalar", "gpc": "gpsimd",
         "cc": "gpsimd", "ldw": "gpsimd", "lsw": "gpsimd", "lgc": "gpsimd",
         "lsc": "sync",
         "ldx0": "gpsimd", "ldx1": "gpsimd", "ldx2": "gpsimd", "ldx3": "gpsimd",
         "ldx4": "gpsimd", "ldx5": "gpsimd", "mgx": "gpsimd", "sel0": "gpsimd",
         "scat": "gpsimd", "rst": "gpsimd", "ew0": "gpsimd",
         "ew1": "gpsimd", "ew3q": "gpsimd", "ebq": "sync",
         "pat0": "gpsimd", "pat1": "gpsimd",
         "st0": "sync", "st1": "sync"}


def build_program(n_tiles=NT):
    nc = bass.Bass("TRN2", target_bir_lowering=False)

    def din(name, shape, dt=F32):
        return nc.dram_tensor(name, shape, dt, kind="ExternalInput").ap()

    x_shard = din("x_shard", [TPC, D])
    x_full = din("x_full", [TOK, D])
    W_in = din("W_in", [D, D])
    U_in = din("U_in", [D, D])
    bias_in = din("bias_in", [D])
    sw1T_in = din("sw1T_in", [D, H])
    sw2T_in = din("sw2T_in", [D, H])
    sw3T_in = din("sw3T_in", [H, D])
    sb1_in = din("sb1_in", [H])
    sb2_in = din("sb2_in", [H])
    sb3_in = din("sb3_in", [D])
    slng_in = din("slng_in", [H])
    slnb_in = din("slnb_in", [H])
    geT_in = din("geT_in", [D, 9])
    geb_in = din("geb_in", [9])
    expb_in = din("expb_in", [128, E])
    ew1t_in = din("ew1t_in", [E, D, H])
    ew2t_in = din("ew2t_in", [E, D, H])
    ew3t_in = din("ew3t_in", [E, H, D])
    eb1_in = din("eb1_in", [E, H])
    eb2_in = din("eb2_in", [E, H])
    eb3_in = din("eb3_in", [E, D])
    selg_in = din("selg_in", [128, 4])
    selb_in = din("selb_in", [128, 4])
    slngc_in = din("slngc_in", [128, 4])
    slnbc_in = din("slnbc_in", [128, 4])
    biasc_in = din("biasc_in", [128, 8])
    ibm_in = din("ibm_in", [MPC], I32)
    itm_in = din("itm_in", [MPC], I32)
    ibf_in = din("ibf_in", [128, 16], I32)
    itf_in = din("itf_in", [128, 16], I32)
    coreoff_in = din("coreoff_in", [128, 1])
    idn_in = din("idn_in", [128, 128])
    out_d = nc.dram_tensor("out", [TPC, D], F32, kind="ExternalOutput").ap()

    ag_in_d = nc.dram_tensor("ag_in", [MPC, D], BF16, kind="Internal")
    ag_out_d = nc.dram_tensor("ag_out", [M + 1, D], BF16, kind="Internal")
    sel_d = nc.dram_tensor("sel_d", [TPC + 132], I32, kind="Internal")

    es = ExitStack()
    sb = lambda name, shape, dt=BF16: es.enter_context(nc.sbuf_tensor(name, shape, dt))
    psum = lambda name, shape: es.enter_context(nc.psum_tensor(name, shape, F32))

    sems = {k: es.enter_context(nc.semaphore(k)) for k in
            ("te", "ve", "sc", "gpc", "cc", "ldw", "lsw", "lgc", "lsc", "ldx0",
             "ldx1", "ldx2", "ldx3", "ldx4", "ldx5", "mgx", "sel0", "scat", "rst",
             "ew0", "ew1", "ew3q", "ebq", "pat0", "pat1", "st0", "st1")}
    planned = {k: 0 for k in sems}
    emitted = {k: 0 for k in sems}
    waited = {}
    queues = {"gpsimd": [], "sync": [], "tensor": [], "vector": [], "scalar": []}

    def _wait(eng_obj, qname, toks):
        for tok in toks:
            if tok is None:
                continue
            sname, val = tok
            key = (qname, sname)
            if waited.get(key, -1) >= val:
                continue
            eng_obj.wait_ge(sems[sname], val)
            waited[key] = val

    def emit(sem_name, fn, waits=(), inc=1):
        q = ENG2Q[sem_name]
        planned[sem_name] += inc
        val = planned[sem_name]
        ws = tuple(waits)

        def run(eng):
            _wait(eng, q, ws)
            instr = fn(eng)
            instr.then_inc(sems[sem_name], inc)
            emitted[sem_name] += inc
            assert emitted[sem_name] == val, (sem_name, emitted[sem_name], val)
        queues[q].append(run)
        return (sem_name, val)

    def emit_seq(sem_name, fn, waits=(), inc=1):
        return emit(sem_name, fn, waits, inc)

    def gdma(dst, src, waits=(), sem="ldw", **kw):
        return emit(sem, lambda g: g.dma_start(out=dst, in_=src, **kw), waits, 16)

    def sdma(dst, src, waits=(), sem="lsc", **kw):
        return emit(sem, lambda sy: sy.dma_start(out=dst, in_=src, **kw), waits, 16)

    def ve(fn, waits=()):
        return emit("ve", fn, waits)

    def sc(fn, waits=()):
        return emit("sc", fn, waits)

    def te(fn, waits=()):
        return emit("te", fn, waits)

    def gp(fn, waits=()):
        return emit("gpc", fn, waits)

    WcT = sb("WcT", [128, 8 * 1024])
    sw1T = sb("sw1T", [128, 8 * 512])
    sw2T = sb("sw2T", [128, 8 * 512])
    sw3Tg = sb("sw3Tg", [128, 4 * 1024])
    geT = sb("geT", [128, 8 * 16])
    idnb = sb("idnb", [128, 128])
    ones_row = sb("ones_row", [1, 128])
    bcrow = sb("bcrow", [1, D])
    vbrow = sb("vbrow", [1, D], F32)
    vbrow2 = [sb("vbrow2_0", [1, D], F32)]
    vbrow_bf = sb("vbrow_bf", [1, D])
    sb1row = sb("sb1row", [1, H])
    sb2row = sb("sb2row", [1, H])
    gebrow = sb("gebrow", [1, 16])
    sb3row = sb("sb3row", [1, D], F32)
    eps_t = sb("eps_t", [128, 1], F32)
    expb = sb("expb", [128, E], F32)
    coreoff = sb("coreoff", [128, 1], F32)
    zrow = sb("zrow", [1, D])
    selg_c = sb("selg_c", [128, 4], F32)
    selb_c = sb("selb_c", [128, 4])
    slnb_c = sb("slnb_c", [128, 4])
    slng_c = sb("slng_c", [128, 4], F32)
    biasc = sb("biasc", [128, 8])

    es_build = ExitStack()
    Wsb = es_build.enter_context(nc.sbuf_tensor("Wsb", [128, 8 * 1024], BF16))
    Usb = es_build.enter_context(nc.sbuf_tensor("Usb", [128, 8 * 1024], BF16))
    es_build.close()

    ew1s = [sb("ew1s0", [128, 8 * 512]), sb("ew1s1", [128, 8 * 512])]
    ew2s = [sb("ew2s0", [128, 8 * 512]), sb("ew2s1", [128, 8 * 512])]
    ew3s = [sb("ew3s0", [128, 4 * 1024])]
    xT2b = [sb(f"xT2b{s}", [128, D]) for s in range(3)]
    eb1r = [sb("eb1r0", [1, H]), sb("eb1r1", [1, H])]
    eb2r = [sb("eb2r0", [1, H]), sb("eb2r1", [1, H])]
    eb3r = [sb("eb3r0", [1, D], F32)]
    vber = [sb("vber0", [1, D])]

    xb = [sb(f"xb{s}", [128, D]) for s in range(6)]
    xn1b = [sb(f"xn1b{s}", [128, D]) for s in range(3)]
    xT1b = [sb(f"xT1b{s}", [128, D]) for s in range(3)]
    hcb = [sb(f"hcb{s}", [128, D]) for s in range(4)]
    x2rb = [sb(f"x2rb{s}", [128, D]) for s in range(7)]
    sqb_ = sb("sqb_", [128, D])
    sqb2 = sb("sqb2", [128, D])
    sqb = [sqb_, sqb2]
    silub = [sb(f"silub{s}", [128, H]) for s in range(2)]
    gb = [sb(f"gb{s}", [128, H]) for s in range(3)]
    zTb = [sb(f"zTb{s}", [128, H]) for s in range(2)]
    fb = [sb(f"fb{s}", [128, D]) for s in range(3)]
    outb = [sb(f"outb{s}", [128, D], F32) for s in range(2)]
    stt = [sb(f"stt{s}", [128, 16], F32) for s in range(10)]
    bnst = [sb(f"bnst{s}", [128, 16], F32) for s in range(2)]
    bnstr = [sb(f"bnstr{s}", [128, 12], F32) for s in range(2)]
    selall_i = sb("selall_i", [128, max(1, n_tiles)], I32)
    ivm_all = sb("ivm_all", [128, max(1, n_tiles)], F32)
    self_all = sb("self_all", [128, max(1, n_tiles)], F32)

    S1, S2, TMP, MU, SIG, A, PM, IVM, A4M, SELF, SHW, EXW, TOP1, MX = range(14)
    BETA = 14

    xmT = sb("xmT", [128, 2 * D])
    estt = [sb(f"estt{j}", [128, 16], F32) for j in range(2)]
    egb = [sb(f"egb{j}", [128, H]) for j in range(2)]
    esilub = [sb(f"esilub{j}", [128, H]) for j in range(2)]
    ezTb = [sb(f"ezTb{j}", [128, H]) for j in range(2)]
    accf = [sb(f"accf{j}", [128, D]) for j in range(2)]
    scoresb = [sb(f"scores{j}", [128, 16], F32) for j in range(2)]
    maskb = [sb(f"maskb{j}", [128, E], F32) for j in range(2)]
    maskpb = [sb(f"maskpb{j}", [128, E], F32) for j in range(2)]
    m8 = sb("m8", [128, E], F32)
    ibm_sb = sb("ibm_sb", [128, 2], I32)
    itm_sb = sb("itm_sb", [128, 2], I32)
    idxf = sb("idxf", [128, 8], F32)
    gmi = sb("gmi", [128, 2], I32)

    ibf_sb = sb("ibf_sb", [128, 16], I32)
    itf_sb = sb("itf_sb", [128, 16], I32)
    bfv = sb("bfv", [128, 16], F32)
    tfv = sb("tfv", [128, 16], F32)
    vmask = sb("vmask", [128, 16], F32)
    tmp16 = sb("tmp16", [128, 16], F32)
    gflat = sb("gflat", [128, 16], F32)
    selt_i = sb("selt_i", [128, 16], I32)
    selt2_i = sb("selt2_i", [128, 16], I32)
    miota = sb("miota", [128, 16], I32)
    selinit = sb("selinit", [128, 33], I32)
    dumrow = sb("dumrow", [128, 16], F32)
    dumi16 = sb("dumi16", [128, 16], I32)

    hcps = psum("hcps", [128, 1024])
    trps = psum("trps", [128, 1024])
    x12ps = psum("x12ps", [128, 1024])
    fps = psum("fps", [128, 1024])

    blk = es.enter_context(nc.Block())

    free = {"tr": None, "hc": None, "x12": None, "f": None}

    t_ones = gp(lambda g: g.memset(ones_row[:], 1.0))
    t_eps = ve(lambda v: v.memset(eps_t[:], EPS))
    t_zrow = ve(lambda v: v.memset(zrow[:], 0.0))
    t_selinit = ve(lambda v: v.memset(selinit[:], M))
    gdma(idnb[:], idn_in, sem="lgc")
    gdma(sb1row[0:1, :], sb1_in[None, :], sem="lgc")
    gdma(sb2row[0:1, :], sb2_in[None, :], sem="lgc")
    gdma(gebrow[0:1, 0:9], geb_in[None, :], sem="lgc")
    gdma(selb_c[:], selb_in, sem="lgc")
    gdma(slnb_c[:], slnbc_in, sem="lgc")
    gdma(biasc[:], biasc_in, sem="lgc")
    LGC_ALL = ("lgc", planned["lgc"])
    sdma(expb[:], expb_in)
    sdma(coreoff[:], coreoff_in)
    sdma(sb3row[0:1, :], sb3_in[None, :])
    sdma(selg_c[:], selg_in)
    sdma(slng_c[:], slngc_in)
    sdma(ibf_sb[:, :], ibf_in)
    sdma(itf_sb[:, :], itf_in)
    sdma(bass.AP(sel_d, 0, [[33, 128], [1, 33]]), selinit[:, :], waits=(t_selinit,))
    sdma(ibm_sb[:, 0:1], ibm_in[0:128][:, None])
    sdma(ibm_sb[:, 1:2], ibm_in[128:256][:, None])
    sdma(itm_sb[:, 0:1], itm_in[0:128][:, None])
    sdma(itm_sb[:, 1:2], itm_in[128:256][:, None])
    t_agz = sdma(ag_out_d.ap()[M:M + 1, :], zrow[0:1, :], waits=(t_zrow,))

    for k in range(8):
        gdma(Wsb[:, k * 1024:(k + 1) * 1024], W_in[k * 128:(k + 1) * 128, :])
    for k in range(8):
        gdma(Usb[:, k * 1024:(k + 1) * 1024], U_in[k * 128:(k + 1) * 128, :])
    LDW_ALL = ("ldw", planned["ldw"])
    for k in range(8):
        gdma(sw1T[:, k * 512:(k + 1) * 512], sw1T_in[k * 128:(k + 1) * 128, :],
             sem="lsw")
        gdma(sw2T[:, k * 512:(k + 1) * 512], sw2T_in[k * 128:(k + 1) * 128, :],
             sem="lsw")
    for k in range(4):
        gdma(sw3Tg[:, k * 1024:(k + 1) * 1024], sw3T_in[k * 128:(k + 1) * 128, :],
             sem="lsw")
    for k in range(8):
        gdma(geT[:, k * 16:k * 16 + 9], geT_in[k * 128:(k + 1) * 128, :], sem="lsw")
    LSW_ALL = ("lsw", planned["lsw"])
    LSC_ALL = ("lsc", planned["lsc"])

    t_idn = LGC_ALL
    t_sb1 = t_sb2 = t_geb = LGC_ALL
    t_selb = t_slnb = t_biasc = LGC_ALL
    t_selg = t_slng = LSC_ALL
    t_w8 = t_u8 = LDW_ALL
    t_sw12 = t_sw3l = t_gel = LSW_ALL
    t_agz = LSC_ALL
    t_expb = t_coff = t_sb3 = LSC_ALL
    t_ibf = t_itf = t_sel0 = LSC_ALL

    wc_evs = []
    for gi, (m_, n_) in enumerate((m_, n_) for m_ in range(8) for n_ in range(2)):
        off = 512 * (gi % 2)

        def mms(t, m_=m_, n_=n_, off=off):
            for k_ in range(8):
                last_ = t.matmul(hcps[:, off:off + 512],
                         Wsb[:, k_ * 1024 + m_ * 128: k_ * 1024 + (m_ + 1) * 128],
                         Usb[:, k_ * 1024 + n_ * 512: k_ * 1024 + (n_ + 1) * 512],
                         start=(k_ == 0), stop=(k_ == 7))
            return last_
        w = [t_w8, t_u8]
        if len(wc_evs) >= 2:
            w.append(wc_evs[-2])
        tmm = te(mms, w)
        wc_evs.append(sc(lambda s_, m_=m_, n_=n_, off=off: s_.copy(
            WcT[:, m_ * 1024 + n_ * 512: m_ * 1024 + (n_ + 1) * 512],
            hcps[:, off:off + 512]), [tmm]))
    t_wc = wc_evs[-1]

    def bc_mms(t):
        for n_ in range(2):
            for k_ in range(8):
                last_ = t.matmul(hcps[0:1, n_ * 512:(n_ + 1) * 512],
                         biasc[:, k_:k_ + 1],
                         Usb[:, k_ * 1024 + n_ * 512: k_ * 1024 + (n_ + 1) * 512],
                         start=(k_ == 0), stop=(k_ == 7))
        return last_
    t_bcmm = te(bc_mms, [t_biasc, t_u8, t_wc])
    t_bc = ve(lambda v: v.tensor_copy(bcrow[0:1, :], hcps[0:1, 0:1024]), [t_bcmm])
    free["hc"] = t_bc

    def vb_mms(t):
        for n_ in range(2):
            for k_ in range(4):
                last_ = t.matmul(fps[0:1, n_ * 512:(n_ + 1) * 512],
                         slnb_c[:, k_:k_ + 1],
                         sw3Tg[:, k_ * 1024 + n_ * 512: k_ * 1024 + (n_ + 1) * 512],
                         start=(k_ == 0), stop=(k_ == 3))
        return last_
    t_vbmm = te(vb_mms, [t_slnb, t_sw3l])
    t_vb1 = ve(lambda v: v.tensor_copy(vbrow[0:1, :], fps[0:1, 0:1024]), [t_vbmm])
    t_vb2 = ve(lambda v: v.tensor_add(vbrow[0:1, :], vbrow[0:1, :], sb3row[0:1, :]),
               [t_vb1, t_sb3])
    t_vb = ve(lambda v: v.tensor_copy(vbrow_bf[0:1, :], vbrow[0:1, :]), [t_vb2])
    free["f"] = t_vb1

    t_g3 = None
    for k in range(4):
        t_g3 = ve(lambda v, k=k: v.tensor_single_scalar(
            sw3Tg[:, k * 1024:(k + 1) * 1024], sw3Tg[:, k * 1024:(k + 1) * 1024],
            slng_c[:, k:k + 1], AL.mult), [t_vbmm, t_slng, t_sw3l])

    t_miota = gp(lambda g: g.iota(miota[:, :], [[128, 16]], channel_multiplier=1))
    t_dumi = gp(lambda g: g.iota(dumi16[:, :], [[8, 16]], base=TPC, channel_multiplier=0,
                                 allow_small_or_imprecise_dtypes=True))
    t_dum = ve(lambda v: v.tensor_copy(dumrow[:, :], dumi16[:, :]), [t_dumi])

    t_b = ve(lambda v: v.tensor_copy(bfv[:, :], ibf_sb[:, :]), [t_ibf])
    t_t = ve(lambda v: v.tensor_copy(tfv[:, :], itf_sb[:, :]), [t_itf])
    t_1 = ve(lambda v: v.tensor_single_scalar(vmask[:, :], bfv[:, :], -0.5, AL.is_gt), [t_b])
    t_2 = ve(lambda v: v.tensor_single_scalar(tmp16[:, :], bfv[:, :], B - 0.5, AL.is_lt), [t_b, t_1])
    t_3 = ve(lambda v: v.tensor_mul(vmask[:, :], vmask[:, :], tmp16[:, :]), [t_1, t_2])
    t_4 = ve(lambda v: v.tensor_single_scalar(tmp16[:, :], tfv[:, :], -0.5, AL.is_gt), [t_t, t_3])
    t_5 = ve(lambda v: v.tensor_mul(vmask[:, :], vmask[:, :], tmp16[:, :]), [t_3, t_4])
    t_6 = ve(lambda v: v.tensor_single_scalar(tmp16[:, :], tfv[:, :], S - 0.5, AL.is_lt), [t_t, t_5])
    t_7 = ve(lambda v: v.tensor_mul(vmask[:, :], vmask[:, :], tmp16[:, :]), [t_5, t_6])
    t_8 = ve(lambda v: v.scalar_tensor_tensor(gflat[:, :], bfv[:, :], float(S), tfv[:, :],
                                              AL.mult, AL.add), [t_b, t_t, t_7])
    t_9 = ve(lambda v: v.tensor_single_scalar(gflat[:, :], gflat[:, :], coreoff[:, 0:1],
                                              AL.subtract), [t_8, t_coff])
    t_10 = ve(lambda v: v.tensor_single_scalar(tmp16[:, :], gflat[:, :], -0.5, AL.is_gt), [t_9, t_7])
    t_11 = ve(lambda v: v.tensor_mul(vmask[:, :], vmask[:, :], tmp16[:, :]), [t_7, t_10])
    t_12 = ve(lambda v: v.tensor_single_scalar(tmp16[:, :], gflat[:, :], TPC - 0.5, AL.is_lt), [t_9, t_11])
    t_13 = ve(lambda v: v.tensor_mul(vmask[:, :], vmask[:, :], tmp16[:, :]), [t_11, t_12])
    t_14 = ve(lambda v: v.tensor_mul(gflat[:, :], gflat[:, :], vmask[:, :]), [t_13, t_9])
    t_14b = ve(lambda v: v.tensor_copy(selt2_i[:, :], gflat[:, :]), [t_14])
    t_15 = ve(lambda v: v.tensor_single_scalar(tmp16[:, :], vmask[:, :], -1.0, AL.mult),
              [t_13, t_14])
    t_15b = ve(lambda v: v.tensor_single_scalar(tmp16[:, :], tmp16[:, :], 1.0, AL.add),
               [t_15])
    t_15c = ve(lambda v: v.tensor_mul(tmp16[:, :], tmp16[:, :], dumrow[:, :]),
               [t_15b, t_dum])
    t_16 = ve(lambda v: v.tensor_add(gflat[:, :], gflat[:, :], tmp16[:, :]),
              [t_14, t_14b, t_15c])
    t_17 = ve(lambda v: v.tensor_copy(selt_i[:, :], gflat[:, :]), [t_16])
    scat_w = (t_17, t_miota, t_sel0)

    def emit_sel_scatters():
        for jj in range(16):
            emit("scat", lambda g, jj=jj: g.indirect_dma_start(
                out=bass.AP(sel_d, 0, [[1, TPC + 132], [1, 1]]),
                out_offset=IndirectOffsetOnAxis(selt_i[:, jj:jj + 1], 0),
                in_=miota[:, jj:jj + 1], in_offset=None),
                scat_w, 16)
        return ("scat", planned["scat"])

    def ln_stats_math(s, t_s1, t_s2, dinv):
        return ln_stats_math_st(stt[s], t_s1, t_s2, dinv)

    def ln_stats_math_st(st, t_s1, t_s2, dinv):
        t_mu = ve(lambda v: v.tensor_single_scalar(
            st[:, MU:MU + 1], st[:, S1:S1 + 1], dinv, AL.mult), [t_s1])
        t_tt = ve(lambda v: v.scalar_tensor_tensor(
            st[:, TMP:TMP + 1], st[:, S1:S1 + 1], dinv * dinv, st[:, S1:S1 + 1],
            AL.mult, AL.mult), [t_s1])
        t_v = ve(lambda v: v.scalar_tensor_tensor(
            st[:, SIG:SIG + 1], st[:, S2:S2 + 1], dinv, st[:, TMP:TMP + 1],
            AL.mult, AL.subtract), [t_s2, t_tt])
        t_sg = sc(lambda s_: s_.activation(
            st[:, SIG:SIG + 1], st[:, SIG:SIG + 1], ACT.Sqrt, bias=eps_t[:]),
            [t_v, t_eps])
        t_a = ve(lambda v: v.reciprocal(st[:, A:A + 1], st[:, SIG:SIG + 1]), [t_sg])
        return t_a, t_mu

    def ln_stats_math_b(st, t_s1, t_s2, dinv):
        t_a, t_mu = ln_stats_math_st(st, t_s1, t_s2, dinv)
        t_b = ve(lambda v: v.scalar_tensor_tensor(
            st[:, BETA:BETA + 1], st[:, MU:MU + 1], -1.0, st[:, A:A + 1],
            AL.mult, AL.mult), [t_a, t_mu])
        return t_a, t_mu, t_b

    def bn_ln_stats(s, src, width, t_src):
        st = stt[s]
        bs = bnstr[s]
        nh = (width + 511) // 512
        toks = []
        for hh in range(nh):
            w0, w1 = hh * 512, min(width, (hh + 1) * 512)
            toks.append(ve(lambda v, hh=hh, w0=w0, w1=w1: v.bn_stats(
                bs[:, hh * 6:(hh + 1) * 6], src[:, w0:w1]), [t_src] + toks[-1:]))
        t_ag = ve(lambda v, nh=nh: v.bn_aggr(bs[:, 10:12], bs[:, 0:6 * nh]), toks)
        t_mu = ve(lambda v: v.tensor_copy(st[:, MU:MU + 1], bs[:, 10:11]), [t_ag])
        t_sg = sc(lambda s_: s_.activation(
            st[:, SIG:SIG + 1], bs[:, 11:12], ACT.Sqrt, bias=eps_t[:]), [t_ag, t_eps])
        t_a = ve(lambda v: v.reciprocal(st[:, A:A + 1], st[:, SIG:SIG + 1]), [t_sg])
        return t_a, t_mu

    def transpose_chunks(src, dst, nchunk, t_src, dst_free=None, off=0,
                         defer=False):
        def mms(t):
            for k in range(nchunk):
                last_ = t.matmul(trps[:, off + k * 128:off + (k + 1) * 128],
                         src[:, k * 128:(k + 1) * 128], idnb[:],
                         start=True, stop=True)
            return last_
        tmm = te(mms, [t_src, free["tr"], t_idn])

        def do_evac(extra_free=None):
            w = [tmm]
            if dst_free:
                w.append(dst_free)
            if extra_free:
                w.append(extra_free)
            tev = sc(lambda s_: s_.copy(dst[:, 0:nchunk * 128],
                                        trps[:, off:off + nchunk * 128]), w)
            free["tr"] = tev
            return tev
        if defer:
            return do_evac, tmm
        return do_evac(), tmm

    def proj(dst_ps, psname, xT_sb, xoff, w_sb, wtile, nout, kchunks, brow, t_brow,
             t_xT, t_w):
        def mms(t):
            for k_ in range(kchunks):
                for n_ in range(nout):
                    last_ = t.matmul(dst_ps[:, n_ * 512:(n_ + 1) * 512],
                             xT_sb[:, xoff + k_ * 128: xoff + (k_ + 1) * 128],
                             w_sb[:, k_ * wtile + n_ * 512: k_ * wtile + (n_ + 1) * 512],
                             start=(k_ == 0), stop=False)
            for n_ in range(nout):
                last_ = t.matmul(dst_ps[:, n_ * 512:(n_ + 1) * 512], ones_row[0:1, :],
                         brow[0:1, n_ * 512:(n_ + 1) * 512], start=False, stop=True)
            return last_
        return te(mms, [t_xT, t_w, t_brow, t_ones, free[psname]])

    def moe_chain(j):
        s = j
        st = stt[s]
        t_xm = t_moe_xm[j]
        t_a1, t_mu1 = bn_ln_stats(s, xb[s], D, t_xm)
        t_xn1 = ve(lambda v: v.tensor_scalar(
            xn1b[s][:, :], xb[s][:, :], st[:, MU:MU + 1], st[:, A:A + 1],
            AL.subtract, AL.mult), [t_a1, t_mu1, t_xm])
        yield
        t_xT1, t_tr1 = transpose_chunks(xn1b[s], xT1b[s], 8, t_xn1)
        t_hcmm = proj(hcps, "hc", xT1b[s], 0, WcT, 1024, 2, 8, bcrow, t_bc, t_xT1, t_wc)
        free["hc"] = t_hcmm
        t_hc = sc(lambda s_: s_.activation(
            hcb[s][:, :], hcps[:, :], ACT.Identity, accum_out=st[:, S1:S1 + 1]),
            [t_hcmm])
        free["hc"] = t_hc
        yield
        t_s2 = ve(lambda v: v.scalar_tensor_tensor(
            sqb[s][:, :], hcb[s][:, :], 1.0, hcb[s][:, :], AL.mult, AL.mult,
            accum_out=st[:, S2:S2 + 1]), [t_hc])
        t_a2, t_mu2 = ln_stats_math(s, t_hc, t_s2, DINV)
        t_t2 = ve(lambda v: v.tensor_scalar(
            xn1b[s][:, :], hcb[s][:, :], st[:, MU:MU + 1], st[:, A:A + 1],
            AL.subtract, AL.mult), [t_a2, t_mu2, t_tr1])
        yield
        t_x2r = ve(lambda v: v.scalar_tensor_tensor(
            x2rb[s][:, :], xn1b[s][:, :], 1.0, xb[s][:, :], AL.mult, AL.add,
            accum_out=st[:, S1:S1 + 1]), [t_t2, t_xm])
        t_s2b = ve(lambda v: v.scalar_tensor_tensor(
            sqb[s][:, :], x2rb[s][:, :], 1.0, x2rb[s][:, :], AL.mult, AL.mult,
            accum_out=st[:, S2:S2 + 1]), [t_x2r])
        t_a3, t_mu3 = ln_stats_math(s, t_x2r, t_s2b, DINV)
        yield
        t_xn2 = ve(lambda v: v.tensor_scalar(
            hcb[s][:, :], x2rb[s][:, :], st[:, MU:MU + 1], st[:, A:A + 1],
            AL.subtract, AL.mult), [t_a3, t_mu3])

        def mms(t):
            for k in range(8):
                last_ = t.matmul(trps[:, k * 128:(k + 1) * 128],
                         hcb[s][:, k * 128:(k + 1) * 128], idnb[:],
                         start=True, stop=True)
            return last_
        t_tr2 = te(mms, [t_xn2, free["tr"], t_idn])
        t_xT2 = sc(lambda s_: s_.copy(xmT[:, j * 1024:j * 1024 + 1024],
                                      trps[:, 0:1024]), [t_tr2])
        free["tr"] = t_xT2
        t_moe_xT[j] = t_xT2
        yield
        t_x1 = proj(x12ps, "x12", xmT, j * 1024, sw1T, 512, 1, 8, sb1row, t_sb1,
                    t_xT2, t_sw12)
        free["x12"] = t_x1

        def mms2(t):
            for k_ in range(8):
                last_ = t.matmul(x12ps[:, 512:1024],
                         xmT[:, j * 1024 + k_ * 128: j * 1024 + (k_ + 1) * 128],
                         sw2T[:, k_ * 512:(k_ + 1) * 512],
                         start=(k_ == 0), stop=False)
            last_ = t.matmul(x12ps[:, 512:1024], ones_row[0:1, :], sb2row[0:1, :],
                     start=False, stop=True)
            return last_
        t_x2 = te(mms2, [t_xT2, t_sb2, t_x1])
        t_silu = sc(lambda s_: s_.activation(silub[s][:, :], x12ps[:, 0:512], ACT.Silu),
                    [t_x1])
        t_g = ve(lambda v: v.scalar_tensor_tensor(
            gb[s][:, :], x12ps[:, 512:1024], 1.0, silub[s][:, :], AL.mult, AL.mult,
            accum_out=st[:, S1:S1 + 1]), [t_x2, t_silu])
        free["x12"] = t_g
        yield
        t_gs2 = ve(lambda v: v.scalar_tensor_tensor(
            sqb[s][:, 0:512], gb[s][:, :], 1.0, gb[s][:, :], AL.mult, AL.mult,
            accum_out=st[:, S2:S2 + 1]), [t_g])
        t_agk, t_mug = ln_stats_math(s, t_g, t_gs2, HINV)
        t_z = ve(lambda v: v.tensor_scalar(
            gb[s][:, :], gb[s][:, :], st[:, MU:MU + 1], st[:, A:A + 1],
            AL.subtract, AL.mult), [t_agk, t_mug])
        yield

        def gat_mms(t):
            for k_ in range(8):
                last_ = t.matmul(x12ps[:, 0:9],
                         xmT[:, j * 1024 + k_ * 128: j * 1024 + (k_ + 1) * 128],
                         geT[:, k_ * 16:k_ * 16 + 9],
                         start=(k_ == 0), stop=False)
            last_ = t.matmul(x12ps[:, 0:9], ones_row[0:1, :], gebrow[0:1, 0:9],
                     start=False, stop=True)
            return last_
        t_gmm = te(gat_mms, [t_moe_xT[j], t_gel, t_geb, free["x12"], t_ones])
        t_scores = sc(lambda s_: s_.activation(
            scoresb[j][:, 0:9], x12ps[:, 0:9], ACT.Sigmoid), [t_gmm])
        free["x12"] = t_scores
        t_eb = ve(lambda v: v.tensor_add(m8[:, :], scoresb[j][:, 1:9], expb[:, :]),
                  [t_scores, t_expb])
        t_mx = ve(lambda v: v.reduce_max(st[:, MX:MX + 1], m8[:, :], axis=AX), [t_eb])
        t_msk = ve(lambda v: v.tensor_single_scalar(
            maskb[j][:, :], m8[:, :], st[:, MX:MX + 1], AL.is_ge), [t_mx, t_eb])
        t_top = ve(lambda v: v.scalar_tensor_tensor(
            m8[:, :], maskb[j][:, :], 1.0, scoresb[j][:, 1:9], AL.mult, AL.mult,
            accum_out=st[:, TOP1:TOP1 + 1]), [t_msk])
        t_df = ve(lambda v: v.tensor_scalar(
            st[:, TMP:TMP + 1], scoresb[j][:, 0:1], st[:, TOP1:TOP1 + 1], None,
            AL.subtract), [t_top])
        t_shw = sc(lambda s_: s_.activation(
            st[:, SHW:SHW + 1], st[:, TMP:TMP + 1], ACT.Sigmoid), [t_df])
        t_exw = ve(lambda v: v.tensor_scalar(
            st[:, EXW:EXW + 1], st[:, SHW:SHW + 1], -1.0, 1.0, AL.mult, AL.add),
            [t_shw])
        t_mskp = ve(lambda v: v.tensor_single_scalar(
            maskpb[j][:, :], maskb[j][:, :], st[:, EXW:EXW + 1], AL.mult),
            [t_msk, t_exw])
        yield
        t_zT, _ = transpose_chunks(gb[s], zTb[s], 4, t_z)
        t_fmm = proj(fps, "f", zTb[s], 0, sw3Tg, 1024, 2, 4, vbrow_bf, t_vb, t_zT, t_g3)
        moe_fmm[j] = t_fmm
        t_acc0 = ve(lambda v: v.tensor_single_scalar(
            accf[j][:, :], fps[:, :], st[:, SHW:SHW + 1], AL.mult),
            [t_fmm, t_shw])
        free["f"] = t_acc0
        t_moe_prep[j] = (t_acc0, t_mskp)

    t_i1 = ve(lambda v: v.tensor_copy(idxf[:, 0:2], ibm_sb[:, :]), [LSC_ALL])
    t_i2 = ve(lambda v: v.tensor_copy(idxf[:, 2:4], itm_sb[:, :]), [LSC_ALL])
    t_i3 = ve(lambda v: v.tensor_scalar(idxf[:, 0:2], idxf[:, 0:2], 0.0, float(B - 1),
                                        AL.max, AL.min), [t_i1])
    t_i4 = ve(lambda v: v.tensor_scalar(idxf[:, 2:4], idxf[:, 2:4], 0.0, float(S - 1),
                                        AL.max, AL.min), [t_i2])
    t_i5 = ve(lambda v: v.scalar_tensor_tensor(idxf[:, 4:6], idxf[:, 0:2], float(S),
                                               idxf[:, 2:4], AL.mult, AL.add),
              [t_i3, t_i4])
    t_gmi = ve(lambda v: v.tensor_copy(gmi[:, :], idxf[:, 4:6]), [t_i5])

    t_moe_xT = [None, None]
    t_moe_prep = [None, None]
    t_moe_xm = [None, None]
    moe_fmm = [None, None]
    for j in range(2):
        emit("mgx", lambda g, j=j: g.indirect_dma_start(
            out=outb[j][:, :], out_offset=None,
            in_=x_full, in_offset=IndirectOffsetOnAxis(gmi[:, j:j + 1], 0)),
            (t_gmi,), 16)
    MGX_ALL = ("mgx", planned["mgx"])
    for j in range(2):
        t_moe_xm[j] = ve(lambda v, j=j: v.tensor_copy(xb[j][:, :], outb[j][:, :]),
                         [MGX_ALL])
    moe_gens = [moe_chain(0), moe_chain(1)]
    moe_alive = [True, True]
    while any(moe_alive):
        for j in range(2):
            if moe_alive[j]:
                try:
                    next(moe_gens[j])
                except StopIteration:
                    moe_alive[j] = False

    t_last_acc = None
    prev_vb_use = [None]
    e_silu_free = [None, None]
    e_g_free = [None, None]
    e_zT_free = [None, None]
    t_ew12_free = [None, None]
    t_ew12_ld = {}
    t_ew3_ld = {}
    t_eb3_ld = {}
    ew3_state = {"omm": None}
    prep = {}
    exp_tok = {}

    def emit_ew12_loads(e):
        sl = e % 2
        wf = (t_ew12_free[sl],) if t_ew12_free[sl] else (t_bcmm,)
        ewsem = f"ew{sl}"
        for wsb, win in ((ew1s[sl], ew1t_in), (ew2s[sl], ew2t_in)):
            gdma(bass.AP(wsb, 0, [[4096, 128], [512, 8], [1, 512]]),
                 bass.AP(win.tensor, e * 1024 * 512,
                         [[512, 128], [512 * 128, 8], [1, 512]]),
                 waits=wf, sem=ewsem)
        gdma(eb1r[sl][0:1, :], eb1_in[e, None, :], waits=wf, sem=ewsem)
        t_ew12_ld[e] = gdma(eb2r[sl][0:1, :], eb2_in[e, None, :], waits=wf,
                            sem=ewsem)

    def emit_ew3_load(e):
        wf = []
        if ew3_state["omm"] is not None:
            wf.append(ew3_state["omm"])
        if e - 1 in prep:
            wf.append(prep[e - 1]["t_vbe2"])
        gdma(bass.AP(ew3s[0], 0, [[4096, 128], [1024, 4], [1, 1024]]),
             bass.AP(ew3t_in.tensor, e * 512 * 1024,
                     [[1024, 128], [1024 * 128, 4], [1, 1024]]),
             waits=tuple(wf), sem="ew3q")
        t_ew3_ld[e] = ("ew3q", planned["ew3q"])
        t_eb3_ld[e] = sdma(eb3r[0][0:1, :], eb3_in[e, None, :], waits=tuple(wf),
                           sem="ebq")

    def emit_prep_vbe(e):
        t_ld = t_ew3_ld[e]
        t_ldb = t_eb3_ld[e]

        def vbe_mms(t):
            for n_ in range(2):
                for k_ in range(4):
                    last_ = t.matmul(fps[0:1, n_ * 512:(n_ + 1) * 512],
                                     selb_c[:, k_:k_ + 1],
                                     ew3s[0][:, k_ * 1024 + n_ * 512:
                                             k_ * 1024 + (n_ + 1) * 512],
                                     start=(k_ == 0), stop=(k_ == 3))
            return last_
        t_vbemm = te(vbe_mms, [t_ld, t_selb, free["f"]])
        t_vbe1 = ve(lambda v: v.tensor_copy(vbrow2[0][0:1, :], fps[0:1, 0:1024]),
                    [t_vbemm] + ([prev_vb_use[0]] if prev_vb_use[0] else []))
        free["f"] = t_vbe1
        t_vbe2 = ve(lambda v: v.tensor_add(vbrow2[0][0:1, :], vbrow2[0][0:1, :],
                                           eb3r[0][0:1, :]), [t_vbe1, t_ldb])
        t_vbe = ve(lambda v: v.tensor_copy(vber[0][0:1, :], vbrow2[0][0:1, :]),
                   [t_vbe2])
        prev_vb_use[0] = t_vbe
        t_gsl = None
        for k in range(4):
            t_gsl = ve(lambda v, k=k: v.tensor_single_scalar(
                ew3s[0][:, k * 1024:(k + 1) * 1024],
                ew3s[0][:, k * 1024:(k + 1) * 1024],
                selg_c[:, k:k + 1], AL.mult), [t_vbemm, t_selg, t_ld])
        prep[e] = {"t_vbe": t_vbe, "t_gsl": t_gsl, "t_vbe2": t_vbe2}

    def emit_expert_A1(e, j):
        sl = e % 2
        st = estt[j]
        t_ld = t_ew12_ld[e]

        def h_mms(t, sl=sl, j=j):
            for k_ in range(8):
                last_ = t.matmul(x12ps[:, 0:512],
                                 xmT[:, j * 1024 + k_ * 128: j * 1024 + (k_ + 1) * 128],
                                 ew1s[sl][:, k_ * 512:(k_ + 1) * 512],
                                 start=(k_ == 0), stop=False)
                last_ = t.matmul(x12ps[:, 512:1024],
                                 xmT[:, j * 1024 + k_ * 128: j * 1024 + (k_ + 1) * 128],
                                 ew2s[sl][:, k_ * 512:(k_ + 1) * 512],
                                 start=(k_ == 0), stop=False)
            last_ = t.matmul(x12ps[:, 0:512], ones_row[0:1, :], eb1r[sl][0:1, :],
                             start=False, stop=True)
            last_ = t.matmul(x12ps[:, 512:1024], ones_row[0:1, :], eb2r[sl][0:1, :],
                             start=False, stop=True)
            return last_
        t_h = te(h_mms, [t_ld, t_moe_xT[j], free["x12"], t_ones])
        if j == 1:
            t_ew12_free[sl] = t_h
        t_silu = sc(lambda s_, j=j: s_.activation(
            esilub[j][:, :], x12ps[:, 0:512], ACT.Silu),
            [t_h] + ([e_silu_free[j]] if e_silu_free[j] else []))
        t_gz = ve(lambda v, j=j, st=st: v.scalar_tensor_tensor(
            egb[j][:, :], x12ps[:, 512:1024], 1.0, esilub[j][:, :], AL.mult, AL.mult,
            accum_out=st[:, S1:S1 + 1]),
            [t_h, t_silu] + ([e_g_free[j]] if e_g_free[j] else []))
        free["x12"] = t_gz
        t_gz2 = ve(lambda v, j=j, st=st: v.scalar_tensor_tensor(
            esilub[j][:, :], egb[j][:, :], 1.0, egb[j][:, :], AL.mult, AL.mult,
            accum_out=st[:, S2:S2 + 1]), [t_gz])
        e_silu_free[j] = t_gz2
        exp_tok[(e, j)] = (t_gz, t_gz2)

    def emit_expert_A2(e, j):
        st = estt[j]
        t_gz, t_gz2 = exp_tok[(e, j)]
        t_az, t_muz = ln_stats_math_st(st, t_gz, t_gz2, HINV)
        t_zz = ve(lambda v, j=j, st=st: v.tensor_scalar(
            egb[j][:, :], egb[j][:, :], st[:, MU:MU + 1], st[:, A:A + 1],
            AL.subtract, AL.mult), [t_az, t_muz, t_gz2])
        return t_zz

    def emit_expert_B1(e, j, t_zz, off=512, defer=True):
        do_ev, t_ztr2 = transpose_chunks(egb[j], ezTb[j], 4, t_zz, e_zT_free[j],
                                         off=off, defer=True)
        e_g_free[j] = t_ztr2
        if not defer:
            zz_tok[(e, j)] = do_ev()
            return

        def fin(e=e, j=j, do_ev=do_ev):
            zz_tok[(e, j)] = do_ev()
        pend_evac.append(fin)

    def emit_expert_B2(e, j, t_zT):
        nonlocal t_last_acc
        pr = prep[e]

        def o_mms(t, j=j):
            for n_ in range(2):
                for k_ in range(4):
                    last_ = t.matmul(fps[:, n_ * 512:(n_ + 1) * 512],
                                     ezTb[j][:, k_ * 128:(k_ + 1) * 128],
                                     ew3s[0][:, k_ * 1024 + n_ * 512:
                                             k_ * 1024 + (n_ + 1) * 512],
                                     start=(k_ == 0), stop=False)
                last_ = t.matmul(fps[:, n_ * 512:(n_ + 1) * 512], ones_row[0:1, :],
                                 vber[0][0:1, n_ * 512:(n_ + 1) * 512],
                                 start=False, stop=True)
            return last_
        t_omm = te(o_mms, [t_zT, pr["t_vbe"], pr["t_gsl"], free["f"], t_ones])
        e_zT_free[j] = t_omm
        if j == 1:
            ew3_state["omm"] = t_omm
        extra = [t_moe_prep[j][1]]
        if e == 0:
            extra.append(t_moe_prep[j][0])
        t_acc = ve(lambda v, j=j, e=e: v.scalar_tensor_tensor(
            accf[j][:, :], fps[:, :], maskpb[j][:, e:e + 1], accf[j][:, :],
            AL.mult, AL.add), [t_omm] + extra + ([t_last_acc] if t_last_acc else []))
        free["f"] = t_acc
        t_last_acc = t_acc

    units = [(e, j) for e in range(E) for j in range(2)]
    emit_ew12_loads(0)
    emit_ew3_load(0)
    emit_ew12_loads(1)
    emit_prep_vbe(0)

    selh = {"ready": None, "t": None}

    def emit_sel_batch(b):
        if b == 0:
            selh["ready"] = emit_sel_scatters()
        n8 = (n_tiles + 3) // 4
        for i in range(b * n8, min((b + 1) * n8, n_tiles)):
            sdma(selall_i[:, i:i + 1],
                 bass.AP(sel_d, i * 128, [[1, 128], [1, 1]]),
                 waits=(selh["ready"],), sem="sel0")

    def emit_ivm():
        SEL_ALL = ("sel0", planned["sel0"])
        t_selfc = ve(lambda v: v.tensor_copy(self_all[:, :], selall_i[:, :]), [SEL_ALL])
        t_pm_all = ve(lambda v: v.tensor_single_scalar(
            ivm_all[:, :], self_all[:, :], float(M) - 0.5, AL.is_lt), [t_selfc])
        selh["t"] = ve(lambda v: v.tensor_scalar(
            ivm_all[:, :], ivm_all[:, :], -1.0, 1.0, AL.mult, AL.add), [t_pm_all])

    prev_x_free = [t_moe_xT[0], t_moe_xT[1], None, None, None, None]
    prev_bn_free = [None, None]
    prev_xn1_free = [t_moe_xT[0], t_moe_xT[1], None]
    prev_xT1_free = [t_moe_xT[0], t_moe_xT[1], None]
    prev_xT2_free = [None] * 3
    prev_hc_free = [t_moe_xT[0], t_moe_xT[1], None, None]
    prev_x2r_free = [t_moe_xT[0], t_moe_xT[1], None, None, None, None, None]
    prev_g_free = [moe_fmm[0], moe_fmm[1], None]
    prev_silu_free = [moe_fmm[0], moe_fmm[1]]
    prev_fb_free = [None] * 3
    prev_out_free = [t_moe_xm[0], t_moe_xm[1]]
    prev_sq_free = [moe_fmm[0], moe_fmm[1]]
    prev_zT_free = [moe_fmm[0], moe_fmm[1]]
    last_store = None
    TI = [dict() for _ in range(n_tiles)]

    def U0(i):
        s5 = (i + 2) % 6
        TI[i]["t_x"] = gdma(xb[s5][:, :], x_shard[i * 128:(i + 1) * 128, :],
                            waits=(prev_x_free[s5],) if prev_x_free[s5] else (),
                            sem=f"ldx{s5}")

    def U1(i):
        d = TI[i]
        s5 = (i + 2) % 6
        sx = (i + 2) % 3
        s2 = i % 2
        st = stt[i % 10]
        bs = bnst[s2]
        d["st"] = st
        t_x = d["t_x"]
        t_b1 = ve(lambda v, s5=s5, bs=bs: v.bn_stats(bs[:, 0:6], xb[s5][:, 0:512]),
                  [t_x] + ([prev_bn_free[s2]] if prev_bn_free[s2] else []))
        t_b2 = ve(lambda v, s5=s5, bs=bs: v.bn_stats(bs[:, 6:12], xb[s5][:, 512:1024]),
                  [t_b1])
        t_ag = ve(lambda v, bs=bs: v.bn_aggr(bs[:, 10:12], bs[:, 0:12]), [t_b2])
        t_sg = sc(lambda s_, bs=bs: s_.activation(
            bs[:, 12:13], bs[:, 11:12], ACT.Sqrt, bias=eps_t[:]), [t_ag, t_eps])
        t_ra = ve(lambda v, bs=bs: v.reciprocal(bs[:, 13:14], bs[:, 12:13]), [t_sg])
        t_bb = ve(lambda v, bs=bs: v.scalar_tensor_tensor(
            bs[:, 14:15], bs[:, 10:11], -1.0, bs[:, 13:14], AL.mult, AL.mult),
            [t_ra, t_ag])
        t_xn1 = sc(lambda s_, s5=s5, sx=sx, bs=bs: s_.activation(
            xn1b[sx][:, :], xb[s5][:, :], ACT.Identity, bias=bs[:, 14:15],
            scale=bs[:, 13:14]),
            [t_bb] + ([prev_xn1_free[sx]] if prev_xn1_free[sx] else []))
        prev_bn_free[s2] = t_xn1
        d["t_xn1"] = t_xn1

    def U2(i):
        d = TI[i]
        sx = (i + 2) % 3
        t_xT1, t_tr1 = transpose_chunks(xn1b[sx], xT1b[sx], 8, d["t_xn1"],
                                        prev_xT1_free[sx])
        prev_xn1_free[sx] = t_tr1
        d["t_xT1"] = t_xT1

    def U3a(i):
        d = TI[i]
        sx = (i + 2) % 3
        t_hcmm = proj(hcps, "hc", xT1b[sx], 0, WcT, 1024, 2, 8, bcrow, t_bc,
                      d["t_xT1"], t_wc)
        free["hc"] = t_hcmm
        prev_xT1_free[sx] = t_hcmm
        d["t_hcmm"] = t_hcmm

    def U3b(i):
        d = TI[i]
        s4 = (i + 2) % 4
        st = d["st"]
        t_hc = sc(lambda s_, s4=s4, st=st: s_.activation(
            hcb[s4][:, :], hcps[:, :], ACT.Identity, accum_out=st[:, S1:S1 + 1]),
            [d["t_hcmm"]] + ([prev_hc_free[s4]] if prev_hc_free[s4] else []))
        free["hc"] = t_hc
        d["t_hc"] = t_hc

    def U4(i):
        d = TI[i]
        s4 = (i + 2) % 4
        sq = i % 2
        st = d["st"]
        t_s2 = ve(lambda v, s4=s4, st=st, sq=sq: v.scalar_tensor_tensor(
            sqb[sq][:, :], hcb[s4][:, :], 1.0, hcb[s4][:, :], AL.mult, AL.mult,
            accum_out=st[:, S2:S2 + 1]),
            [d["t_hc"]] + ([prev_sq_free[sq]] if prev_sq_free[sq] else []))
        t_a2, t_mu2, t_be2 = ln_stats_math_b(st, d["t_hc"], t_s2, DINV)
        d["t_s2"] = t_s2
        d["t_a2"] = t_a2
        d["t_be2"] = t_be2

    def U5(i):
        d = TI[i]
        s4 = (i + 2) % 4
        s6 = (i + 2) % 7
        s5 = (i + 2) % 6
        sq = i % 2
        st = d["st"]
        t_t2 = sc(lambda s_, s4=s4, st=st, sq=sq: s_.activation(
            sqb[sq][:, :], hcb[s4][:, :], ACT.Identity, bias=st[:, BETA:BETA + 1],
            scale=st[:, A:A + 1]),
            [d["t_a2"], d["t_be2"], d["t_s2"]] +
            ([prev_sq_free[sq]] if prev_sq_free[sq] else []))
        t_x2r = ve(lambda v, s5=s5, s6=s6, st=st, sq=sq: v.scalar_tensor_tensor(
            x2rb[s6][:, :], sqb[sq][:, :], 1.0, xb[s5][:, :], AL.mult, AL.add,
            accum_out=st[:, S1:S1 + 1]),
            [t_t2, d["t_x"]] + ([prev_x2r_free[s6]] if prev_x2r_free[s6] else []))
        prev_x_free[s5] = t_x2r
        t_s2b = ve(lambda v, s6=s6, st=st, sq=sq: v.scalar_tensor_tensor(
            sqb[sq][:, :], x2rb[s6][:, :], 1.0, x2rb[s6][:, :], AL.mult, AL.mult,
            accum_out=st[:, S2:S2 + 1]), [t_x2r])
        prev_sq_free[sq] = t_s2b
        t_a3, t_mu3, t_be3 = ln_stats_math_b(st, t_x2r, t_s2b, DINV)
        d["t_x2r"] = t_x2r
        d["t_a3"] = t_a3
        d["t_be3"] = t_be3

    def U6a(i):
        d = TI[i]
        s4 = (i + 2) % 4
        s6 = (i + 2) % 7
        st = d["st"]
        t_xn2 = sc(lambda s_, s4=s4, s6=s6, st=st: s_.activation(
            hcb[s4][:, :], x2rb[s6][:, :], ACT.Identity, bias=st[:, BETA:BETA + 1],
            scale=st[:, A:A + 1]), [d["t_a3"], d["t_be3"]])
        d["t_xn2"] = t_xn2

    def U6b(i):
        d = TI[i]
        s4 = (i + 2) % 4
        sx = i % 3

        def mms(t, s4=s4):
            for k in range(8):
                last_ = t.matmul(trps[:, k * 128:(k + 1) * 128],
                                 hcb[s4][:, k * 128:(k + 1) * 128], idnb[:],
                                 start=True, stop=True)
            return last_
        t_tr2 = te(mms, [d["t_xn2"], free["tr"], t_idn])
        prev_hc_free[s4] = t_tr2
        t_xT2 = sc(lambda s_, sx=sx: s_.copy(xT2b[sx][:, :], trps[:, 0:1024]),
                   [t_tr2] + ([prev_xT2_free[sx]] if prev_xT2_free[sx] else []))
        free["tr"] = t_xT2
        d["t_xT2"] = t_xT2

    def U7(i):
        d = TI[i]
        sx = i % 3

        def mms12(t, sx=sx):
            for k_ in range(8):
                last_ = t.matmul(x12ps[:, 0:512],
                                 xT2b[sx][:, k_ * 128:(k_ + 1) * 128],
                                 sw1T[:, k_ * 512:(k_ + 1) * 512],
                                 start=(k_ == 0), stop=False)
                last_ = t.matmul(x12ps[:, 512:1024],
                                 xT2b[sx][:, k_ * 128:(k_ + 1) * 128],
                                 sw2T[:, k_ * 512:(k_ + 1) * 512],
                                 start=(k_ == 0), stop=False)
            last_ = t.matmul(x12ps[:, 0:512], ones_row[0:1, :], sb1row[0:1, :],
                             start=False, stop=True)
            last_ = t.matmul(x12ps[:, 512:1024], ones_row[0:1, :], sb2row[0:1, :],
                             start=False, stop=True)
            return last_
        t_x2mm = te(mms12, [d["t_xT2"], t_sb1, t_sb2, t_sw12, t_ones, free["x12"]])
        free["x12"] = t_x2mm
        prev_xT2_free[sx] = t_x2mm
        d["t_x1"] = t_x2mm
        d["t_x2mm"] = t_x2mm

    def U8a(i):
        d = TI[i]
        s3 = i % 3
        ss = i % 2
        sq = i % 2
        st = d["st"]
        t_silu = sc(lambda s_, ss=ss: s_.activation(
            silub[ss][:, :], x12ps[:, 0:512], ACT.Silu),
            [d["t_x1"]] + ([prev_silu_free[ss]] if prev_silu_free[ss] else []))
        t_g = ve(lambda v, s3=s3, ss=ss, st=st: v.scalar_tensor_tensor(
            gb[s3][:, :], x12ps[:, 512:1024], 1.0, silub[ss][:, :], AL.mult, AL.mult,
            accum_out=st[:, S1:S1 + 1]),
            [d["t_x2mm"], t_silu] + ([prev_g_free[s3]] if prev_g_free[s3] else []))
        free["x12"] = t_g
        prev_silu_free[ss] = t_g
        t_gs2 = ve(lambda v, s3=s3, st=st, sq=sq: v.scalar_tensor_tensor(
            sqb[sq][:, 0:512], gb[s3][:, :], 1.0, gb[s3][:, :], AL.mult, AL.mult,
            accum_out=st[:, S2:S2 + 1]),
            [t_g] + ([prev_sq_free[sq]] if prev_sq_free[sq] else []))
        prev_sq_free[sq] = t_gs2
        d["t_g"] = t_g
        d["t_gs2"] = t_gs2

    def U8b(i):
        d = TI[i]
        s3 = i % 3
        st = d["st"]
        t_agk, t_mug = ln_stats_math_st(st, d["t_g"], d["t_gs2"], HINV)
        t_z = ve(lambda v, s3=s3, st=st: v.tensor_scalar(
            gb[s3][:, :], gb[s3][:, :], st[:, MU:MU + 1], st[:, A:A + 1],
            AL.subtract, AL.mult), [t_agk, t_mug, d["t_gs2"]])
        d["t_z"] = t_z

    def U9a(i):
        d = TI[i]
        s3 = i % 3
        sz = i % 2
        do_ev, t_ztr = transpose_chunks(gb[s3], zTb[sz], 4, d["t_z"],
                                        prev_zT_free[sz], off=0, defer=True)
        prev_g_free[s3] = t_ztr

        def fin(d=d, do_ev=do_ev):
            d["t_zT"] = do_ev()
        pend_evac.append(fin)

    def U9b(i):
        d = TI[i]
        sz = i % 2
        t_fmm = proj(fps, "f", zTb[sz], 0, sw3Tg, 1024, 2, 4, vbrow_bf, t_vb,
                     d["t_zT"], t_g3)
        free["f"] = t_fmm
        prev_zT_free[sz] = t_fmm
        d["t_fmm"] = t_fmm

    def U10(i):
        nonlocal last_store
        d = TI[i]
        sf = i % 3
        s6 = (i + 2) % 7
        so = i % 2
        sq = i % 2
        st = d["st"]
        t_f = sc(lambda s_, sf=sf, st=st: s_.activation(
            fb[sf][:, :], fps[:, :], ACT.Identity, accum_out=st[:, S1:S1 + 1]),
            [d["t_fmm"], prev_fb_free[sf]])
        free["f"] = t_f
        t_fs2 = ve(lambda v, sf=sf, st=st, sq=sq: v.scalar_tensor_tensor(
            sqb[sq][:, :], fb[sf][:, :], 1.0, fb[sf][:, :], AL.mult, AL.mult,
            accum_out=st[:, S2:S2 + 1]),
            [t_f] + ([prev_sq_free[sq]] if prev_sq_free[sq] else []))
        prev_sq_free[sq] = t_fs2
        t_a4, t_mu4 = ln_stats_math_st(st, t_f, t_fs2, DINV)
        t_a4m = ve(lambda v, st=st, i=i: v.tensor_mul(
            st[:, A4M:A4M + 1], st[:, A:A + 1], ivm_all[:, i:i + 1]),
            [t_a4, selh["t"]])
        t_t4 = ve(lambda v, sf=sf, st=st: v.tensor_scalar(
            fb[sf][:, :], fb[sf][:, :], st[:, MU:MU + 1], st[:, A4M:A4M + 1],
            AL.subtract, AL.mult), [t_a4m, t_mu4, t_fs2])
        w5 = [t_t4, d["t_x2r"]] + ([prev_out_free[so]] if prev_out_free[so] else [])
        t_t5a = gp(lambda g, sf=sf, s6=s6, so=so: g.tensor_tensor(
            outb[so][:, 0:512], fb[sf][:, 0:512], x2rb[s6][:, 0:512], AL.add), w5)
        t_t5b = ve(lambda v, sf=sf, s6=s6, so=so: v.tensor_tensor(
            outb[so][:, 512:1024], fb[sf][:, 512:1024], x2rb[s6][:, 512:1024],
            AL.add), w5)
        t_st = sdma(out_d[i * 128:(i + 1) * 128, :], outb[so][:, :],
                    waits=(t_t5a, t_t5b), sem=f"st{so}")
        prev_fb_free[sf] = t_st
        prev_x2r_free[s6] = t_st
        prev_out_free[so] = t_st
        last_store = t_st

    exp = {"ua1": 0, "ua2": 0, "ub1": 0, "ub": 0, "vbe_due": None, "t_cc": None}

    def emit_r_and_cc():
        for j in range(2):
            st = estt[j]
            bs = bnstr[j]
            t_b1 = ve(lambda v, j=j, bs=bs: v.bn_stats(bs[:, 0:6], accf[j][:, 0:512]),
                      [t_last_acc])
            t_b2 = ve(lambda v, j=j, bs=bs: v.bn_stats(bs[:, 6:12], accf[j][:, 512:1024]),
                      [t_b1])
            t_ag2 = ve(lambda v, bs=bs: v.bn_aggr(bs[:, 10:12], bs[:, 0:12]), [t_b2])
            t_sg = sc(lambda s_, st=st, bs=bs: s_.activation(
                st[:, SIG:SIG + 1], bs[:, 11:12], ACT.Sqrt, bias=eps_t[:]),
                [t_ag2, t_eps])
            t_a = ve(lambda v, st=st: v.reciprocal(st[:, A:A + 1], st[:, SIG:SIG + 1]),
                     [t_sg])
            t_r = ve(lambda v, j=j, st=st, bs=bs: v.tensor_scalar(
                fb[j][:, :], accf[j][:, :], bs[:, 10:11], st[:, A:A + 1],
                AL.subtract, AL.mult),
                [t_a, t_ag2, prev_fb_free[j]])
            gdma(ag_in_d.ap()[j * 128:(j + 1) * 128, :], fb[j][:, :],
                 waits=(t_r,), sem="rst")
        t_rall2 = ("rst", planned["rst"])
        prev_fb_free[0] = t_rall2
        prev_fb_free[1] = t_rall2
        exp["t_cc"] = emit("cc", lambda g: g.collective_compute(
            "AllGather", AL.bypass,
            replica_groups=[list(range(NC))],
            ins=[ag_in_d.ap().opt()],
            outs=[ag_out_d.ap()[0:M, :].opt()]),
            (t_rall2, t_agz), 1)

    zz_tok = {}
    pend_evac = []

    def flush_evacs():
        for f in pend_evac:
            f()
        pend_evac.clear()

    def pump_A2():
        if exp["ua2"] >= exp["ua1"] or exp["ua2"] >= len(units):
            return
        e, j = units[exp["ua2"]]
        zz_tok[(e, j)] = emit_expert_A2(e, j)
        exp["ua2"] += 1

    def pump_B1(off=512, defer=True):
        if (exp["ub1"] >= exp["ua2"] or exp["ub1"] >= len(units)
                or exp["ub1"] >= exp["ub"] + 2):
            return
        e, j = units[exp["ub1"]]
        emit_expert_B1(e, j, zz_tok[(e, j)], off, defer)
        exp["ub1"] += 1

    def pump_B2(lag=None):
        if lag is None:
            lag = 1 if exp["ub1"] < len(units) else 0
        if exp["ub"] >= exp["ub1"] - lag or exp["ub"] >= len(units):
            return
        e, j = units[exp["ub"]]
        emit_expert_B2(e, j, zz_tok[(e, j)])
        if j == 1:
            if e + 1 < E:
                emit_ew3_load(e + 1)
                exp["vbe_due"] = e + 1
            else:
                emit_r_and_cc()
        exp["ub"] += 1

    def pump_A1():
        if exp["ua1"] >= len(units):
            return
        e, j = units[exp["ua1"]]
        emit_expert_A1(e, j)
        if j == 0 and 2 <= e + 1 < E:
            emit_ew12_loads(e + 1)
        exp["ua1"] += 1

    def pump_vbe():
        if exp["vbe_due"] is not None:
            emit_prep_vbe(exp["vbe_due"])
            exp["vbe_due"] = None

    NS = 13
    FRONT = 10
    for i in range(min(3, n_tiles)):
        U0(i)
    for t_it in range(n_tiles + NS - 1):
        if t_it == 3 and n_tiles:
            emit_sel_batch(0)
        if t_it in (4, 5, 6) and n_tiles:
            emit_sel_batch(t_it - 3)
        if t_it == 7 and n_tiles:
            emit_ivm()
        if 0 <= t_it < n_tiles:
            U1(t_it)
        if 0 <= t_it - 10 < n_tiles:
            U8b(t_it - 10)
        if 0 <= t_it - 12 < n_tiles:
            U10(t_it - 12)
        flush_evacs()
        if 0 <= t_it - 4 < n_tiles:
            U3b(t_it - 4)
        pump_A2()
        if t_it < FRONT:
            pump_A2()
        if 0 <= t_it - 4 < n_tiles:
            U4(t_it - 4)
        if 0 <= t_it - 5 < n_tiles:
            U5(t_it - 5)
        if 0 <= t_it - 6 < n_tiles:
            U6a(t_it - 6)
        if 0 <= t_it - 9 < n_tiles:
            U8a(t_it - 9)
        pump_vbe()
        if t_it < FRONT:
            pump_B2(0)
            pump_vbe()
            pump_B2(0)
            pump_B1(512, defer=False)
            pump_B1(0, defer=False)
        else:
            pump_B2()
        if 0 <= t_it - 11 < n_tiles:
            U9b(t_it - 11)
        if 0 <= t_it - 7 < n_tiles:
            U6b(t_it - 7)
        if 0 <= t_it - 3 < n_tiles:
            U3a(t_it - 3)
        if 0 <= t_it - 1 < n_tiles:
            U2(t_it - 1)
        pump_A1()
        if t_it < FRONT:
            pump_A1()
        if 0 <= t_it - 8 < n_tiles:
            U7(t_it - 8)
        if 0 <= t_it - 10 < n_tiles:
            U9a(t_it - 10)
        pump_B1()
        if 3 <= t_it + 1 < n_tiles:
            U0(t_it + 1)
    while exp["ub"] < len(units):
        flush_evacs()
        pump_A2()
        pump_vbe()
        pump_B2()
        pump_B1()
        pump_A1()
    flush_evacs()

    ST0 = ("st0", planned["st0"])
    ST1 = ("st1", planned["st1"])
    PFB = [prev_fb_free[0], prev_fb_free[1]]
    t_ps = None
    t_rm = [None, None]
    t_scat2 = [None, None]
    for jj in range(16):
        s = jj % 2
        t_ld = gdma(fb[s][:, :], ag_out_d.ap()[jj * 128:(jj + 1) * 128, :],
                    waits=tuple(x for x in (exp["t_cc"], t_rm[s], PFB[s]) if x),
                    sem="pat0")
        t_m = ve(lambda v, s=s, jj=jj: v.tensor_single_scalar(
            outb[s][:, :], fb[s][:, :], vmask[:, jj:jj + 1], AL.mult),
            [t_ld, ST0, ST1, t_16] + [x for x in (t_scat2[s], prev_out_free[s]) if x])
        t_rm[s] = t_m
        t_ps = emit("pat1", lambda g, s=s, jj=jj: g.indirect_dma_start(
            out=out_d, out_offset=IndirectOffsetOnAxis(selt2_i[:, jj:jj + 1], 0),
            in_=outb[s][:, :], in_offset=None, compute_op=AL.add),
            tuple(x for x in (t_m, t_ps) if x), 16)
        t_scat2[s] = t_ps
    if n_tiles:
        last_store = t_ps

    if last_store is None:
        last_store = t_cc
    queues["gpsimd"].append(lambda g: _wait(g, "gpsimd", [last_store]))
    queues["sync"].append(lambda sy: _wait(sy, "sync", [last_store]))
    queues["vector"].append(lambda v: _wait(v, "vector", [last_store]))
    queues["scalar"].append(lambda s_: _wait(s_, "scalar", [last_store]))
    queues["tensor"].append(lambda t: _wait(t, "tensor", [last_store]))

    @blk.gpsimd
    def _(g):
        for f in queues["gpsimd"]:
            f(g)

    @blk.sync
    def _(sy):
        for f in queues["sync"]:
            f(sy)

    @blk.tensor
    def _(t):
        for f in queues["tensor"]:
            f(t)

    @blk.vector
    def _(v):
        for f in queues["vector"]:
            f(v)

    @blk.scalar
    def _(s_):
        for f in queues["scalar"]:
            f(s_)

    es.close()
    return nc


def make_in_maps(inputs):
    i = inputs
    x = np.ascontiguousarray(np.asarray(i["L0_attn_x"], np.float32).reshape(TOK, D))
    common = {
        "x_full": x,
        "W_in": np.ascontiguousarray(np.asarray(i["L0_attn_W"], np.float32)),
        "U_in": np.triu(np.ones((D, D), np.float32)),
        "bias_in": np.ascontiguousarray(np.asarray(i["L0_attn_bias"], np.float32)),
        "sw1T_in": np.ascontiguousarray(np.asarray(i["L0_ffn_sw1"], np.float32).T),
        "sw2T_in": np.ascontiguousarray(np.asarray(i["L0_ffn_sw2"], np.float32).T),
        "sw3T_in": np.ascontiguousarray(np.asarray(i["L0_ffn_sw3"], np.float32).T),
        "sb1_in": np.ascontiguousarray(np.asarray(i["L0_ffn_sb1"], np.float32)),
        "sb2_in": np.ascontiguousarray(np.asarray(i["L0_ffn_sb2"], np.float32)),
        "sb3_in": np.ascontiguousarray(np.asarray(i["L0_ffn_sb3"], np.float32)),
        "slng_in": np.ascontiguousarray(np.asarray(i["L0_ffn_s_ln_g"], np.float32)),
        "slnb_in": np.ascontiguousarray(np.asarray(i["L0_ffn_s_ln_b"], np.float32)),
        "geT_in": np.ascontiguousarray(
            np.concatenate([np.asarray(i["L0_ffn_sg_w"], np.float32),
                            np.asarray(i["L0_ffn_eg_w"], np.float32)], axis=0).T),
        "geb_in": np.ascontiguousarray(
            np.concatenate([np.asarray(i["L0_ffn_sg_b"], np.float32),
                            np.asarray(i["L0_ffn_eg_b"], np.float32)], axis=0)),
        "expb_in": np.ascontiguousarray(
            np.broadcast_to(np.asarray(i["L0_ffn_exp_bias"], np.float32)[None, :],
                            (128, E))),
        "ew1t_in": np.ascontiguousarray(
            np.asarray(i["L0_ffn_ew1"], np.float32).transpose(0, 2, 1)),
        "ew2t_in": np.ascontiguousarray(
            np.asarray(i["L0_ffn_ew2"], np.float32).transpose(0, 2, 1)),
        "ew3t_in": np.ascontiguousarray(
            np.asarray(i["L0_ffn_ew3"], np.float32).transpose(0, 2, 1)),
        "eb1_in": np.ascontiguousarray(np.asarray(i["L0_ffn_eb1"], np.float32)),
        "eb2_in": np.ascontiguousarray(np.asarray(i["L0_ffn_eb2"], np.float32)),
        "eb3_in": np.ascontiguousarray(np.asarray(i["L0_ffn_eb3"], np.float32)),
        "selg_in": np.ascontiguousarray(
            np.asarray(i["L0_ffn_sel_ln_g"], np.float32).reshape(4, 128).T),
        "selb_in": np.ascontiguousarray(
            np.asarray(i["L0_ffn_sel_ln_b"], np.float32).reshape(4, 128).T),
        "slngc_in": np.ascontiguousarray(
            np.asarray(i["L0_ffn_s_ln_g"], np.float32).reshape(4, 128).T),
        "slnbc_in": np.ascontiguousarray(
            np.asarray(i["L0_ffn_s_ln_b"], np.float32).reshape(4, 128).T),
        "biasc_in": np.ascontiguousarray(
            np.asarray(i["L0_attn_bias"], np.float32).reshape(8, 128).T),
        "ibf_in": np.ascontiguousarray(
            np.asarray(i["L0_ffn_index_b"], np.int32).reshape(16, 128).T),
        "itf_in": np.ascontiguousarray(
            np.asarray(i["L0_ffn_index_t"], np.int32).reshape(16, 128).T),
        "idn_in": np.eye(128, dtype=np.float32),
    }
    ib = np.ascontiguousarray(np.asarray(i["L0_ffn_index_b"], np.int32))
    it = np.ascontiguousarray(np.asarray(i["L0_ffn_index_t"], np.int32))
    maps = []
    for cid in range(NC):
        mm = dict(common)
        mm["x_shard"] = np.ascontiguousarray(x[cid * TPC:(cid + 1) * TPC])
        mm["ibm_in"] = np.ascontiguousarray(ib[cid * MPC:(cid + 1) * MPC])
        mm["itm_in"] = np.ascontiguousarray(it[cid * MPC:(cid + 1) * MPC])
        mm["coreoff_in"] = np.full((128, 1), float(cid * TPC), np.float32)
        maps.append(mm)
    return maps


def kernel(**inputs):
    nc = build_program(NT)
    in_maps = make_in_maps(inputs)
    res = run_bass_kernel_spmd(nc, in_maps, core_ids=list(range(NC)))
    out = np.concatenate([np.asarray(r["out"], np.float32) for r in res.results], axis=0)
    return out.reshape(B, S, D)
